# version 06
# baseline (speedup 1.0000x reference)
import numpy as np
import ml_dtypes
from contextlib import ExitStack

import concourse.bass as bass
import concourse.mybir as mybir
from concourse.bass_utils import run_bass_kernel_spmd

F32 = mybir.dt.float32
BF16 = mybir.dt.bfloat16
AF = mybir.ActivationFunctionType
ALU = mybir.AluOpType
AX = mybir.AxisListType
NPBF = ml_dtypes.bfloat16

D = 1024
NH_C = 8
HD = 64
FC = NH_C * HD
CC = 16
STRIP = 128
NE = 32
EI = 512
EPS = float(1e-5)
BIG = float(1e30)

ENGS = ("sync", "tensor", "vector", "scalar", "gpsimd")

R_CA, R_CB = 0, 1
R_CC, R_CD, R_CDN = 2, 3, 4
R_MU, R_EX2, R_VAR = 5, 6, 7
R_S0, R_S1, R_S2 = 8, 9, 10
R_NW0, R_NW1 = 11, 12
R_RB = 13
NROWS = 14


class Prog:
    def __init__(self):
        self.nc = bass.Bass("TRN2", target_bir_lowering=False, debug=False,
                            detect_race_conditions=False)
        self.es = ExitStack()
        self.es.__enter__()
        self.phases = [{e: [] for e in ENGS}]
        self.c = {}

    def add(self, name, n=1):
        self.c[name] = self.c.get(name, 0) + n
        return self.c[name]

    def get(self, name):
        return self.c.get(name, 0)

    def dram_in(self, name, shape, dt=F32):
        return self.nc.dram_tensor(name, list(shape), dt, kind="ExternalInput").ap()

    def dram_out(self, name, shape, dt=F32):
        return self.nc.dram_tensor(name, list(shape), dt, kind="ExternalOutput").ap()

    def sb(self, name, shape, dt=F32):
        return self.es.enter_context(self.nc.sbuf_tensor(name, list(shape), dt))

    def ps(self, name, shape, dt=F32):
        return self.es.enter_context(self.nc.psum_tensor(name, list(shape), dt))

    def sem(self, name):
        return self.es.enter_context(self.nc.semaphore(name))

    def on(self, eng, fn):
        self.phases[-1][eng].append(fn)

    def barrier(self):
        self.phases.append({e: [] for e in ENGS})

    def finalize(self):
        nc = self.nc
        ph = self.sem("ph")
        nphase = len(self.phases)
        block = self.es.enter_context(nc.Block())

        def stream(ename):
            def run(eng):
                for k in range(nphase):
                    if k > 0:
                        eng.wait_ge(ph, 5 * k)
                    for fn in self.phases[k][ename]:
                        fn(eng)
                    if k < nphase - 1:
                        if ename == "sync" or not self.phases[k][ename]:
                            eng.nop().then_inc(ph, 1)
                        else:
                            eng.drain().then_inc(ph, 1)
            return run

        block.sync(stream("sync"))
        block.tensor(stream("tensor"))
        block.vector(stream("vector"))
        block.scalar(stream("scalar"))
        block.gpsimd(stream("gpsimd"))
        self.es.__exit__(None, None, None)
        return nc


def bc(e, bank_ap, ones1, row_ap):
    m = bank_ap.shape[0]
    e.matmul(bank_ap, ones1[:, 0:m], row_ap, start=True, stop=True)


def emit_stats(p, x_dram, Tn, S, dst_ca, dst_cb, dst_cc, dst_cd):
    sem = S["sem"]
    xt, onesf, sqt, pb = S["xt"], S["onesf"], S["sqt"], S["pb"]
    t8a, t8b = S["t8a"], S["t8b"]
    epsc = S["epsc"]
    ntile = Tn // 512
    w0 = p.get("wl"); t0 = p.get("ts"); v0 = p.get("vs"); s0 = p.get("ss")

    def gp(e):
        for tt in range(ntile):
            if tt > 0:
                e.wait_ge(sem["ts"], t0 + 2 * tt)
            e.dma_start(out=xt[:].rearrange("p (c n) -> p c n", c=8),
                        in_=x_dram.rearrange("(c p) (u n) -> u p c n", c=8, u=ntile)[tt]
                        ).then_inc(sem["wl"], 16)
    p.on("gpsimd", gp)
    p.add("wl", 16 * ntile)

    def tens(e):
        for tt in range(ntile):
            e.wait_ge(sem["wl"], w0 + 16 * (tt + 1))
            for ch in range(8):
                e.matmul(pb[0][:, :], onesf[:, :], xt[:, ch * 512:(ch + 1) * 512],
                         start=(ch == 0), stop=(ch == 7))
            e.wait_ge(sem["ss"], s0 + 3 * tt + 1)
            for ch in range(8):
                e.matmul(pb[1][:, :], onesf[:, :], sqt[:, ch * 512:(ch + 1) * 512],
                         start=(ch == 0), stop=(ch == 7))
            e.drain().then_inc(sem["ts"], 1)
            e.wait_ge(sem["vs"], v0 + 3 * (tt + 1))
            e.drain().then_inc(sem["ts"], 1)
    p.on("tensor", tens)
    p.add("ts", 2 * ntile)

    def scal(e):
        for tt in range(ntile):
            e.wait_ge(sem["wl"], w0 + 16 * (tt + 1))
            e.activation(sqt[:, 0:2048], xt[:, 0:2048], AF.Square)
            e.activation(sqt[:, 2048:4096], xt[:, 2048:4096], AF.Square)
            e.drain().then_inc(sem["ss"], 1)
            e.wait_ge(sem["vs"], v0 + 3 * tt + 1)
            e.activation(t8b[:, :], t8a[:, :], AF.Sqrt, bias=epsc[:, 0:1])
            e.drain().then_inc(sem["ss"], 1)
            e.wait_ge(sem["vs"], v0 + 3 * tt + 2)
            e.activation(t8b[:, :], sqt[:, 1536:2048], AF.Sqrt)
            e.drain().then_inc(sem["ss"], 1)
    p.on("scalar", scal)
    p.add("ss", 3 * ntile)

    def vec(e):
        for tt in range(ntile):
            sl = slice(tt * 512, (tt + 1) * 512)
            e.wait_ge(sem["ts"], t0 + 2 * tt + 1)
            e.tensor_scalar(sqt[:, 512:1024], pb[0][:, :], float(1.0 / D), None, ALU.mult)
            e.tensor_scalar(t8a[:, :], pb[1][:, :], float(1.0 / D), None, ALU.mult)
            e.tensor_tensor(sqt[:, 0:512], sqt[:, 512:1024], sqt[:, 512:1024], ALU.mult)
            e.tensor_tensor(t8a[:, :], t8a[:, :], sqt[:, 0:512], ALU.subtract)
            e.drain().then_inc(sem["vs"], 1)
            e.wait_ge(sem["ss"], s0 + 3 * tt + 2)
            e.reciprocal(sqt[:, 1024:1536], t8b[:, :])
            if dst_cc is not None:
                e.tensor_copy(dst_cc[:, sl], sqt[:, 1024:1536])
            if dst_cd is not None:
                e.tensor_tensor(dst_cd[:, sl], sqt[:, 512:1024], sqt[:, 1024:1536], ALU.mult)
            e.tensor_tensor(sqt[:, 0:512], sqt[:, 1024:1536], sqt[:, 1024:1536], ALU.mult)
            e.tensor_tensor(sqt[:, 0:512], sqt[:, 0:512], t8a[:, :], ALU.mult)
            e.tensor_scalar(sqt[:, 0:512], sqt[:, 0:512], EPS, None, ALU.add)
            e.reciprocal(sqt[:, 1536:2048], sqt[:, 0:512])
            e.drain().then_inc(sem["vs"], 1)
            e.wait_ge(sem["ss"], s0 + 3 * tt + 3)
            if dst_ca is not None:
                e.tensor_tensor(dst_ca[:, sl], sqt[:, 1024:1536], t8b[:, :], ALU.mult)
                e.tensor_tensor(dst_cb[:, sl], sqt[:, 512:1024], dst_ca[:, sl], ALU.mult)
            e.drain().then_inc(sem["vs"], 1)
    p.on("vector", vec)
    p.add("vs", 3 * ntile)
    p.barrier()



def build_A(T, debug=0, stage=4):
    TH = T // 2
    NTH = TH // 512
    NSTRIP = T // STRIP
    p = Prog()

    x_fm = p.dram_in("x_fm", (D, T))
    x_own = p.dram_in("x_own", (D, TH))
    wqkv = p.dram_in("wqkv", (D, 2048))
    bqkv = p.dram_in("bqkv", (64, 32))
    ln_g = p.dram_in("ln_g", (128, 8))
    ln_b = p.dram_in("ln_b", (128, 8))
    rwT = p.dram_in("rwT", (D, NE))
    rbias = p.dram_in("rbias", (128, NE))
    gu_w = p.dram_in("gu_w", (34, D, 2 * EI))
    dn_w = p.dram_in("dn_w", (34, EI, D))
    msku_in = p.dram_in("msku_in", (CC, 1024), BF16)
    mskg_in = p.dram_in("mskg_in", (CC, 1024), BF16)
    eye64 = p.dram_in("eye64", (64, 64))

    O_out = p.dram_out("O", (T, FC), BF16)
    ffn_out = p.dram_out("ffn", (8, 128, TH))
    stO_out = p.dram_out("stO", (2, TH))
    dbg = {}
    if debug:
        dbg["W"] = p.dram_out("dbgW", (34, TH))
        dbg["xn"] = p.dram_out("dbgxn", (D, TH), BF16)

    ones1 = p.sb("ones1", (1, 128))
    onesc = p.sb("onesc", (128, 1))
    onesf = p.sb("onesf", (128, 128))
    xt = p.sb("xt", (128, 8 * 512))
    sqt = p.sb("sqt", (128, 8 * 512))
    t8a = p.sb("t8a", (128, 512))
    t8b = p.sb("t8b", (128, 512))
    epsc = p.sb("epsc", (128, 1))
    CAt = p.sb("CAt", (128, T), BF16)
    CBt = p.sb("CBt", (128, T), BF16)
    ovl1 = p.sb("ovl1", (128, max(2 * TH, 4096)))
    CCt = ovl1[:, 0:TH]
    CDt = ovl1[:, TH:2 * TH]
    stoc = p.sb("stoc", (128, (TH // 128) * 2))
    xnbf = p.sb("xnbf", (128, 8 * TH), BF16)
    wmT = p.sb("wmT", (34, TH))
    rsumb = p.sb("rsumb", (128, NE))
    rbb = p.sb("rbb", (128, NE))
    rwTs = p.sb("rwTs", (128, 8 * NE))
    lngs = p.sb("lngs", (128, 8))
    lnbs = p.sb("lnbs", (128, 8))
    bqs = p.sb("bqs", (64, 32))
    gub = [p.sb(f"gub{i}", (128, 8 * 1024), BF16) for i in range(2)]
    dnb = p.sb("dnb", (128, 4 * 1024), BF16)
    wrow = p.sb("wrow", (128, 512))
    hbuf = p.sb("hbuf", (128, 4 * 512), BF16)
    ovl2 = p.sb("ovl2", (128, 4096))
    eacc = ovl2[:, 0:4096]
    tokw = p.sb("tokw", (128, NE))
    tok2 = p.sb("tok2", (128, NE))
    tokg = p.sb("tokg", (128, 8))
    tcol = p.sb("tcol", (128, 4))
    wtok = p.sb("wtok", (128, NE))
    xnl = p.sb("xnl", (128, 8 * STRIP), BF16)
    qkvb = ovl1[0:64, 0:4096]
    ktm2 = ovl2[0:CC, 0:1024]
    vtm2 = ovl2[0:CC, 1024:2048]
    btm2 = ovl2[0:CC, 2048:3072]
    ktmS = p.sb("ktmS", (128, 512))
    vtmS = p.sb("vtmS", (128, 512))
    btmS = p.sb("btmS", (128, 512))
    Asp = p.sb("Asp", (CC, 1024))
    Gm = p.sb("Gm", (CC, 1024))
    msku = p.sb("msku", (CC, 1024), BF16)
    mskg = p.sb("mskg", (CC, 1024), BF16)
    eyes = p.sb("eyes", (64, 64))
    Umat = ovl2[0:CC, 3072:3584]
    Wp = ovl2[0:CC, 3584:4096]
    ST = p.sb("ST", (64, FC))
    Ost = p.sb("Ost", (CC, 2 * 512), BF16)

    pb = [p.ps(f"pb{i}", (128, 512)) for i in range(8)]

    wm_d = p.nc.dram_tensor("wm_d", [34, TH], F32, kind="Internal").ap()
    sem = {k: p.sem(k) for k in ("ld", "wl", "ts", "vs", "ss", "st", "fr", "ac", "wm", "wb")}
    S = dict(xt=xt, onesf=onesf, sqt=sqt, t8a=t8a, t8b=t8b, pb=pb, sem=sem, epsc=epsc)

    def ph0_sync(e):
        e.dma_start(out=rwTs[:].rearrange("p (c n) -> p c n", c=8),
                    in_=rwT.rearrange("(c p) n -> p c n", c=8)).then_inc(sem["ld"], 16)
        e.dma_start(out=lngs[:], in_=ln_g).then_inc(sem["ld"], 16)
        e.dma_start(out=lnbs[:], in_=ln_b).then_inc(sem["ld"], 16)
        e.dma_start(out=bqs[:], in_=bqkv).then_inc(sem["ld"], 16)
        e.dma_start(out=msku[:], in_=msku_in).then_inc(sem["ld"], 16)
        e.dma_start(out=mskg[:], in_=mskg_in).then_inc(sem["ld"], 16)
        e.dma_start(out=eyes[:], in_=eye64).then_inc(sem["ld"], 16)
        e.dma_start(out=rbb[:], in_=rbias).then_inc(sem["ld"], 16)
    p.on("sync", ph0_sync)
    p.add("ld", 8 * 16)

    def ph0_vec(e):
        e.wait_ge(sem["ld"], 8 * 16)
        e.memset(ones1[:], 1.0)
        e.memset(onesc[:], 1.0)
        e.memset(onesf[:], 1.0)
        e.memset(epsc[:], EPS)
        e.memset(ST[:], 0.0)
        e.memset(wmT[32:34, :], 1.0)
        e.drain().then_inc(sem["vs"], 1)
    p.on("vector", ph0_vec)
    p.add("vs", 1)
    p.barrier()

    if stage < 1:
        nst = p.get("st"); nld = p.get("ld")
        def finz(e):
            e.wait_ge(sem["ld"], nld)
            e.wait_ge(sem["st"], nst)
        p.on("sync", finz)
        return p, dbg

    emit_stats(p, x_fm, T, S, CAt, CBt, None, None)
    emit_stats(p, x_own, TH, S, None, None, CCt, CDt)

    def sto_sync(e):
        e.dma_start(out=stO_out[0:1, :], in_=CCt[0:1, :]).then_inc(sem["st"], 16)
        e.dma_start(out=stO_out[1:2, :], in_=CDt[0:1, :]).then_inc(sem["st"], 16)
    p.on("sync", sto_sync)
    p.add("st", 32)

    if stage < 2:
        nst = p.get("st"); nwl = p.get("wl")
        def fin0(e):
            e.wait_ge(sem["st"], nst)
            e.wait_ge(sem["wl"], nwl)
        p.on("sync", fin0)
        return p, dbg

    tc0 = p.get("ts"); vc0 = p.get("vs")
    NB = TH // 128
    n_tgrp = (NB + 3) // 4

    def phc_tens(e):
        for ch in range(8):
            e.matmul(pb[0][:, 0:NE], onesf[:, :], rwTs[:, ch * NE:(ch + 1) * NE],
                     start=(ch == 0), stop=(ch == 7))
        e.drain().then_inc(sem["ts"], 1)
        for b in range(NB):
            e.transpose(pb[3][0:128, (b % 4) * 2:(b % 4) * 2 + 1],
                        CCt[0:1, b * 128:(b + 1) * 128], eyes[0:1, 0:1])
            e.transpose(pb[3][0:128, (b % 4) * 2 + 1:(b % 4) * 2 + 2],
                        CDt[0:1, b * 128:(b + 1) * 128], eyes[0:1, 0:1])
            if b % 4 == 3 or b == NB - 1:
                e.drain().then_inc(sem["ts"], 1)
                e.wait_ge(sem["vs"], vc0 + 2 + b // 4)
    p.on("tensor", phc_tens)
    p.add("ts", 1 + n_tgrp)

    def phc_vec(e):
        e.wait_ge(sem["ts"], tc0 + 1)
        e.tensor_copy(rsumb[:, :], pb[0][:, 0:NE])
        e.drain().then_inc(sem["vs"], 2)
        for g in range(n_tgrp):
            e.wait_ge(sem["ts"], tc0 + 2 + g)
            nb = min(4, NB - g * 4)
            e.tensor_copy(stoc[:, g * 8: g * 8 + nb * 2], pb[3][0:128, 0:nb * 2])
            e.drain().then_inc(sem["vs"], 1)
        e.tensor_scalar(stoc[:].rearrange("p (b k) -> p b k", k=2)[:, :, 1:2],
                        stoc[:].rearrange("p (b k) -> p b k", k=2)[:, :, 1:2],
                        -1.0, None, ALU.mult)
        e.drain().then_inc(sem["vs"], 1)
    p.on("vector", phc_vec)
    p.add("vs", 2 + n_tgrp + 1)
    p.barrier()

    wx0 = p.get("wl"); tx0 = p.get("ts"); vx0 = p.get("vs"); sx0 = p.get("ss")
    RV1 = 3
    RV2 = 30
    VT = 16 + 4 * (RV1 + RV2)

    def phx_gp(e):
        for tt in range(NTH):
            if tt > 0:
                e.wait_ge(sem["ts"], tx0 + 2 * tt)
            e.dma_start(out=xt[:].rearrange("p (c n) -> p c n", c=8),
                        in_=x_own.rearrange("(c p) (u n) -> u p c n", c=8, u=NTH)[tt]
                        ).then_inc(sem["wl"], 16)
    p.on("gpsimd", phx_gp)
    p.add("wl", 16 * NTH)

    def phx_tens(e):
        for tt in range(NTH):
            e.wait_ge(sem["wl"], wx0 + 16 * (tt + 1))
            for it in range(4):
                for ch in range(8):
                    e.matmul(pb[2][:, it * NE:(it + 1) * NE],
                             xt[:, ch * 512 + it * 128: ch * 512 + it * 128 + 128],
                             rwTs[:, ch * NE:(ch + 1) * NE],
                             start=(ch == 0), stop=(ch == 7))
            e.drain().then_inc(sem["ts"], 1)
            e.wait_ge(sem["vs"], vx0 + (tt + 1) * VT)
            e.drain().then_inc(sem["ts"], 1)
    p.on("tensor", phx_tens)
    p.add("ts", 2 * NTH)

    def phx_scal(e):
        for tt in range(NTH):
            for it in range(4):
                e.wait_ge(sem["vs"], vx0 + tt * VT + 16 + it * (RV1 + RV2) + RV1)
                e.activation(tok2[:, :], tokw[:, :], AF.Sigmoid).then_inc(sem["ss"], 1)
    p.on("scalar", phx_scal)
    p.add("ss", 4 * NTH)

    def phx_vec(e):
        for tt in range(NTH):
            e.wait_ge(sem["ts"], tx0 + 2 * tt + 1)
            slt = slice(tt * 512, (tt + 1) * 512)
            for ch in range(8):
                e.tensor_tensor(sqt[:, ch * 512:(ch + 1) * 512], xt[:, ch * 512:(ch + 1) * 512],
                                CCt[:, slt], ALU.mult)
                e.tensor_tensor(xnbf[:, ch * TH + tt * 512: ch * TH + (tt + 1) * 512],
                                sqt[:, ch * 512:(ch + 1) * 512], CDt[:, slt], ALU.subtract)
                e.drain().then_inc(sem["vs"], 2)
            def fz():
                e.drain(fusable=False)
            for it in range(4):
                tok0 = tt * 512 + it * 128
                blk = tok0 // 128
                ccol = stoc[:, blk * 2: blk * 2 + 1]
                cdn = stoc[:, blk * 2 + 1: blk * 2 + 2]
                e.tensor_scalar(tok2[:, :], pb[2][:, it * NE:(it + 1) * NE], ccol, None, ALU.mult)
                e.drain().then_inc(sem["vs"], 1)
                e.scalar_tensor_tensor(tokw[:, :], rsumb[:, :], cdn, tok2[:, :], ALU.mult, ALU.add)
                e.drain().then_inc(sem["vs"], 2)
                e.wait_ge(sem["ss"], sx0 + tt * 4 + it + 1)
                e.tensor_tensor(tokw[:, :], tok2[:, :], rbb[:, :], ALU.add)
                fz()
                e.tensor_reduce(tokg[:, 0:4], tokw[:, :].rearrange("p (g k) -> p g k", g=4),
                                AX.X, ALU.max)
                fz()
                for g in range(4):
                    e.tensor_scalar(sqt[:, g * 8:(g + 1) * 8], tokw[:, g * 8:(g + 1) * 8],
                                    tokg[:, g:g + 1], None, ALU.subtract)
                fz()
                e.tensor_scalar(sqt[:, 0:NE], sqt[:, 0:NE], 0.0, None, ALU.is_ge)
                fz()
                e.scalar_tensor_tensor(sqt[:, NE:2 * NE], sqt[:, 0:NE], -BIG, tokw[:, :],
                                       ALU.mult, ALU.add)
                fz()
                e.tensor_reduce(tokg[:, 4:8], sqt[:, NE:2 * NE].rearrange("p (g k) -> p g k", g=4),
                                AX.X, ALU.max)
                fz()
                e.tensor_tensor(tokg[:, 0:4], tokg[:, 0:4], tokg[:, 4:8], ALU.add)
                fz()
                e.tensor_reduce(tcol[:, 0:1], tokg[:, 0:4], AX.X, ALU.max)
                fz()
                e.tensor_scalar(tokg[:, 4:8], tokg[:, 0:4], tcol[:, 0:1], None, ALU.subtract)
                fz()
                e.tensor_scalar(tokg[:, 4:8], tokg[:, 4:8], 0.0, None, ALU.is_ge)
                fz()
                e.scalar_tensor_tensor(tokg[:, 4:8], tokg[:, 4:8], -BIG, tokg[:, 0:4],
                                       ALU.mult, ALU.add)
                fz()
                e.tensor_reduce(tcol[:, 1:2], tokg[:, 4:8], AX.X, ALU.max)
                fz()
                e.tensor_scalar(tokg[:, 0:4], tokg[:, 0:4], tcol[:, 1:2], None, ALU.subtract)
                fz()
                e.tensor_scalar(tokg[:, 0:4], tokg[:, 0:4], 0.0, None, ALU.is_ge)
                fz()
                for g in range(4):
                    e.tensor_scalar(wtok[:, g * 8:(g + 1) * 8], tokw[:, g * 8:(g + 1) * 8],
                                    tokg[:, g:g + 1], None, ALU.mult)
                e.drain().then_inc(sem["vs"], 16)
                e.tensor_copy(sqt[:, 0:NE], wtok[:, :])
                fz()
                for _ in range(3):
                    e.tensor_reduce(tcol[:, 2:3], sqt[:, 0:NE], AX.X, ALU.max)
                    fz()
                    e.tensor_scalar(sqt[:, NE:2 * NE], sqt[:, 0:NE], tcol[:, 2:3], None, ALU.subtract)
                    fz()
                    e.tensor_scalar(sqt[:, NE:2 * NE], sqt[:, NE:2 * NE], 0.0, None, ALU.is_ge)
                    fz()
                    e.scalar_tensor_tensor(sqt[:, 0:NE], sqt[:, NE:2 * NE], -BIG, sqt[:, 0:NE],
                                           ALU.mult, ALU.add)
                    fz()
                e.tensor_reduce(tcol[:, 2:3], sqt[:, 0:NE], AX.X, ALU.max)
                fz()
                e.tensor_scalar(sqt[:, 0:NE], wtok[:, :], tcol[:, 2:3], None, ALU.subtract)
                fz()
                e.tensor_scalar(sqt[:, 0:NE], sqt[:, 0:NE], 0.0, None, ALU.is_ge)
                fz()
                e.tensor_tensor(wtok[:, :], tok2[:, :], sqt[:, 0:NE], ALU.mult)
                fz()
                e.tensor_reduce(tcol[:, 3:4], wtok[:, :], AX.X, ALU.add)
                fz()
                e.tensor_scalar(tcol[:, 3:4], tcol[:, 3:4], 1e-20, None, ALU.add)
                fz()
                e.reciprocal(tcol[:, 3:4], tcol[:, 3:4])
                fz()
                e.tensor_scalar(wtok[:, :], wtok[:, :], tcol[:, 3:4], None, ALU.mult)
                fz()
                e.tensor_scalar(wtok[:, :], wtok[:, :], 2.5, None, ALU.mult)
                fz()
                for b4 in range(4):
                    e.transpose(wmT[0:32, tok0 + b4 * 32: tok0 + (b4 + 1) * 32],
                                wtok[b4 * 32:(b4 + 1) * 32, 0:32])
                e.drain().then_inc(sem["vs"], 14)
    p.on("vector", phx_vec)
    p.add("vs", NTH * VT)
    p.barrier()

    if stage < 3:
        if debug:
            def dbg2(e):
                e.dma_start(out=dbg["W"], in_=wmT[:]).then_inc(sem["st"], 16)
                e.dma_start(out=dbg["xn"].rearrange("(c p) n -> p c n", c=8),
                            in_=xnbf[:].rearrange("p (c n) -> p c n", c=8)).then_inc(sem["st"], 16)
            p.on("sync", dbg2)
            p.add("st", 32)
        if debug >= 2:
            dbg["stoc"] = p.dram_out("dbgstoc", (128, (TH // 128) * 2))
            dbg["rsumb"] = p.dram_out("dbgrsumb", (128, NE))
            dbg["tokw"] = p.dram_out("dbgtokw", (128, NE))
            dbg["tok2"] = p.dram_out("dbgtok2", (128, NE))
            dbg["wtok"] = p.dram_out("dbgwtok", (128, NE))
            dbg["tokg"] = p.dram_out("dbgtokg", (128, 8))
            dbg["tcol"] = p.dram_out("dbgtcol", (128, 4))
            dbg["sqt"] = p.dram_out("dbgsqt", (128, 2 * NE))
            def dbg3(e):
                for k in ("stoc", "rsumb", "tokw", "tok2", "wtok", "tokg", "tcol"):
                    src = {"stoc": stoc, "rsumb": rsumb, "tokw": tokw, "tok2": tok2,
                           "wtok": wtok, "tokg": tokg, "tcol": tcol}[k]
                    e.dma_start(out=dbg[k], in_=src[:]).then_inc(sem["st"], 16)
                e.dma_start(out=dbg["sqt"], in_=sqt[:, 0:2 * NE]).then_inc(sem["st"], 16)
            p.on("sync", dbg3)
            p.add("st", 8 * 16)
        nst = p.get("st"); nwl = p.get("wl")
        def fin1(e):
            e.wait_ge(sem["st"], nst)
            e.wait_ge(sem["wl"], nwl)
        p.on("sync", fin1)
        return p, dbg

    NTT = NTH
    wl0 = p.get("wl"); t0 = p.get("ts"); v0 = p.get("vs"); s0 = p.get("ss")
    ac0 = p.get("ac"); vroute = p.get("vs")

    def tbase(ee):
        return t0 + 2 * NTT * ee

    def sbase(ee):
        return s0 + NTT * ee

    def vbase(ee):
        return v0 + 2 * NTT * ee

    def moe_gp(e):
        e.dma_start(out=wm_d, in_=wmT[:]).then_inc(sem["wm"], 16)
        for ee in range(35):
            if ee >= 1:
                ep = ee - 1
                for tt in range(NTT):
                    e.wait_ge(sem["vs"], vbase(ep) + 2 * tt + 2)
                    e.dma_start(out=ffn_out[:, :, tt * 512:(tt + 1) * 512].rearrange("c p n -> p c n"),
                                in_=eacc[:].rearrange("p (c n) -> p c n", c=8),
                                accum_op=(ALU.bypass if ep == 0 else ALU.add)).then_inc(sem["ac"], 16)
            if ee < 34:
                if ee >= 2:
                    e.wait_ge(sem["fr"], 2 * (ee - 2) + 1)
                e.dma_start(out=gub[ee % 2][:].rearrange("p (c n) -> p c n", c=8),
                            in_=gu_w[ee].rearrange("(c p) n -> p c n", c=8)).then_inc(sem["wl"], 16)
                if ee >= 1:
                    e.wait_ge(sem["fr"], 2 * ee)
                e.dma_start(out=dnb[:].rearrange("p (c n) -> p c n", c=4),
                            in_=dn_w[ee].rearrange("(c p) n -> p c n", c=4)).then_inc(sem["wl"], 16)
    p.on("gpsimd", moe_gp)
    p.add("wl", 34 * 32)
    p.add("ac", 34 * NTT * 16)
    p.add("wm", 16)

    def moe_sync(e):
        e.wait_ge(sem["wm"], 16)
        for ee in range(NE):
            for tt in range(NTT):
                e.wait_ge(sem["vs"], max(0, vbase(ee) + 2 * tt - 1))
                e.dma_start(out=wrow[:, :],
                            in_=wm_d[ee:ee + 1, tt * 512:(tt + 1) * 512].broadcast_to([128, 512])
                            ).then_inc(sem["wb"], 16)
    p.on("sync", moe_sync)
    p.add("wb", NE * NTT * 16)

    def moe_tens(e):
        for ee in range(34):
            e.wait_ge(sem["wl"], wl0 + 32 * (ee + 1))
            tb = tbase(ee)
            for tt in range(NTT):
                e.wait_ge(sem["vs"], vbase(ee) + 2 * tt)
                for ic in range(4):
                    for ch in range(8):
                        e.matmul(pb[ic][:, :],
                                 gub[ee % 2][:, ch * 1024 + ic * 128: ch * 1024 + ic * 128 + 128],
                                 xnbf[:, ch * TH + tt * 512: ch * TH + (tt + 1) * 512],
                                 start=(ch == 0), stop=(ch == 7))
                for ic in range(4):
                    for ch in range(8):
                        mm = e.matmul(pb[4 + ic][:, :],
                                      gub[ee % 2][:, ch * 1024 + 512 + ic * 128: ch * 1024 + 512 + ic * 128 + 128],
                                      xnbf[:, ch * TH + tt * 512: ch * TH + (tt + 1) * 512],
                                      start=(ch == 0), stop=(ch == 7))
                mm.then_inc(sem["ts"], 1)
                if tt == NTT - 1:
                    e.drain().then_inc(sem["fr"], 1)
                e.wait_ge(sem["vs"], vbase(ee) + 2 * tt + 1)
                for dt in range(8):
                    for ic in range(4):
                        mm = e.matmul(pb[dt][:, :],
                                      dnb[:, ic * 1024 + dt * 128: ic * 1024 + dt * 128 + 128],
                                      hbuf[:, ic * 512:(ic + 1) * 512],
                                      start=(ic == 0), stop=(ic == 3))
                mm.then_inc(sem["ts"], 1)
            e.drain().then_inc(sem["fr"], 1)
    p.on("tensor", moe_tens)
    p.add("ts", 34 * NTT * 2)
    p.add("fr", 68)

    def moe_scal(e):
        for ee in range(34):
            tb = tbase(ee)
            for tt in range(NTT):
                e.wait_ge(sem["ts"], tb + 2 * tt + 1)
                for ic in range(4):
                    e.activation(sqt[:, 2048 + ic * 512: 2048 + (ic + 1) * 512], pb[ic][:, :], AF.Silu)
                e.drain().then_inc(sem["ss"], 1)
    p.on("scalar", moe_scal)
    p.add("ss", 34 * NTT)

    def moe_vec(e):
        for ee in range(34):
            tb = tbase(ee)
            for tt in range(NTT):
                e.wait_ge(sem["ts"], tb + 2 * tt + 1)
                e.wait_ge(sem["ss"], sbase(ee) + tt + 1)
                if ee < NE:
                    e.wait_ge(sem["wb"], 16 * (ee * NTT + tt + 1))
                for ic in range(4):
                    if ee < NE:
                        e.tensor_tensor(sqt[:, ic * 512:(ic + 1) * 512], pb[4 + ic][:, :],
                                        wrow[:, :], ALU.mult)
                    else:
                        e.tensor_copy(sqt[:, ic * 512:(ic + 1) * 512], pb[4 + ic][:, :])
                for ic in range(4):
                    e.tensor_tensor(hbuf[:, ic * 512:(ic + 1) * 512],
                                    sqt[:, 2048 + ic * 512: 2048 + (ic + 1) * 512],
                                    sqt[:, ic * 512:(ic + 1) * 512], ALU.mult)
                e.drain().then_inc(sem["vs"], 1)
                e.wait_ge(sem["ts"], tb + 2 * tt + 2)
                e.wait_ge(sem["ac"], ac0 + 16 * (ee * NTT + tt))
                for dt in range(8):
                    e.tensor_copy(eacc[:, dt * 512:(dt + 1) * 512], pb[dt][:, :])
                e.drain().then_inc(sem["vs"], 1)
    p.on("vector", moe_vec)
    p.add("vs", 34 * NTT * 2)

    if debug:
        nvs_moe = p.get("vs")
        def moe_dbg(e):
            e.wait_ge(sem["vs"], nvs_moe)
            e.dma_start(out=dbg["W"], in_=wmT[:]).then_inc(sem["st"], 16)
            e.dma_start(out=dbg["xn"].rearrange("(c p) n -> p c n", c=8),
                        in_=xnbf[:].rearrange("p (c n) -> p c n", c=8)).then_inc(sem["st"], 16)
        p.on("sync", moe_dbg)
        p.add("st", 32)
    p.barrier()

    if stage < 4:
        nst = p.get("st"); nwl = p.get("wl"); nac = p.get("ac")
        def fin2(e):
            e.wait_ge(sem["st"], nst)
            e.wait_ge(sem["wl"], nwl)
            e.wait_ge(sem["ac"], nac)
        p.on("sync", fin2)
        return p, dbg

    emit_deltanet(p, T, sem, pb, dict(
        x_fm=x_fm, CAt=CAt, CBt=CBt, lngs=lngs, lnbs=lnbs, bqs=bqs, gub=gub,
        wqkv=wqkv, eyes=eyes, msku=msku, mskg=mskg, xt=xt, xnl=xnl, qkvb=qkvb,
        sqt=sqt, ktm2=ktm2, vtm2=vtm2, btm2=btm2, ktmS=ktmS, vtmS=vtmS,
        btmS=btmS, Asp=Asp, Gm=Gm, Umat=Umat, Wp=Wp, ST=ST, Ost=Ost,
        onesf=onesf), O_out)

    nst = p.get("st"); nwl = p.get("wl"); nac = p.get("ac")
    def fin_sync(e):
        e.wait_ge(sem["st"], nst)
        e.wait_ge(sem["wl"], nwl)
        e.wait_ge(sem["ac"], nac)
    p.on("sync", fin_sync)
    return p, dbg


def emit_deltanet(p, T, sem, pb, B, O_out):
    NSTRIP = T // STRIP
    x_fm = B["x_fm"]
    CAt, CBt = B["CAt"], B["CBt"]
    lngs, lnbs, bqs = B["lngs"], B["lnbs"], B["bqs"]
    gub, wqkv = B["gub"], B["wqkv"]
    eyes, msku, mskg = B["eyes"], B["msku"], B["mskg"]
    xt, xnl, qkvb, sqt = B["xt"], B["xnl"], B["qkvb"], B["sqt"]
    ktm2, vtm2, btm2 = B["ktm2"], B["vtm2"], B["btm2"]
    ktmS, vtmS, btmS = B["ktmS"], B["vtmS"], B["btmS"]
    Asp, Gm = B["Asp"], B["Gm"]
    Umat, Wp, ST, Ost = B["Umat"], B["Wp"], B["ST"], B["Ost"]
    onesf = B["onesf"]

    NIT = CC - 1
    w0 = p.get("wl"); t0 = p.get("ts"); v0 = p.get("vs"); s0 = p.get("ss")

    def wq_ap(ch, j):
        return gub[ch // 4][:, (ch % 4) * 2048 + j * 64: (ch % 4) * 2048 + (j + 1) * 64]

    TPC = 1 + NIT + 1 + 1
    T_PRE = 4
    TPS = T_PRE + 8 * TPC
    VPC = 1 + NIT + 1
    V_PRE = 4
    VPS = V_PRE + 8 * VPC
    S_PRE = 6
    SPS = S_PRE + 8
    GPS = 16 * (1 + 24 + 8)

    def tw(s, k):
        return t0 + s * TPS + k

    def vw(s, k):
        return v0 + s * VPS + k

    def sw(s, k):
        return s0 + s * SPS + k

    def gp(e):
        e.wait_ge(sem["fr"], 68)
        for half in range(2):
            e.dma_start(out=gub[half][:].rearrange("p (c n) -> p c n", c=4),
                        in_=wqkv.rearrange("(u c p) n -> u p c n", u=2, c=4)[half]
                        ).then_inc(sem["wl"], 16)
        for s in range(NSTRIP):
            if s > 0:
                e.wait_ge(sem["vs"], vw(s - 1, 1))
            e.dma_start(out=xt[:, 0:8 * STRIP].rearrange("p (c n) -> p c n", c=8),
                        in_=x_fm.rearrange("(c p) n -> p c n", c=8)[:, :, s * STRIP:(s + 1) * STRIP]
                        ).then_inc(sem["wl"], 16)
            e.wait_ge(sem["ss"], sw(s, 5))
            for c in range(8):
                if c >= 2:
                    e.wait_ge(sem["vs"], vw(s, V_PRE + (c - 1) * VPC))
                sl = slice((c % 2) * 512, (c % 2) * 512 + 512)
                e.dma_start(out=ktm2[:, sl], in_=ktmS[c * CC:(c + 1) * CC, :]).then_inc(sem["wl"], 16)
                e.dma_start(out=vtm2[:, sl], in_=vtmS[c * CC:(c + 1) * CC, :]).then_inc(sem["wl"], 16)
                e.dma_start(out=btm2[:, sl], in_=btmS[c * CC:(c + 1) * CC, :]).then_inc(sem["wl"], 16)
                if c >= 1:
                    cp = c - 1
                    e.wait_ge(sem["ss"], sw(s, S_PRE + cp + 1))
                    e.dma_start(out=O_out.rearrange("(u c p) f -> u c p f", u=NSTRIP, c=8)[s, cp],
                                in_=Ost[:, (cp % 2) * 512:(cp % 2) * 512 + 512]).then_inc(sem["wl"], 16)
            e.wait_ge(sem["ss"], sw(s, S_PRE + 8))
            e.dma_start(out=O_out.rearrange("(u c p) f -> u c p f", u=NSTRIP, c=8)[s, 7],
                        in_=Ost[:, 512:1024]).then_inc(sem["wl"], 16)
    p.on("gpsimd", gp)
    p.add("wl", 32 + NSTRIP * GPS)
    nwq = w0 + 32

    def tens(e):
        e.wait_ge(sem["wl"], nwq)
        for s in range(NSTRIP):
            e.wait_ge(sem["ss"], sw(s, 1))
            mm = None
            for j in range(32):
                bank, col = j // 4, (j % 4) * 128
                for ch in range(8):
                    mm = e.matmul(pb[bank][0:64, col:col + 128], wq_ap(ch, j),
                                  xnl[:, ch * STRIP:(ch + 1) * STRIP],
                                  start=(ch == 0), stop=(ch == 7))
            mm.then_inc(sem["ts"], 1)
            e.wait_ge(sem["ss"], sw(s, 3))
            for h in range(8):
                bank = pb[0] if h < 4 else pb[1]
                mm = e.matmul(bank[:, (h % 4) * 128:(h % 4 + 1) * 128],
                              onesf[0:64, :], sqt[0:64, h * 128:(h + 1) * 128],
                              start=True, stop=True)
            mm.then_inc(sem["ts"], 1)
            e.wait_ge(sem["vs"], vw(s, 3))
            for h in range(8):
                e.transpose(pb[4][0:128, h * 64:(h + 1) * 64], qkvb[:, (8 + h) * 128:(9 + h) * 128], eyes[:])
                e.transpose(pb[5][0:128, h * 64:(h + 1) * 64], qkvb[:, (16 + h) * 128:(17 + h) * 128], eyes[:])
                mm = e.transpose(pb[6][0:128, h * 64:(h + 1) * 64], qkvb[:, (24 + h) * 128:(25 + h) * 128], eyes[:])
            mm.then_inc(sem["ts"], 1)
            e.wait_ge(sem["ss"], sw(s, 5))
            for c in range(8):
                bank = pb[4] if c < 4 else pb[5]
                colb = (c % 4) * 128
                for h in range(8):
                    ksl = qkvb[:, (8 + h) * 128 + c * CC: (8 + h) * 128 + (c + 1) * CC]
                    e.matmul(bank[0:CC, colb + h * CC: colb + (h + 1) * CC], ksl, ksl,
                             start=True, stop=True)
            for c in range(8):
                bank = pb[6] if c < 4 else pb[7]
                colb = (c % 4) * 128
                for h in range(8):
                    ksl = qkvb[:, (8 + h) * 128 + c * CC: (8 + h) * 128 + (c + 1) * CC]
                    qsl = qkvb[:, h * 128 + c * CC: h * 128 + (c + 1) * CC]
                    mm = e.matmul(bank[0:CC, colb + h * CC: colb + (h + 1) * CC], ksl, qsl,
                                  start=True, stop=True)
            mm.then_inc(sem["ts"], 1)
            for c in range(8):
                e.wait_ge(sem["vs"], vw(s, V_PRE + c * VPC))
                for h in range(8):
                    mm = e.matmul(pb[0][0:CC, h * 64:(h + 1) * 64],
                                  qkvb[:, (8 + h) * 128 + c * CC: (8 + h) * 128 + (c + 1) * CC],
                                  ST[:, h * 64:(h + 1) * 64], start=True, stop=True)
                mm.then_inc(sem["ts"], 1)
                for it in range(NIT):
                    bank = pb[1 + (it % 2)]
                    e.wait_ge(sem["vs"], vw(s, V_PRE + c * VPC + 1 + it))
                    for h in range(8):
                        mm = e.matmul(bank[0:CC, h * 64:(h + 1) * 64],
                                      Asp[:, (c * 8 + h) * CC: (c * 8 + h + 1) * CC],
                                      Umat[:, h * 64:(h + 1) * 64], start=True, stop=True)
                    mm.then_inc(sem["ts"], 1)
                e.wait_ge(sem["vs"], vw(s, V_PRE + c * VPC + 1 + NIT))
                e.wait_ge(sem["ss"], sw(s, S_PRE + c))
                for h in range(8):
                    e.matmul(pb[3][0:CC, h * 64:(h + 1) * 64],
                             qkvb[:, h * 128 + c * CC: h * 128 + (c + 1) * CC],
                             ST[:, h * 64:(h + 1) * 64], start=True, stop=False)
                    mm = e.matmul(pb[3][0:CC, h * 64:(h + 1) * 64],
                                  Gm[:, (c * 8 + h) * CC: (c * 8 + h + 1) * CC],
                                  Umat[:, h * 64:(h + 1) * 64], start=False, stop=True)
                mm.then_inc(sem["ts"], 1)
                for h in range(8):
                    mm = e.matmul(pb[4][0:64, h * 64:(h + 1) * 64],
                                  ktm2[:, (c % 2) * 512 + h * 64: (c % 2) * 512 + (h + 1) * 64],
                                  Umat[:, h * 64:(h + 1) * 64], start=True, stop=True)
                mm.then_inc(sem["ts"], 1)
    p.on("tensor", tens)
    p.add("ts", NSTRIP * TPS)

    def scal(e):
        for s in range(NSTRIP):
            e.wait_ge(sem["vs"], vw(s, 1))
            for ch in range(8):
                e.activation(xnl[:, ch * STRIP:(ch + 1) * STRIP],
                             sqt[:, 1024 + ch * STRIP: 1024 + (ch + 1) * STRIP],
                             AF.Identity, bias=lnbs[:, ch:ch + 1], scale=lngs[:, ch:ch + 1])
            e.drain().then_inc(sem["ss"], 1)
            e.wait_ge(sem["ts"], tw(s, 1))
            for j in range(32):
                bank, col = j // 4, (j % 4) * 128
                func = AF.Sigmoid if j >= 24 else AF.Identity
                e.activation(qkvb[:, j * 128:(j + 1) * 128], pb[bank][0:64, col:col + 128],
                             func, bias=bqs[0:64, j:j + 1])
            e.drain().then_inc(sem["ss"], 1)
            e.activation(sqt[0:64, 0:512], qkvb[0:64, 8 * 128:12 * 128], AF.Square)
            e.activation(sqt[0:64, 512:1024], qkvb[0:64, 12 * 128:16 * 128], AF.Square)
            e.drain().then_inc(sem["ss"], 1)
            e.wait_ge(sem["ts"], tw(s, 2))
            e.activation(sqt[:, 2048:2560], pb[0][:, :], AF.Sqrt)
            e.activation(sqt[:, 2560:3072], pb[1][:, :], AF.Sqrt)
            e.drain().then_inc(sem["ss"], 1)
            e.wait_ge(sem["ts"], tw(s, 3))
            e.copy(ktmS[:, :], pb[4][0:128, 0:512])
            e.copy(vtmS[:, :], pb[5][0:128, 0:512])
            e.copy(btmS[:, :], pb[6][0:128, 0:512])
            e.drain().then_inc(sem["ss"], 1)
            e.wait_ge(sem["ts"], tw(s, 4))
            e.copy(Asp[:, 0:512], pb[4][0:CC, 0:512])
            e.copy(Asp[:, 512:1024], pb[5][0:CC, 0:512])
            e.copy(Gm[:, 0:512], pb[6][0:CC, 0:512])
            e.copy(Gm[:, 512:1024], pb[7][0:CC, 0:512])
            e.drain().then_inc(sem["ss"], 1)
            for c in range(8):
                e.wait_ge(sem["ts"], tw(s, T_PRE + c * TPC + 1 + NIT + 1))
                if s * 8 + c >= 2:
                    prev = s * 8 + c - 2
                    ps_, pc_ = prev // 8, prev % 8
                    e.wait_ge(sem["wl"], w0 + 32 + ps_ * GPS + 16 * (4 * pc_ + 9))
                e.copy(Ost[:, (c % 2) * 512:(c % 2) * 512 + 512], pb[3][0:CC, 0:512]).then_inc(sem["ss"], 1)
    p.on("scalar", scal)
    p.add("ss", NSTRIP * SPS)

    def vec(e):
        for s in range(NSTRIP):
            e.wait_ge(sem["wl"], w0 + 32 + s * GPS + 16)
            ssl = slice(s * STRIP, (s + 1) * STRIP)
            for ch in range(8):
                e.tensor_tensor(sqt[:, ch * STRIP:(ch + 1) * STRIP],
                                xt[:, ch * STRIP:(ch + 1) * STRIP], CAt[:, ssl], ALU.mult)
                e.tensor_tensor(sqt[:, 1024 + ch * STRIP: 1024 + (ch + 1) * STRIP],
                                sqt[:, ch * STRIP:(ch + 1) * STRIP], CBt[:, ssl], ALU.subtract)
            e.drain().then_inc(sem["vs"], 1)
            e.wait_ge(sem["ss"], sw(s, 4))
            e.tensor_scalar(sqt[:, 2048:3072], sqt[:, 2048:3072], 1e-6, None, ALU.max)
            e.reciprocal(sqt[:, 2048:3072], sqt[:, 2048:3072])
            e.drain().then_inc(sem["vs"], 1)
            e.tensor_tensor(qkvb[:, 8 * 128:16 * 128], qkvb[:, 8 * 128:16 * 128],
                            sqt[0:64, 2048:3072], ALU.mult)
            e.drain().then_inc(sem["vs"], 1)
            e.wait_ge(sem["ss"], sw(s, 5 + 1))
            e.tensor_tensor(Asp[:, :], Asp[:, :], msku[:, :], ALU.mult)
            e.tensor_tensor(Gm[:, :], Gm[:, :], mskg[:, :], ALU.mult)
            e.drain().then_inc(sem["vs"], 1)
            for c in range(8):
                e.wait_ge(sem["wl"], w0 + 32 + s * GPS + 16 * (4 * c + 4))
                e.wait_ge(sem["ts"], tw(s, T_PRE + c * TPC + 1))
                sl = slice((c % 2) * 512, (c % 2) * 512 + 512)
                e.tensor_tensor(Wp[:, :], vtm2[:, sl], pb[0][0:CC, 0:512], ALU.subtract)
                e.tensor_tensor(Wp[:, :], btm2[:, sl], Wp[:, :], ALU.mult)
                e.tensor_copy(Umat[:, :], Wp[:, :]).then_inc(sem["vs"], 1)
                for it in range(NIT):
                    bank = pb[1 + (it % 2)]
                    e.wait_ge(sem["ts"], tw(s, T_PRE + c * TPC + 1 + it + 1))
                    e.tensor_tensor(sqt[0:CC, 1536:2048], btm2[:, sl], bank[0:CC, 0:512], ALU.mult)
                    e.tensor_tensor(Umat[:, :], Wp[:, :], sqt[0:CC, 1536:2048], ALU.subtract).then_inc(sem["vs"], 1)
                e.wait_ge(sem["ts"], tw(s, T_PRE + (c + 1) * TPC))
                e.tensor_tensor(ST[:, :], ST[:, :], pb[4][0:64, 0:512], ALU.add).then_inc(sem["vs"], 1)
    p.on("vector", vec)
    p.add("vs", NSTRIP * VPS)



H1 = 4096
NH1C = H1 // 128


def build_B(T, debug=0):
    NT = T // 512
    p = Prog()

    x_fm = p.dram_in("x_fm", (D, T))
    OT_in = p.dram_in("OT_in", (D, T), BF16)
    ffn0_in = p.dram_in("ffn0_in", (D, T))
    stB = p.dram_in("stB", (2, T))
    wo0T = p.dram_in("wo0T", (D, D))
    b_out0c = p.dram_in("b_out0c", (128, 8))
    WinT = p.dram_in("WinT", (D, D))
    WigT = p.dram_in("WigT", (D, D))
    WrgT = p.dram_in("WrgT", (D, D))
    WoutT = p.dram_in("WoutT", (D, D))
    bigc = p.dram_in("bigc", (128, 8))
    brgc = p.dram_in("brgc", (128, 8))
    lamc = p.dram_in("lamc", (128, 8))
    h0c = p.dram_in("h0c", (128, 8))
    nwc = p.dram_in("nwc", (128, 8))
    g1c = p.dram_in("g1c", (128, 8))
    b1c = p.dram_in("b1c", (128, 8))
    bb1c = p.dram_in("bb1c", (128, 32))
    g2c = p.dram_in("g2c", (128, 32))
    b2c = p.dram_in("b2c", (128, 32))
    bb2c = p.dram_in("bb2c", (128, 8))
    w1T = p.dram_in("w1T", (D, H1))
    w2T = p.dram_in("w2T", (H1, D))
    eye128 = p.dram_in("eye128", (128, 128))

    xout = p.dram_out("xout", (D, T))

    x1d = p.nc.dram_tensor("x1d", [D, T], F32, kind="Internal").ap()
    xn1d = p.nc.dram_tensor("xn1d", [D, T], BF16, kind="Internal").ap()
    qw1d = p.nc.dram_tensor("qw1d", [D, H1], BF16, kind="Internal").ap()
    qw2d = p.nc.dram_tensor("qw2d", [H1, D], BF16, kind="Internal").ap()
    rowd = p.nc.dram_tensor("rowd", [4, 512], F32, kind="Internal").ap()

    ones1 = p.sb("ones1", (1, 128))
    onesf = p.sb("onesf", (128, 128))
    onesb = p.sb("onesb", (128, 128), BF16)
    epsc = p.sb("epsc", (128, 1))
    eyes = p.sb("eyes", (128, 128))
    f32a = p.sb("f32a", (128, 16384))
    bf16a = p.sb("bf16a", (128, 32768), BF16)
    CAt = p.sb("CAt", (128, T), BF16)
    CBt = p.sb("CBt", (128, T), BF16)
    xnt = p.sb("xnt", (128, 4096), BF16)
    qt = p.sb("qt", (128, 4096), BF16)
    wdb = p.sb("wdb", (128, 8192), BF16)
    t8a = p.sb("t8a", (128, 512))
    t8b = p.sb("t8b", (128, 512))
    t8c = p.sb("t8c", (128, 512))
    t8d = p.sb("t8d", (128, 512))
    murb = p.sb("murb", (128, 512))
    rsrb = p.sb("rsrb", (128, 512))
    amr = p.sb("amr", (128, 512))
    am2r = p.sb("am2r", (128, 512))
    stbsA = p.sb("stbsA", (1, 512))
    stbsB = p.sb("stbsB", (1, 512))
    wcol = p.sb("wcol", (128, 32))
    swc = p.sb("swc", (128, 8))
    logac = p.sb("logac", (128, 8))
    pb = [p.ps(f"pb{i}", (128, 512)) for i in range(8)]
    sem = {k: p.sem(k) for k in ("ld", "wl", "ts", "vs", "ss", "st")}

    fA = lambda i: f32a[:, i * 4096:(i + 1) * 4096]

    def fz(e):
        e.drain(fusable=False)

    def h0c_col(c):
        return CB_h0[:, c:c + 1]

    def ph0_sync(e):
        for dst, src in ((b_out0c, None),):
            pass
        for nm, dr, sb_ in (("b_out0c", b_out0c, None),):
            pass
        for dr, w in ((b_out0c, 8), (bigc, 8), (brgc, 8), (lamc, 8), (h0c, 8),
                      (nwc, 8), (g1c, 8), (b1c, 8), (bb1c, 32), (g2c, 32),
                      (b2c, 32), (bb2c, 8)):
            pass
    colsB = p.sb("colsB", (128, 160))
    CB_b_out0 = colsB[:, 0:8]
    CB_big = colsB[:, 8:16]
    CB_brg = colsB[:, 16:24]
    CB_h0 = colsB[:, 24:32]
    CB_nw = colsB[:, 32:40]
    CB_g1 = colsB[:, 40:48]
    CB_b1 = colsB[:, 48:56]
    CB_bb1 = colsB[:, 56:88]
    CB_g2 = colsB[:, 88:120]
    CB_b2 = colsB[:, 120:152]
    CB_bb2 = colsB[:, 152:160]

    def ph0_sync2(e):
        for dst, src in ((CB_b_out0, b_out0c), (CB_big, bigc), (CB_brg, brgc),
                         (CB_h0, h0c), (CB_nw, nwc), (CB_g1, g1c), (CB_b1, b1c),
                         (CB_bb1, bb1c), (CB_g2, g2c), (CB_b2, b2c), (CB_bb2, bb2c)):
            e.dma_start(out=dst, in_=src).then_inc(sem["ld"], 16)
        e.dma_start(out=eyes[:], in_=eye128).then_inc(sem["ld"], 16)
        e.dma_start(out=swc[:, 0:8], in_=lamc).then_inc(sem["ld"], 16)
    p.on("sync", ph0_sync2)
    p.add("ld", 13 * 16)

    def ph0_gp(e):
        e.dma_start(out=wdb[:].rearrange("p (k n) -> p k n", k=8),
                    in_=wo0T.rearrange("(k p) n -> p k n", k=8)).then_inc(sem["wl"], 16)
    p.on("gpsimd", ph0_gp)
    p.add("wl", 16)

    def ph0_vec(e):
        e.memset(ones1[:], 1.0)
        e.memset(onesf[:], 1.0)
        e.memset(onesb[:], 1.0)
        e.memset(epsc[:], EPS)
        e.drain().then_inc(sem["vs"], 1)
        e.wait_ge(sem["ss"], 1)
        e.tensor_scalar(logac[:, :], logac[:, :], 1e-8, None, ALU.add)
        e.drain().then_inc(sem["vs"], 1)
    p.on("vector", ph0_vec)
    p.add("vs", 2)

    def ph0_scal(e):
        e.wait_ge(sem["ld"], 13 * 16)
        e.activation(logac[:, :], swc[:, 0:8], AF.Sigmoid)
        e.drain().then_inc(sem["ss"], 1)
        e.wait_ge(sem["vs"], 2)
        e.activation(logac[:, :], logac[:, :], AF.Ln)
        e.drain().then_inc(sem["ss"], 1)
    p.on("scalar", ph0_scal)
    p.add("ss", 2)

    def ph0_fin(e):
        e.wait_ge(sem["wl"], 16)
        e.wait_ge(sem["ss"], 2)
    p.on("sync", ph0_fin)
    p.barrier()

    xr = x_fm.rearrange("(c p) (u n) -> u p c n", c=8, u=NT)
    fr = ffn0_in.rearrange("(c p) (u n) -> u p c n", c=8, u=NT)
    Or = OT_in.rearrange("(c p) (u n) -> u p c n", c=8, u=NT)
    x1r = x1d.rearrange("(c p) (u n) -> u c p n", c=8, u=NT)
    xn1r = xn1d.rearrange("(c p) (u n) -> u c p n", c=8, u=NT)

    for tt in range(NT):
        l0 = p.get("ld"); t0 = p.get("ts"); v0 = p.get("vs"); s0 = p.get("ss")

        def p1_sync(e, tt=tt, l0=l0, v0=v0):
            e.dma_start(out=stbsA[:], in_=stB[0:1, tt * 512:(tt + 1) * 512]).then_inc(sem["ld"], 16)
            e.dma_start(out=stbsB[:], in_=stB[1:2, tt * 512:(tt + 1) * 512]).then_inc(sem["ld"], 16)
            e.dma_start(out=fA(0).rearrange("p (c n) -> p c n", c=8), in_=xr[tt]).then_inc(sem["ld"], 16)
            e.dma_start(out=fA(1).rearrange("p (c n) -> p c n", c=8), in_=fr[tt]).then_inc(sem["ld"], 16)
            e.dma_start(out=xnt[:].rearrange("p (c n) -> p c n", c=8), in_=Or[tt]).then_inc(sem["ld"], 16)
            for c in range(8):
                e.wait_ge(sem["vs"], v0 + c + 1)
                e.dma_start(out=x1r[tt, c], in_=fA(0)[:, c * 512:(c + 1) * 512]).then_inc(sem["st"], 16)
        p.on("sync", p1_sync)
        p.add("ld", 80)
        p.add("st", 8 * 16)

        def p1_tens(e, l0=l0, s0=s0):
            e.wait_ge(sem["ld"], l0 + 80)
            bc(e, pb[6][:, :], ones1, stbsA[0:1, :])
            bc(e, pb[7][:, :], ones1, stbsB[0:1, :])
            for c in range(4):
                for k in range(8):
                    e.matmul(pb[c][:, :], wdb[:, k * 1024 + c * 128: k * 1024 + c * 128 + 128],
                             xnt[:, k * 512:(k + 1) * 512], start=(k == 0), stop=(k == 7))
            e.drain().then_inc(sem["ts"], 1)
            e.wait_ge(sem["ss"], s0 + 4)
            for c in range(4):
                for k in range(8):
                    e.matmul(pb[c][:, :], wdb[:, k * 1024 + (c + 4) * 128: k * 1024 + (c + 4) * 128 + 128],
                             xnt[:, k * 512:(k + 1) * 512], start=(k == 0), stop=(k == 7))
            e.drain().then_inc(sem["ts"], 1)
        p.on("tensor", p1_tens)
        p.add("ts", 2)

        def p1_scal(e, t0=t0, v0=v0):
            for c in range(8):
                e.wait_ge(sem["ts"], t0 + (1 if c < 4 else 2))
                if c > 0:
                    e.wait_ge(sem["vs"], v0 + c)
                e.activation(t8a[:, :], pb[c % 4][:, :], AF.Identity,
                             bias=CB_b_out0[:, c:c + 1])
                e.drain().then_inc(sem["ss"], 1)
        p.on("scalar", p1_scal)
        p.add("ss", 8)

        def p1_vec(e, t0=t0, s0=s0):
            e.wait_ge(sem["ts"], t0 + 1)
            for c in range(8):
                xc = fA(0)[:, c * 512:(c + 1) * 512]
                fc = fA(1)[:, c * 512:(c + 1) * 512]
                e.wait_ge(sem["ss"], s0 + c + 1)
                e.tensor_tensor(t8b[:, :], xc, pb[6][:, :], ALU.mult)
                e.tensor_tensor(t8b[:, :], t8b[:, :], pb[7][:, :], ALU.subtract)
                e.tensor_tensor(t8b[:, :], t8b[:, :], xc, ALU.add)
                e.tensor_tensor(t8b[:, :], t8b[:, :], fc, ALU.add)
                e.tensor_tensor(xc, t8b[:, :], t8a[:, :], ALU.add)
                e.drain().then_inc(sem["vs"], 1)
        p.on("vector", p1_vec)
        p.add("vs", 8)
        p.barrier()

    nst1 = p.get("st")
    def p1_fin(e, nst1=nst1):
        e.wait_ge(sem["st"], nst1)
    p.on("sync", p1_fin)
    p.barrier()

    CCv = f32a[:, 8192:12288]
    CDv = f32a[:, 12288:16384]
    S2 = dict(xt=fA(0), onesf=onesf, sqt=fA(1), t8a=t8a, t8b=t8b, pb=pb,
              sem=sem, epsc=epsc)
    emit_stats(p, x1d, T, S2, CAt, CBt, CCv, CDv)

    for tt in range(NT):
        l0 = p.get("ld"); v0 = p.get("vs")

        def p3_sync(e, tt=tt, v0=v0):
            e.dma_start(out=fA(0).rearrange("p (c n) -> p c n", c=8),
                        in_=x1r[tt].rearrange("c p n -> p c n")).then_inc(sem["ld"], 16)
            e.wait_ge(sem["vs"], v0 + 1)
            e.dma_start(out=xn1r[tt].rearrange("c p n -> p c n"), in_=xnt[:].rearrange("p (c n) -> p c n", c=8)).then_inc(sem["st"], 16)
        p.on("sync", p3_sync)
        p.add("ld", 16)
        p.add("st", 16)

        def p3_vec(e, tt=tt, l0=l0):
            e.wait_ge(sem["ld"], l0 + 16)
            sl = slice(tt * 512, (tt + 1) * 512)
            for c in range(8):
                xc = fA(0)[:, c * 512:(c + 1) * 512]
                e.tensor_tensor(t8a[:, :], xc, CCv[:, sl], ALU.mult)
                e.tensor_tensor(xnt[:, c * 512:(c + 1) * 512], t8a[:, :], CDv[:, sl], ALU.subtract)
            e.drain().then_inc(sem["vs"], 1)
        p.on("vector", p3_vec)
        p.add("vs", 1)
        p.barrier()

    nst3 = p.get("st")
    def p3_fin(e, nst3=nst3):
        e.wait_ge(sem["st"], nst3)
    p.on("sync", p3_fin)
    p.barrier()
    areg, greg, hsreg, rmsacc = fA(0), fA(1), fA(2), fA(3)

    for c in range(8):
        l0 = p.get("ld"); w0 = p.get("wl"); t0 = p.get("ts")
        v0 = p.get("vs"); s0 = p.get("ss")

        def p4_gp(e, c=c):
            for i, W in enumerate((WinT, WigT, WrgT)):
                e.dma_start(out=wdb[:, i * 1024:(i + 1) * 1024].rearrange("p (k n) -> p k n", k=8),
                            in_=W.rearrange("(k p) n -> p k n", k=8)[:, :, c * 128:(c + 1) * 128]
                            ).then_inc(sem["wl"], 16)
        p.on("gpsimd", p4_gp)
        p.add("wl", 48)

        def p4_sync(e, t0=t0):
            for tt in range(NT):
                if tt > 0:
                    e.wait_ge(sem["ts"], t0 + tt)
                e.dma_start(out=xnt[:].rearrange("p (c n) -> p c n", c=8),
                            in_=xn1r[tt].rearrange("c p n -> p c n")).then_inc(sem["ld"], 16)
        p.on("sync", p4_sync)
        p.add("ld", 16 * NT)

        def p4_tens(e, l0=l0, w0=w0, s0=s0, v0=v0):
            e.wait_ge(sem["wl"], w0 + 48)
            for tt in range(NT):
                e.wait_ge(sem["ld"], l0 + 16 * (tt + 1))
                if tt > 0:
                    e.wait_ge(sem["ss"], s0 + 5 * tt - 2)
                for i in range(3):
                    for k in range(8):
                        e.matmul(pb[i][:, :], wdb[:, i * 1024 + k * 128:i * 1024 + (k + 1) * 128],
                                 xnt[:, k * 512:(k + 1) * 512], start=(k == 0), stop=(k == 7))
                e.drain().then_inc(sem["ts"], 1)
            for tt in range(NT):
                e.wait_ge(sem["ss"], s0 + 5 * NT + 1 + tt)
                if tt > 0:
                    e.wait_ge(sem["vs"], v0 + 3 * NT + 1 + tt)
                e.matmul(pb[5][:, :], onesf[:, :], t8a[:, :], start=True, stop=True)
                e.drain().then_inc(sem["ts"], 1)
        p.on("tensor", p4_tens)
        p.add("ts", 2 * NT)

        def p4_scal(e, c=c, t0=t0, v0=v0):
            for tt in range(NT):
                e.wait_ge(sem["ts"], t0 + tt + 1)
                if tt > 0:
                    e.wait_ge(sem["vs"], v0 + 3 * tt)
                e.activation(t8a[:, :], pb[0][:, :], AF.Copy)
                e.drain().then_inc(sem["ss"], 1)
                e.activation(t8b[:, :], pb[1][:, :], AF.Sigmoid, bias=CB_big[:, c:c + 1])
                e.drain().then_inc(sem["ss"], 1)
                e.activation(t8c[:, :], pb[2][:, :], AF.Sigmoid, bias=CB_brg[:, c:c + 1])
                e.drain().then_inc(sem["ss"], 1)
                e.wait_ge(sem["vs"], v0 + 3 * tt + 1)
                e.activation(areg[:, tt * 512:(tt + 1) * 512], areg[:, tt * 512:(tt + 1) * 512],
                             AF.Exp, scale=8.0)
                e.drain().then_inc(sem["ss"], 1)
                e.wait_ge(sem["vs"], v0 + 3 * tt + 2)
                e.activation(t8c[:, :], t8d[:, :], AF.Sqrt)
                e.drain().then_inc(sem["ss"], 1)
            e.wait_ge(sem["vs"], v0 + 3 * NT + 1)
            for tt in range(NT):
                if tt > 0:
                    e.wait_ge(sem["ts"], t0 + NT + tt)
                e.activation(t8a[:, :], hsreg[:, tt * 512:(tt + 1) * 512], AF.Square)
                e.drain().then_inc(sem["ss"], 1)
        p.on("scalar", p4_scal)
        p.add("ss", 6 * NT)

        def p4_vec(e, c=c, t0=t0, s0=s0):
            for tt in range(NT):
                e.wait_ge(sem["ss"], s0 + 5 * tt + 3)
                e.tensor_tensor(greg[:, tt * 512:(tt + 1) * 512], t8b[:, :], t8a[:, :], ALU.mult)
                e.tensor_scalar(areg[:, tt * 512:(tt + 1) * 512], t8c[:, :],
                                logac[:, c:c + 1], None, ALU.mult)
                e.drain().then_inc(sem["vs"], 1)
                e.wait_ge(sem["ss"], s0 + 5 * tt + 4)
                e.tensor_tensor(t8d[:, :], areg[:, tt * 512:(tt + 1) * 512],
                                areg[:, tt * 512:(tt + 1) * 512], ALU.mult)
                e.tensor_scalar(t8d[:, :], t8d[:, :], -1.0, float(1.0 + 1e-6), ALU.mult, ALU.add)
                e.drain().then_inc(sem["vs"], 1)
                e.wait_ge(sem["ss"], s0 + 5 * tt + 5)
                e.tensor_tensor(greg[:, tt * 512:(tt + 1) * 512],
                                greg[:, tt * 512:(tt + 1) * 512], t8c[:, :], ALU.mult)
                e.drain().then_inc(sem["vs"], 1)
            for tt in range(NT):
                init = h0c_col(c) if tt == 0 else hsreg[:, tt * 512 - 1:tt * 512]
                e.tensor_tensor_scan(hsreg[:, tt * 512:(tt + 1) * 512],
                                     areg[:, tt * 512:(tt + 1) * 512],
                                     greg[:, tt * 512:(tt + 1) * 512],
                                     init, ALU.mult, ALU.add)
                fz(e)
            e.tensor_copy(bf16a[:, c * T:(c + 1) * T], hsreg[:, 0:T])
            e.drain().then_inc(sem["vs"], 1)
            for tt in range(NT):
                e.wait_ge(sem["ts"], t0 + NT + tt + 1)
                if c == 0:
                    e.tensor_copy(rmsacc[:, tt * 512:(tt + 1) * 512], pb[5][:, :])
                else:
                    e.tensor_tensor(rmsacc[:, tt * 512:(tt + 1) * 512],
                                    rmsacc[:, tt * 512:(tt + 1) * 512], pb[5][:, :], ALU.add)
                e.drain().then_inc(sem["vs"], 1)
        p.on("vector", p4_vec)
        p.add("vs", 3 * NT + 1 + NT)
        p.barrier()

    v0 = p.get("vs"); s0 = p.get("ss")

    def p5_vec(e, v0=v0, s0=s0):
        for tt in range(NT):
            e.tensor_scalar(t8a[:, :], rmsacc[:, tt * 512:(tt + 1) * 512],
                            float(1.0 / D), 1e-6, ALU.mult, ALU.add)
            e.drain().then_inc(sem["vs"], 1)
            e.wait_ge(sem["ss"], s0 + tt + 1)
            e.reciprocal(rmsacc[:, tt * 512:(tt + 1) * 512], t8b[:, :])
            e.drain().then_inc(sem["vs"], 1)
    p.on("vector", p5_vec)
    p.add("vs", 2 * NT)

    def p5_scal(e, v0=v0):
        for tt in range(NT):
            e.wait_ge(sem["vs"], v0 + 2 * tt + 1)
            e.activation(t8b[:, :], t8a[:, :], AF.Sqrt)
            e.drain().then_inc(sem["ss"], 1)
    p.on("scalar", p5_scal)
    p.add("ss", NT)
    p.barrier()

    xor_ = xout.rearrange("(c p) (u n) -> u c p n", c=8, u=NT)
    for tt in range(NT):
        l0 = p.get("ld"); w0 = p.get("wl"); t0 = p.get("ts")
        v0 = p.get("vs"); s0 = p.get("ss")

        def p6_vec(e, tt=tt, s0=s0, l0=l0, st0=p.get("st")):
            for c in range(8):
                e.tensor_tensor(t8a[:, :], bf16a[:, c * T + tt * 512: c * T + (tt + 1) * 512],
                                rmsacc[:, tt * 512:(tt + 1) * 512], ALU.mult)
                e.tensor_scalar(xnt[:, c * 512:(c + 1) * 512], t8a[:, :],
                                CB_nw[:, c:c + 1], None, ALU.mult)
            e.drain().then_inc(sem["vs"], 1)
            for oc in range(8):
                e.wait_ge(sem["ss"], s0 + oc + 1)
                e.wait_ge(sem["ld"], l0 + 16 * (oc + 1))
                if oc > 0:
                    e.wait_ge(sem["st"], st0 + 16 * oc)
                e.tensor_tensor(t8c[:, :], t8b[:, :], fA(0)[:, oc * 512:(oc + 1) * 512], ALU.add)
                e.drain().then_inc(sem["vs"], 1)
        p.on("vector", p6_vec)
        p.add("vs", 9)

        def p6_gp(e, w0=w0, t0=t0):
            for oc in range(8):
                if oc >= 2:
                    e.wait_ge(sem["ts"], t0 + oc - 1)
                e.dma_start(out=wdb[:, (oc % 2) * 1024:(oc % 2) * 1024 + 1024].rearrange("p (k n) -> p k n", k=8),
                            in_=WoutT.rearrange("(k p) n -> p k n", k=8)[:, :, oc * 128:(oc + 1) * 128]
                            ).then_inc(sem["wl"], 16)
        p.on("gpsimd", p6_gp)
        p.add("wl", 8 * 16)

        def p6_sync(e, tt=tt, v0=v0):
            for oc in range(8):
                e.dma_start(out=fA(0)[:, oc * 512:(oc + 1) * 512],
                            in_=x1r[tt, oc]).then_inc(sem["ld"], 16)
            for oc in range(8):
                e.wait_ge(sem["vs"], v0 + 2 + oc)
                e.dma_start(out=xor_[tt, oc], in_=t8c[:, :]).then_inc(sem["st"], 16)
        p.on("sync", p6_sync)
        p.add("ld", 8 * 16)
        p.add("st", 8 * 16)

        def p6_tens(e, w0=w0, v0=v0, s0=s0):
            e.wait_ge(sem["vs"], v0 + 1)
            for oc in range(8):
                e.wait_ge(sem["wl"], w0 + 16 * (oc + 1))
                if oc >= 4:
                    e.wait_ge(sem["ss"], s0 + oc - 3)
                for k in range(8):
                    e.matmul(pb[oc % 4][:, :], wdb[:, (oc % 2) * 1024 + k * 128:(oc % 2) * 1024 + (k + 1) * 128],
                             xnt[:, k * 512:(k + 1) * 512], start=(k == 0), stop=(k == 7))
                e.drain().then_inc(sem["ts"], 1)
        p.on("tensor", p6_tens)
        p.add("ts", 8)

        def p6_scal(e, t0=t0, v0=v0):
            for oc in range(8):
                e.wait_ge(sem["ts"], t0 + oc + 1)
                if oc > 0:
                    e.wait_ge(sem["vs"], v0 + 1 + oc)
                e.activation(t8b[:, :], pb[oc % 4][:, :], AF.Copy)
                e.drain().then_inc(sem["ss"], 1)
        p.on("scalar", p6_scal)
        p.add("ss", 8)
        p.barrier()

    nst6 = p.get("st")
    def p6_fin(e, nst6=nst6):
        e.wait_ge(sem["st"], nst6)
    p.on("sync", p6_fin)
    p.barrier()

    if debug:
        dbg_x2a = p.dram_out("dbg_x2a", (D, T))
        dbg_hs = p.dram_out("dbg_hs", (8, 128, T), BF16)
        dbg_rs = p.dram_out("dbg_rs", (128, T))
        dbg_xnt = p.dram_out("dbg_xnt", (128, 4096), BF16)

        def pdbg_sync(e):
            e.dma_start(out=dbg_x2a, in_=xout).then_inc(sem["st"], 16)
            e.dma_start(out=dbg_hs.rearrange("c p n -> p c n"),
                        in_=bf16a[:, 0:8 * T].rearrange("p (c n) -> p c n", c=8)
                        ).then_inc(sem["st"], 16)
            e.dma_start(out=dbg_rs, in_=rmsacc[:, 0:T]).then_inc(sem["st"], 16)
            e.dma_start(out=dbg_xnt, in_=xnt[:, 0:4096]).then_inc(sem["st"], 16)
        p.on("sync", pdbg_sync)
        p.add("st", 64)
        p.barrier()
    qw1r = qw1d.rearrange("(k p) n -> k p n", k=8)
    qw2r = qw2d.rearrange("(k p) n -> k p n", k=32)
    for (Wt, KC, WD, qwr, swi) in ((w1T, 8, H1, qw1r, 0), (w2T, 32, D, qw2r, 2)):
        l0 = p.get("ld"); v0 = p.get("vs"); t0 = p.get("ts")
        Wr = Wt.rearrange("(k p) n -> k p n", k=KC)

        def p7a_sync(e, Wr=Wr, KC=KC, WD=WD, v0=v0):
            for k in range(KC):
                if k > 0:
                    e.wait_ge(sem["vs"], v0 + k)
                e.dma_start(out=fA(0)[:, 0:WD], in_=Wr[k]).then_inc(sem["ld"], 16)
        p.on("sync", p7a_sync)
        p.add("ld", 16 * KC)

        def p7a_vec(e, KC=KC, WD=WD, l0=l0, t0=t0, swi=swi):
            for k in range(KC):
                e.wait_ge(sem["ld"], l0 + 16 * (k + 1))
                e.scalar_tensor_tensor(fA(1)[:, 0:WD], fA(0)[:, 0:WD], -1.0,
                                       fA(0)[:, 0:WD], ALU.mult, ALU.max)
                e.tensor_reduce(wcol[:, k:k + 1], fA(1)[:, 0:WD], AX.X, ALU.add)
                e.drain().then_inc(sem["vs"], 1)
            e.wait_ge(sem["ts"], t0 + 1)
            fz(e)
            e.tensor_reduce(swc[:, swi + 1:swi + 2], pb[5][:, 0:KC], AX.X, ALU.add)
            fz(e)
            e.tensor_scalar(swc[:, swi + 1:swi + 2], swc[:, swi + 1:swi + 2],
                            float(1.0 / (D * H1)), 1e-5, ALU.mult, ALU.max)
            fz(e)
            e.reciprocal(swc[:, swi:swi + 1], swc[:, swi + 1:swi + 2])
            fz(e)
            e.drain().then_inc(sem["vs"], 1)
        p.on("vector", p7a_vec)
        p.add("vs", KC + 1)

        def p7a_tens(e, KC=KC, v0=v0):
            e.wait_ge(sem["vs"], v0 + KC)
            e.matmul(pb[5][:, 0:KC], onesf[:, :], wcol[:, 0:KC], start=True, stop=True)
            e.drain().then_inc(sem["ts"], 1)
        p.on("tensor", p7a_tens)
        p.add("ts", 1)
        p.barrier()

        l0 = p.get("ld"); v0 = p.get("vs"); st0 = p.get("st")

        def p7b_sync(e, Wr=Wr, qwr=qwr, KC=KC, WD=WD, v0=v0):
            for k in range(KC):
                if k > 0:
                    e.wait_ge(sem["vs"], v0 + k)
                e.dma_start(out=fA(0)[:, 0:WD], in_=Wr[k]).then_inc(sem["ld"], 16)
                e.wait_ge(sem["vs"], v0 + k + 1)
                e.dma_start(out=qwr[k], in_=xnt[:, 0:WD]).then_inc(sem["st"], 16)
        p.on("sync", p7b_sync)
        p.add("ld", 16 * KC)
        p.add("st", 16 * KC)

        def p7b_vec(e, KC=KC, WD=WD, l0=l0, st0=st0, swi=swi):
            for k in range(KC):
                e.wait_ge(sem["ld"], l0 + 16 * (k + 1))
                if k > 0:
                    e.wait_ge(sem["st"], st0 + 16 * k)
                e.tensor_scalar(fA(1)[:, 0:WD], fA(0)[:, 0:WD], swc[:, swi:swi + 1],
                                None, ALU.mult)
                e.tensor_scalar(fA(2)[:, 0:WD], fA(1)[:, 0:WD], 0.5, None, ALU.is_ge)
                e.tensor_scalar(fA(3)[:, 0:WD], fA(1)[:, 0:WD], -0.5, None, ALU.is_lt)
                e.tensor_tensor(xnt[:, 0:WD], fA(2)[:, 0:WD], fA(3)[:, 0:WD], ALU.subtract)
                e.drain().then_inc(sem["vs"], 1)
        p.on("vector", p7b_vec)
        p.add("vs", KC)
        p.barrier()

    BITS = [128.0, 64.0, 32.0, 16.0, 8.0, 4.0, 2.0, 1.0]
    o1v = bf16a[:, 0:16384]
    q2v = bf16a[:, 16384:32768]

    def ladder(e, ybuf, qdst):
        e.tensor_scalar(t8c[:, :], ybuf, 1.0, 128.5, ALU.mult, ALU.add)
        e.tensor_copy(t8d[:, :], t8c[:, :])
        for b in BITS:
            e.tensor_scalar(t8b[:, :], t8d[:, :], float(b), None, ALU.is_ge)
            e.scalar_tensor_tensor(t8d[:, :], t8b[:, :], float(-b), t8d[:, :],
                                   ALU.mult, ALU.add)
        e.tensor_tensor(t8c[:, :], t8c[:, :], t8d[:, :], ALU.subtract)
        e.tensor_scalar(qdst, t8c[:, :], 1.0, -128.0, ALU.mult, ALU.add)

    for tt in range(NT):
        sl = slice(tt * 512, (tt + 1) * 512)
        l0 = p.get("ld"); v0 = p.get("vs"); t0 = p.get("ts"); s0 = p.get("ss")

        def p8a_sync(e, tt=tt):
            e.dma_start(out=fA(1).rearrange("p (c n) -> p c n", c=8),
                        in_=x1r[tt].rearrange("c p n -> p c n")).then_inc(sem["ld"], 16)
        p.on("sync", p8a_sync)
        p.add("ld", 16)

        def p8a_vec(e, tt=tt, sl=sl, l0=l0, t0=t0, s0=s0):
            e.wait_ge(sem["ld"], l0 + 16)
            for c in range(8):
                x1c = fA(1)[:, c * 512:(c + 1) * 512]
                hc = fA(0)[:, c * 512:(c + 1) * 512]
                e.tensor_tensor(t8a[:, :], x1c, CAt[:, sl], ALU.mult)
                e.tensor_tensor(t8a[:, :], t8a[:, :], CBt[:, sl], ALU.subtract)
                e.tensor_scalar(t8a[:, :], t8a[:, :], CB_g1[:, c:c + 1], None, ALU.mult)
                e.tensor_scalar(hc, t8a[:, :], CB_b1[:, c:c + 1], None, ALU.add)
                if c == 0:
                    e.scalar_tensor_tensor(t8b[:, :], hc, -1.0, hc, ALU.mult, ALU.max)
                else:
                    e.scalar_tensor_tensor(t8c[:, :], hc, -1.0, hc, ALU.mult, ALU.max)
                    e.tensor_tensor(t8b[:, :], t8b[:, :], t8c[:, :], ALU.max)
            e.drain().then_inc(sem["vs"], 1)
            e.wait_ge(sem["ts"], t0 + 1)
            for blk in range(4):
                e.tensor_reduce(wcol[:, blk:blk + 1], pb[6][:, blk * 128:(blk + 1) * 128],
                                AX.X, ALU.max)
                fz(e)
            e.tensor_scalar(wcol[:, 0:4], wcol[:, 0:4], 1e-5, None, ALU.max)
            fz(e)
            e.reciprocal(wcol[:, 4:8], wcol[:, 0:4])
            fz(e)
            e.tensor_scalar(wcol[:, 4:8], wcol[:, 4:8], 127.0, None, ALU.mult)
            e.tensor_scalar(wcol[:, 8:12], wcol[:, 0:4], float(1.0 / 127.0), None, ALU.mult)
            e.drain().then_inc(sem["vs"], 1)
            e.wait_ge(sem["ts"], t0 + 3)
            e.tensor_copy(amr[:, :], pb[4][:, :])
            e.tensor_copy(am2r[:, :], pb[5][:, :])
            e.drain().then_inc(sem["vs"], 1)
            for c in range(8):
                hc = fA(0)[:, c * 512:(c + 1) * 512]
                e.tensor_tensor(fA(2)[:, 0:512], hc, amr[:, :], ALU.mult)
                ladder(e, fA(2)[:, 0:512], qt[:, c * 512:(c + 1) * 512])
            e.drain().then_inc(sem["vs"], 1)
        p.on("vector", p8a_vec)
        p.add("vs", 4)

        def p8a_tens(e, v0=v0, s0=s0):
            e.wait_ge(sem["vs"], v0 + 1)
            for blk in range(4):
                e.transpose(pb[6][:, blk * 128:(blk + 1) * 128],
                            t8b[:, blk * 128:(blk + 1) * 128], eyes[:, :])
            e.drain().then_inc(sem["ts"], 1)
            e.wait_ge(sem["vs"], v0 + 2)
            for blk in range(4):
                e.transpose(pb[7][0:1, blk * 128:(blk + 1) * 128],
                            wcol[:, 4 + blk:5 + blk], eyes[:, :])
                e.transpose(pb[3][0:1, blk * 128:(blk + 1) * 128],
                            wcol[:, 8 + blk:9 + blk], eyes[:, :])
            e.drain().then_inc(sem["ts"], 1)
            e.wait_ge(sem["ss"], s0 + 1)
            bc(e, pb[4][:, :], ones1, t8d[0:1, :])
            bc(e, pb[5][:, :], ones1, t8c[0:1, :])
            e.drain().then_inc(sem["ts"], 1)
        p.on("tensor", p8a_tens)
        p.add("ts", 3)

        def p8a_scal(e, t0=t0):
            e.wait_ge(sem["ts"], t0 + 2)
            e.activation(t8d[0:1, :], pb[7][0:1, 0:512], AF.Copy)
            e.activation(t8c[0:1, :], pb[3][0:1, 0:512], AF.Copy)
            e.drain().then_inc(sem["ss"], 1)
        p.on("scalar", p8a_scal)
        p.add("ss", 1)
        p.barrier()

        l0 = p.get("ld"); v0 = p.get("vs"); t0 = p.get("ts"); s0 = p.get("ss")

        def p8b_sync(e, tt=tt, t0=t0):
            for oc in range(NH1C):
                if oc >= 2:
                    e.wait_ge(sem["ts"], t0 + 2 * (oc - 1))
                e.dma_start(out=wdb[:, (oc % 2) * 1024:(oc % 2) * 1024 + 1024].rearrange("p (k n) -> p k n", k=8),
                            in_=qw1d.rearrange("(k p) (o n) -> o p k n", k=8, o=NH1C)[oc]
                            ).then_inc(sem["ld"], 16)
        p.on("sync", p8b_sync)
        p.add("ld", 16 * NH1C)

        def p8b_tens(e, l0=l0, s0=s0, v0=v0):
            for oc in range(NH1C):
                e.wait_ge(sem["ld"], l0 + 16 * (oc + 1))
                if oc >= 4:
                    e.wait_ge(sem["ss"], s0 + 2 * (oc - 4) + 1)
                for k in range(8):
                    e.matmul(pb[oc % 4][:, :], wdb[:, (oc % 2) * 1024 + k * 128:(oc % 2) * 1024 + (k + 1) * 128],
                             qt[:, k * 512:(k + 1) * 512], start=(k == 0), stop=(k == 7))
                e.drain().then_inc(sem["ts"], 1)
                e.wait_ge(sem["ss"], s0 + 2 * oc + 2)
                e.matmul(pb[6][:, :], onesb[:, :], o1v[:, oc * 512:(oc + 1) * 512],
                         start=(oc == 0), stop=(oc == NH1C - 1))
                e.matmul(pb[7][:, :], onesf[:, :], t8a[:, :],
                         start=(oc == 0), stop=(oc == NH1C - 1))
                e.drain().then_inc(sem["ts"], 1)
        p.on("tensor", p8b_tens)
        p.add("ts", 2 * NH1C)

        def p8b_scal(e, t0=t0, v0=v0):
            for oc in range(NH1C):
                e.wait_ge(sem["ts"], t0 + 2 * oc + 1)
                if oc > 0:
                    e.wait_ge(sem["vs"], v0 + oc)
                e.activation(t8b[:, :], pb[oc % 4][:, :], AF.Copy)
                e.drain().then_inc(sem["ss"], 1)
                e.wait_ge(sem["vs"], v0 + oc + 1)
                if oc > 0:
                    e.wait_ge(sem["ts"], t0 + 2 * oc)
                e.activation(o1v[:, oc * 512:(oc + 1) * 512], t8b[:, :], AF.Gelu)
                e.activation(t8a[:, :], o1v[:, oc * 512:(oc + 1) * 512], AF.Square)
                e.drain().then_inc(sem["ss"], 1)
        p.on("scalar", p8b_scal)
        p.add("ss", 2 * NH1C)

        def p8b_vec(e, s0=s0):
            for oc in range(NH1C):
                e.wait_ge(sem["ss"], s0 + 2 * oc + 1)
                e.tensor_tensor(t8b[:, :], t8b[:, :], am2r[:, :], ALU.mult)
                e.tensor_scalar(t8b[:, :], t8b[:, :], swc[:, 1:2], None, ALU.mult)
                e.tensor_scalar(t8b[:, :], t8b[:, :], CB_bb1[:, oc:oc + 1], None, ALU.add)
                e.drain().then_inc(sem["vs"], 1)
        p.on("vector", p8b_vec)
        p.add("vs", NH1C)
        p.barrier()

        v0 = p.get("vs"); s0 = p.get("ss")

        def p8c_vec(e, v0=v0, s0=s0):
            e.tensor_scalar(murb[:, :], pb[6][:, :], float(1.0 / H1), None, ALU.mult)
            e.tensor_scalar(t8a[:, :], pb[7][:, :], float(1.0 / H1), None, ALU.mult)
            e.tensor_tensor(t8d[:, :], murb[:, :], murb[:, :], ALU.mult)
            e.tensor_tensor(t8a[:, :], t8a[:, :], t8d[:, :], ALU.subtract)
            e.tensor_scalar(t8a[:, :], t8a[:, :], EPS, None, ALU.add)
            e.drain().then_inc(sem["vs"], 1)
            e.wait_ge(sem["ss"], s0 + 1)
            e.reciprocal(rsrb[:, :], t8b[:, :])
            e.drain().then_inc(sem["vs"], 1)
        p.on("vector", p8c_vec)
        p.add("vs", 2)

        def p8c_scal(e, v0=v0):
            e.wait_ge(sem["vs"], v0 + 1)
            e.activation(t8b[:, :], t8a[:, :], AF.Sqrt)
            e.drain().then_inc(sem["ss"], 1)
        p.on("scalar", p8c_scal)
        p.add("ss", 1)
        p.barrier()

        v0 = p.get("vs"); t0 = p.get("ts"); s0 = p.get("ss")

        def xn2_into(e, oc, dst):
            oslc = slice(oc * 512, (oc + 1) * 512)
            e.tensor_tensor(dst, o1v[:, oslc], murb[:, :], ALU.subtract)
            e.tensor_tensor(dst, dst, rsrb[:, :], ALU.mult)
            e.tensor_scalar(dst, dst, CB_g2[:, oc:oc + 1], None, ALU.mult)
            e.tensor_scalar(dst, dst, CB_b2[:, oc:oc + 1], None, ALU.add)

        def p8d_vec(e, t0=t0, s0=s0):
            for oc in range(NH1C):
                xn2_into(e, oc, fA(2)[:, 0:512])
                if oc == 0:
                    e.scalar_tensor_tensor(t8b[:, :], fA(2)[:, 0:512], -1.0,
                                           fA(2)[:, 0:512], ALU.mult, ALU.max)
                else:
                    e.scalar_tensor_tensor(t8c[:, :], fA(2)[:, 0:512], -1.0,
                                           fA(2)[:, 0:512], ALU.mult, ALU.max)
                    e.tensor_tensor(t8b[:, :], t8b[:, :], t8c[:, :], ALU.max)
            e.drain().then_inc(sem["vs"], 1)
            e.wait_ge(sem["ts"], t0 + 1)
            for blk in range(4):
                e.tensor_reduce(wcol[:, blk:blk + 1], pb[6][:, blk * 128:(blk + 1) * 128],
                                AX.X, ALU.max)
                fz(e)
            e.tensor_scalar(wcol[:, 0:4], wcol[:, 0:4], 1e-5, None, ALU.max)
            fz(e)
            e.reciprocal(wcol[:, 4:8], wcol[:, 0:4])
            fz(e)
            e.tensor_scalar(wcol[:, 4:8], wcol[:, 4:8], 127.0, None, ALU.mult)
            e.tensor_scalar(wcol[:, 8:12], wcol[:, 0:4], float(1.0 / 127.0), None, ALU.mult)
            e.drain().then_inc(sem["vs"], 1)
            e.wait_ge(sem["ts"], t0 + 3)
            e.tensor_copy(amr[:, :], pb[4][:, :])
            e.tensor_copy(am2r[:, :], pb[5][:, :])
            e.drain().then_inc(sem["vs"], 1)
            for oc in range(NH1C):
                xn2_into(e, oc, fA(2)[:, 0:512])
                e.tensor_tensor(fA(2)[:, 0:512], fA(2)[:, 0:512], amr[:, :], ALU.mult)
                ladder(e, fA(2)[:, 0:512], q2v[:, oc * 512:(oc + 1) * 512])
            e.drain().then_inc(sem["vs"], 1)
        p.on("vector", p8d_vec)
        p.add("vs", 4)

        def p8d_tens(e, v0=v0, s0=s0):
            e.wait_ge(sem["vs"], v0 + 1)
            for blk in range(4):
                e.transpose(pb[6][:, blk * 128:(blk + 1) * 128],
                            t8b[:, blk * 128:(blk + 1) * 128], eyes[:, :])
            e.drain().then_inc(sem["ts"], 1)
            e.wait_ge(sem["vs"], v0 + 2)
            for blk in range(4):
                e.transpose(pb[7][0:1, blk * 128:(blk + 1) * 128],
                            wcol[:, 4 + blk:5 + blk], eyes[:, :])
                e.transpose(pb[3][0:1, blk * 128:(blk + 1) * 128],
                            wcol[:, 8 + blk:9 + blk], eyes[:, :])
            e.drain().then_inc(sem["ts"], 1)
            e.wait_ge(sem["ss"], s0 + 1)
            bc(e, pb[4][:, :], ones1, t8d[0:1, :])
            bc(e, pb[5][:, :], ones1, t8c[0:1, :])
            e.drain().then_inc(sem["ts"], 1)
        p.on("tensor", p8d_tens)
        p.add("ts", 3)

        def p8d_scal(e, t0=t0):
            e.wait_ge(sem["ts"], t0 + 2)
            e.activation(t8d[0:1, :], pb[7][0:1, 0:512], AF.Copy)
            e.activation(t8c[0:1, :], pb[3][0:1, 0:512], AF.Copy)
            e.drain().then_inc(sem["ss"], 1)
        p.on("scalar", p8d_scal)
        p.add("ss", 1)
        p.barrier()

        l0 = p.get("ld"); v0 = p.get("vs"); t0 = p.get("ts")
        s0 = p.get("ss"); st0 = p.get("st")

        def p8e_sync(e, tt=tt, t0=t0, v0=v0):
            for c in range(8):
                if c >= 2:
                    e.wait_ge(sem["ts"], t0 + c - 1)
                e.dma_start(out=wdb[:, (c % 2) * 4096:(c % 2) * 4096 + 4096].rearrange("p (k n) -> p k n", k=32),
                            in_=qw2d.rearrange("(k p) (o n) -> o p k n", k=32, o=8)[c]
                            ).then_inc(sem["ld"], 16)
                e.dma_start(out=fA(1)[:, c * 512:(c + 1) * 512], in_=xor_[tt, c]).then_inc(sem["ld"], 16)
            for c in range(8):
                e.wait_ge(sem["vs"], v0 + c + 1)
                e.dma_start(out=xor_[tt, c], in_=t8d[:, :]).then_inc(sem["st"], 16)
        p.on("sync", p8e_sync)
        p.add("ld", 32 * 8)
        p.add("st", 16 * 8)

        def p8e_tens(e, l0=l0, s0=s0):
            for c in range(8):
                e.wait_ge(sem["ld"], l0 + 32 * (c + 1))
                if c >= 4:
                    e.wait_ge(sem["ss"], s0 + c - 3)
                for k in range(32):
                    e.matmul(pb[c % 4][:, :], wdb[:, (c % 2) * 4096 + k * 128:(c % 2) * 4096 + (k + 1) * 128],
                             q2v[:, k * 512:(k + 1) * 512], start=(k == 0), stop=(k == 31))
                e.drain().then_inc(sem["ts"], 1)
        p.on("tensor", p8e_tens)
        p.add("ts", 8)

        def p8e_scal(e, t0=t0, v0=v0):
            for c in range(8):
                e.wait_ge(sem["ts"], t0 + c + 1)
                if c > 0:
                    e.wait_ge(sem["vs"], v0 + c)
                e.activation(t8b[:, :], pb[c % 4][:, :], AF.Copy)
                e.drain().then_inc(sem["ss"], 1)
        p.on("scalar", p8e_scal)
        p.add("ss", 8)

        def p8e_vec(e, s0=s0, l0=l0, st0=st0):
            for c in range(8):
                e.wait_ge(sem["ss"], s0 + c + 1)
                e.wait_ge(sem["ld"], l0 + 32 * (c + 1))
                if c > 0:
                    e.wait_ge(sem["st"], st0 + 16 * c)
                e.tensor_tensor(t8d[:, :], t8b[:, :], am2r[:, :], ALU.mult)
                e.tensor_scalar(t8d[:, :], t8d[:, :], swc[:, 3:4], None, ALU.mult)
                e.tensor_scalar(t8d[:, :], t8d[:, :], CB_bb2[:, c:c + 1], None, ALU.add)
                e.tensor_tensor(t8d[:, :], t8d[:, :], fA(1)[:, c * 512:(c + 1) * 512], ALU.add)
                e.drain().then_inc(sem["vs"], 1)
        p.on("vector", p8e_vec)
        p.add("vs", 8)
        p.barrier()

    if debug:
        dbg_q1 = p.dram_out("dbg_q1", (128, 4096), BF16)
        dbg_o1 = p.dram_out("dbg_o1", (128, 16384), BF16)
        dbg_q2 = p.dram_out("dbg_q2", (128, 16384), BF16)
        dbg_amr = p.dram_out("dbg_amr", (128, 512))
        dbg_am2r = p.dram_out("dbg_am2r", (128, 512))
        dbg_mur = p.dram_out("dbg_mur", (128, 512))
        dbg_rsr = p.dram_out("dbg_rsr", (128, 512))
        dbg_sw = p.dram_out("dbg_sw", (128, 8))

        def pdbg2_sync(e):
            e.dma_start(out=dbg_q1, in_=qt[:, 0:4096]).then_inc(sem["st"], 16)
            e.dma_start(out=dbg_o1, in_=o1v[:, :]).then_inc(sem["st"], 16)
            e.dma_start(out=dbg_q2, in_=q2v[:, :]).then_inc(sem["st"], 16)
            e.dma_start(out=dbg_amr, in_=amr[:, :]).then_inc(sem["st"], 16)
            e.dma_start(out=dbg_am2r, in_=am2r[:, :]).then_inc(sem["st"], 16)
            e.dma_start(out=dbg_mur, in_=murb[:, :]).then_inc(sem["st"], 16)
            e.dma_start(out=dbg_rsr, in_=rsrb[:, :]).then_inc(sem["st"], 16)
            e.dma_start(out=dbg_sw, in_=swc[:, :]).then_inc(sem["st"], 16)
        p.on("sync", pdbg2_sync)
        p.add("st", 8 * 16)
        p.barrier()

    nstF = p.get("st")
    def p8_fin(e, nstF=nstF):
        e.wait_ge(sem["st"], nstF)
    p.on("sync", p8_fin)
    return p, dict()


def stage_B(d, x, OT, ffn0, stB):
    F = np.float32
    return dict(
        x_fm=np.ascontiguousarray(x.T.astype(F)),
        OT_in=np.ascontiguousarray(OT),
        ffn0_in=np.ascontiguousarray(ffn0.astype(F)),
        stB=np.ascontiguousarray(stB.astype(F)),
        wo0T=np.ascontiguousarray(d["L0_attn_w_out"].astype(F)),
        b_out0c=np.ascontiguousarray(d["L0_attn_b_out"].astype(F).reshape(8, 128).T),
        WinT=np.ascontiguousarray(d["L1_attn_W_in"].T.astype(F)),
        WigT=np.ascontiguousarray(d["L1_attn_W_ig"].T.astype(F)),
        WrgT=np.ascontiguousarray(d["L1_attn_W_rg"].T.astype(F)),
        WoutT=np.ascontiguousarray(d["L1_attn_W_out"].T.astype(F)),
        bigc=np.ascontiguousarray(d["L1_attn_b_ig"].astype(F).reshape(8, 128).T),
        brgc=np.ascontiguousarray(d["L1_attn_b_rg"].astype(F).reshape(8, 128).T),
        lamc=np.ascontiguousarray(d["L1_attn_lam"].astype(F).reshape(8, 128).T),
        h0c=np.ascontiguousarray(d["L1_attn_h0"][0, 0].astype(F).reshape(8, 128).T),
        nwc=np.ascontiguousarray(d["L1_attn_norm_w"].astype(F).reshape(8, 128).T),
        g1c=np.ascontiguousarray(d["L1_ffn_ln1_g"].astype(F).reshape(8, 128).T),
        b1c=np.ascontiguousarray(d["L1_ffn_ln1_b"].astype(F).reshape(8, 128).T),
        bb1c=np.ascontiguousarray(d["L1_ffn_b1"].astype(F).reshape(32, 128).T),
        g2c=np.ascontiguousarray(d["L1_ffn_ln2_g"].astype(F).reshape(32, 128).T),
        b2c=np.ascontiguousarray(d["L1_ffn_ln2_b"].astype(F).reshape(32, 128).T),
        bb2c=np.ascontiguousarray(d["L1_ffn_b2"].astype(F).reshape(8, 128).T),
        w1T=np.ascontiguousarray(d["L1_ffn_w1"].T.astype(F)),
        w2T=np.ascontiguousarray(d["L1_ffn_w2"].T.astype(F)),
        eye128=np.eye(128, dtype=F),
    )



def stage_A_full(d, core, T):
    F = np.float32
    TH = T // 2
    b, half = core // 2, core % 2
    heads = range(half * 8, half * 8 + 8)
    x = np.ascontiguousarray(d["L0_attn_x"][b, :T].astype(F))
    x_fm = np.ascontiguousarray(x.T)
    x_own = np.ascontiguousarray(x_fm[:, half * TH:(half + 1) * TH])
    wq = d["L0_attn_w_qkvb"]; bq = d["L0_attn_b_qkvb"]
    wcols, bcols = [], []
    for qq in range(4):
        for h in heads:
            base = qq * 1024 + h * 64
            wcols.append(wq[:, base:base + 64])
            bcols.append(bq[base:base + 64])
    wqkv = np.ascontiguousarray(np.concatenate(wcols, axis=1))
    bqkv = np.ascontiguousarray(np.concatenate(bcols))
    gu = np.empty((34, 1024, 1024), F)
    dn = np.empty((34, 512, 1024), F)
    for e in range(32):
        gu[e, :, 0:512] = d["L0_ffn_gate_w"][e].T
        gu[e, :, 512:1024] = d["L0_ffn_up_w"][e].T
        dn[e] = d["L0_ffn_down_w"][e].T
    sgT = d["L0_ffn_shared_gate_w"].T; suT = d["L0_ffn_shared_up_w"].T
    sdT = d["L0_ffn_shared_down_w"].T
    gu[32, :, 0:512] = sgT[:, 0:512]; gu[32, :, 512:1024] = suT[:, 0:512]
    gu[33, :, 0:512] = sgT[:, 512:1024]; gu[33, :, 512:1024] = suT[:, 512:1024]
    dn[32] = sdT[0:512]; dn[33] = sdT[512:1024]
    m = np.arange(16)
    msku = (m[:, None] < m[None, :]).astype(F)
    mskg = (m[:, None] <= m[None, :]).astype(F)
    return dict(
        x_fm=x_fm, x_own=x_own, wqkv=wqkv,
        bqkv=np.ascontiguousarray(bqkv.reshape(32, 64).T),
        ln_g=np.ascontiguousarray(d["L0_attn_ln_g"].astype(F).reshape(8, 128).T),
        ln_b=np.ascontiguousarray(d["L0_attn_ln_b"].astype(F).reshape(8, 128).T),
        rwT=np.ascontiguousarray(d["L0_ffn_router_weight"].T.astype(F)),
        rbias=np.ascontiguousarray(np.repeat(d["L0_ffn_router_bias"].reshape(1, 32), 128, 0).astype(F)),
        gu_w=gu, dn_w=dn,
        msku_in=np.ascontiguousarray(np.tile(msku, (1, 64))).astype(ml_dtypes.bfloat16),
        mskg_in=np.ascontiguousarray(np.tile(mskg, (1, 64))).astype(ml_dtypes.bfloat16),
        eye64=np.eye(64, dtype=F),
    )


def kernel(**inputs):
    d = inputs
    T = 4096
    TH = T // 2

    imsA = [stage_A_full(d, core, T) for core in range(8)]
    pA, _ = build_A(T)
    ncA = pA.finalize()
    resA = run_bass_kernel_spmd(ncA, imsA, core_ids=list(range(8)))

    imsB = []
    for core in range(8):
        b = core % 4
        rE = resA.results[2 * b]
        rO = resA.results[2 * b + 1]
        O = np.concatenate([np.asarray(rE["O"]), np.asarray(rO["O"])], axis=1)
        OT = np.ascontiguousarray(O.T)
        ffn0 = np.concatenate([np.asarray(rE["ffn"]).reshape(D, TH),
                               np.asarray(rO["ffn"]).reshape(D, TH)], axis=1)
        stB = np.concatenate([np.asarray(rE["stO"]), np.asarray(rO["stO"])], axis=1)
        x = d["L0_attn_x"][b].astype(np.float32)
        imsB.append(stage_B(d, x, OT, ffn0, stB))
    pB, _ = build_B(T)
    ncB = pB.finalize()
    resB = run_bass_kernel_spmd(ncB, imsB, core_ids=list(range(8)))

    out = np.stack([np.ascontiguousarray(np.asarray(resB.results[b]["xout"]).T)
                    for b in range(4)], axis=0)
    return out.astype(np.float32)


# version 07
# speedup vs baseline: 1.1804x; 1.1804x over previous
import numpy as np
import ml_dtypes
from contextlib import ExitStack

import concourse.bass as bass
import concourse.mybir as mybir
from concourse.bass_utils import run_bass_kernel_spmd

F32 = mybir.dt.float32
BF16 = mybir.dt.bfloat16
AF = mybir.ActivationFunctionType
ALU = mybir.AluOpType
AX = mybir.AxisListType
NPBF = ml_dtypes.bfloat16

D = 1024
NH_C = 8
HD = 64
FC = NH_C * HD
CC = 16
STRIP = 128
NE = 32
EI = 512
EPS = float(1e-5)
BIG = float(1e30)

ENGS = ("sync", "tensor", "vector", "scalar", "gpsimd")

R_CA, R_CB = 0, 1
R_CC, R_CD, R_CDN = 2, 3, 4
R_MU, R_EX2, R_VAR = 5, 6, 7
R_S0, R_S1, R_S2 = 8, 9, 10
R_NW0, R_NW1 = 11, 12
R_RB = 13
NROWS = 14


class Prog:
    def __init__(self):
        self.nc = bass.Bass("TRN2", target_bir_lowering=False, debug=False,
                            detect_race_conditions=False)
        self.es = ExitStack()
        self.es.__enter__()
        self.phases = [{e: [] for e in ENGS}]
        self.c = {}

    def add(self, name, n=1):
        self.c[name] = self.c.get(name, 0) + n
        return self.c[name]

    def get(self, name):
        return self.c.get(name, 0)

    def dram_in(self, name, shape, dt=F32):
        return self.nc.dram_tensor(name, list(shape), dt, kind="ExternalInput").ap()

    def dram_out(self, name, shape, dt=F32):
        return self.nc.dram_tensor(name, list(shape), dt, kind="ExternalOutput").ap()

    def sb(self, name, shape, dt=F32):
        return self.es.enter_context(self.nc.sbuf_tensor(name, list(shape), dt))

    def ps(self, name, shape, dt=F32):
        return self.es.enter_context(self.nc.psum_tensor(name, list(shape), dt))

    def sem(self, name):
        return self.es.enter_context(self.nc.semaphore(name))

    def on(self, eng, fn):
        self.phases[-1][eng].append(fn)

    def barrier(self):
        self.phases.append({e: [] for e in ENGS})

    def finalize(self):
        nc = self.nc
        ph = self.sem("ph")
        nphase = len(self.phases)
        block = self.es.enter_context(nc.Block())

        def stream(ename):
            def run(eng):
                for k in range(nphase):
                    if k > 0:
                        eng.wait_ge(ph, 5 * k)
                    for fn in self.phases[k][ename]:
                        fn(eng)
                    if k < nphase - 1:
                        if ename == "sync" or not self.phases[k][ename]:
                            eng.nop().then_inc(ph, 1)
                        else:
                            eng.drain().then_inc(ph, 1)
            return run

        block.sync(stream("sync"))
        block.tensor(stream("tensor"))
        block.vector(stream("vector"))
        block.scalar(stream("scalar"))
        block.gpsimd(stream("gpsimd"))
        self.es.__exit__(None, None, None)
        return nc


def bc(e, bank_ap, ones1, row_ap):
    m = bank_ap.shape[0]
    e.matmul(bank_ap, ones1[:, 0:m], row_ap, start=True, stop=True)


def emit_stats(p, x_dram, Tn, S, dst_ca, dst_cb, dst_cc, dst_cd):
    sem = S["sem"]
    xt, onesf, sqt, pb = S["xt"], S["onesf"], S["sqt"], S["pb"]
    t8a, t8b = S["t8a"], S["t8b"]
    epsc = S["epsc"]
    ntile = Tn // 512
    w0 = p.get("wl"); t0 = p.get("ts"); v0 = p.get("vs"); s0 = p.get("ss")

    def gp(e):
        for tt in range(ntile):
            if tt > 0:
                e.wait_ge(sem["ts"], t0 + 2 * tt)
            e.dma_start(out=xt[:].rearrange("p (c n) -> p c n", c=8),
                        in_=x_dram.rearrange("(c p) (u n) -> u p c n", c=8, u=ntile)[tt]
                        ).then_inc(sem["wl"], 16)
    p.on("gpsimd", gp)
    p.add("wl", 16 * ntile)

    def tens(e):
        for tt in range(ntile):
            e.wait_ge(sem["wl"], w0 + 16 * (tt + 1))
            for ch in range(8):
                e.matmul(pb[0][:, :], onesf[:, :], xt[:, ch * 512:(ch + 1) * 512],
                         start=(ch == 0), stop=(ch == 7))
            e.wait_ge(sem["ss"], s0 + 3 * tt + 1)
            for ch in range(8):
                e.matmul(pb[1][:, :], onesf[:, :], sqt[:, ch * 512:(ch + 1) * 512],
                         start=(ch == 0), stop=(ch == 7))
            e.drain().then_inc(sem["ts"], 1)
            e.wait_ge(sem["vs"], v0 + 3 * (tt + 1))
            e.drain().then_inc(sem["ts"], 1)
    p.on("tensor", tens)
    p.add("ts", 2 * ntile)

    def scal(e):
        for tt in range(ntile):
            e.wait_ge(sem["wl"], w0 + 16 * (tt + 1))
            e.activation(sqt[:, 0:2048], xt[:, 0:2048], AF.Square)
            e.activation(sqt[:, 2048:4096], xt[:, 2048:4096], AF.Square)
            e.drain().then_inc(sem["ss"], 1)
            e.wait_ge(sem["vs"], v0 + 3 * tt + 1)
            e.activation(t8b[:, :], t8a[:, :], AF.Sqrt, bias=epsc[:, 0:1])
            e.drain().then_inc(sem["ss"], 1)
            e.wait_ge(sem["vs"], v0 + 3 * tt + 2)
            e.activation(t8b[:, :], sqt[:, 1536:2048], AF.Sqrt)
            e.drain().then_inc(sem["ss"], 1)
    p.on("scalar", scal)
    p.add("ss", 3 * ntile)

    def vec(e):
        for tt in range(ntile):
            sl = slice(tt * 512, (tt + 1) * 512)
            e.wait_ge(sem["ts"], t0 + 2 * tt + 1)
            e.tensor_scalar(sqt[:, 512:1024], pb[0][:, :], float(1.0 / D), None, ALU.mult)
            e.tensor_scalar(t8a[:, :], pb[1][:, :], float(1.0 / D), None, ALU.mult)
            e.tensor_tensor(sqt[:, 0:512], sqt[:, 512:1024], sqt[:, 512:1024], ALU.mult)
            e.tensor_tensor(t8a[:, :], t8a[:, :], sqt[:, 0:512], ALU.subtract)
            e.drain().then_inc(sem["vs"], 1)
            e.wait_ge(sem["ss"], s0 + 3 * tt + 2)
            e.reciprocal(sqt[:, 1024:1536], t8b[:, :])
            if dst_cc is not None:
                e.tensor_copy(dst_cc[:, sl], sqt[:, 1024:1536])
            if dst_cd is not None:
                e.tensor_tensor(dst_cd[:, sl], sqt[:, 512:1024], sqt[:, 1024:1536], ALU.mult)
            e.tensor_tensor(sqt[:, 0:512], sqt[:, 1024:1536], sqt[:, 1024:1536], ALU.mult)
            e.tensor_tensor(sqt[:, 0:512], sqt[:, 0:512], t8a[:, :], ALU.mult)
            e.tensor_scalar(sqt[:, 0:512], sqt[:, 0:512], EPS, None, ALU.add)
            e.reciprocal(sqt[:, 1536:2048], sqt[:, 0:512])
            e.drain().then_inc(sem["vs"], 1)
            e.wait_ge(sem["ss"], s0 + 3 * tt + 3)
            if dst_ca is not None:
                e.tensor_tensor(dst_ca[:, sl], sqt[:, 1024:1536], t8b[:, :], ALU.mult)
                e.tensor_tensor(dst_cb[:, sl], sqt[:, 512:1024], dst_ca[:, sl], ALU.mult)
            e.drain().then_inc(sem["vs"], 1)
    p.on("vector", vec)
    p.add("vs", 3 * ntile)
    p.barrier()



def build_A(T, debug=0, stage=4):
    TH = T // 2
    NTH = TH // 512
    NSTRIP = T // STRIP
    p = Prog()

    x_fm = p.dram_in("x_fm", (D, T))
    x_own = p.dram_in("x_own", (D, TH))
    wqkv = p.dram_in("wqkv", (D, 2048))
    bqkv = p.dram_in("bqkv", (64, 32))
    ln_g = p.dram_in("ln_g", (128, 8))
    ln_b = p.dram_in("ln_b", (128, 8))
    rwT = p.dram_in("rwT", (D, NE))
    rbias = p.dram_in("rbias", (128, NE))
    gu_w = p.dram_in("gu_w", (34, D, 2 * EI))
    dn_w = p.dram_in("dn_w", (34, EI, D))
    msku_in = p.dram_in("msku_in", (CC, 1024), BF16)
    mskg_in = p.dram_in("mskg_in", (CC, 1024), BF16)
    eye64 = p.dram_in("eye64", (64, 64))

    O_out = p.dram_out("O", (T, FC), BF16)
    ffn_out = p.dram_out("ffn", (8, 128, TH))
    stO_out = p.dram_out("stO", (2, TH))
    dbg = {}
    if debug:
        dbg["W"] = p.dram_out("dbgW", (34, TH))
        dbg["xn"] = p.dram_out("dbgxn", (D, TH), BF16)

    ones1 = p.sb("ones1", (1, 128))
    onesc = p.sb("onesc", (128, 1))
    onesf = p.sb("onesf", (128, 128))
    xt = p.sb("xt", (128, 8 * 512))
    sqt = p.sb("sqt", (128, 8 * 512))
    t8a = p.sb("t8a", (128, 512))
    t8b = p.sb("t8b", (128, 512))
    epsc = p.sb("epsc", (128, 1))
    CAt = p.sb("CAt", (128, T), BF16)
    CBt = p.sb("CBt", (128, T), BF16)
    ovl1 = p.sb("ovl1", (128, max(2 * TH, 4096)))
    CCt = ovl1[:, 0:TH]
    CDt = ovl1[:, TH:2 * TH]
    stoc = p.sb("stoc", (128, (TH // 128) * 2))
    xnbf = p.sb("xnbf", (128, 8 * TH), BF16)
    wmT = p.sb("wmT", (34, TH))
    rsumb = p.sb("rsumb", (128, NE))
    rbb = p.sb("rbb", (128, NE))
    rwTs = p.sb("rwTs", (128, 8 * NE))
    lngs = p.sb("lngs", (128, 8))
    lnbs = p.sb("lnbs", (128, 8))
    bqs = p.sb("bqs", (64, 32))
    gub = [p.sb(f"gub{i}", (128, 8 * 1024), BF16) for i in range(2)]
    dnb = p.sb("dnb", (128, 4 * 1024), BF16)
    wrow = p.sb("wrow", (128, 512))
    hbuf = p.sb("hbuf", (128, 4 * 512), BF16)
    ovl2 = p.sb("ovl2", (128, 4096))
    eacc = ovl2[:, 0:4096]
    tokw = p.sb("tokw", (128, NE))
    tok2 = p.sb("tok2", (128, NE))
    tokg = p.sb("tokg", (128, 8))
    tcol = p.sb("tcol", (128, 4))
    wtok = p.sb("wtok", (128, NE))
    xnl = p.sb("xnl", (128, 8 * STRIP), BF16)
    qkvb = ovl1[0:64, 0:4096]
    ktm2 = ovl2[0:CC, 0:1024]
    vtm2 = ovl2[0:CC, 1024:2048]
    btm2 = ovl2[0:CC, 2048:3072]
    ktmS = p.sb("ktmS", (128, 512))
    vtmS = p.sb("vtmS", (128, 512))
    btmS = p.sb("btmS", (128, 512))
    Asp = p.sb("Asp", (CC, 1024))
    Gm = p.sb("Gm", (CC, 1024))
    msku = p.sb("msku", (CC, 1024), BF16)
    mskg = p.sb("mskg", (CC, 1024), BF16)
    eyes = p.sb("eyes", (64, 64))
    Umat = ovl2[0:CC, 3072:3584]
    Wp = ovl2[0:CC, 3584:4096]
    ST = p.sb("ST", (64, FC))
    Ost = p.sb("Ost", (CC, 2 * 512), BF16)

    pb = [p.ps(f"pb{i}", (128, 512)) for i in range(8)]

    wm_d = p.nc.dram_tensor("wm_d", [34, TH], F32, kind="Internal").ap()
    sem = {k: p.sem(k) for k in ("ld", "wl", "ts", "vs", "ss", "st", "fr", "ac", "wm", "wb",
                                 "tg0", "tg1", "vg0", "vg1")}
    S = dict(xt=xt, onesf=onesf, sqt=sqt, t8a=t8a, t8b=t8b, pb=pb, sem=sem, epsc=epsc)

    def ph0_sync(e):
        e.dma_start(out=rwTs[:].rearrange("p (c n) -> p c n", c=8),
                    in_=rwT.rearrange("(c p) n -> p c n", c=8)).then_inc(sem["ld"], 16)
        e.dma_start(out=lngs[:], in_=ln_g).then_inc(sem["ld"], 16)
        e.dma_start(out=lnbs[:], in_=ln_b).then_inc(sem["ld"], 16)
        e.dma_start(out=bqs[:], in_=bqkv).then_inc(sem["ld"], 16)
        e.dma_start(out=msku[:], in_=msku_in).then_inc(sem["ld"], 16)
        e.dma_start(out=mskg[:], in_=mskg_in).then_inc(sem["ld"], 16)
        e.dma_start(out=eyes[:], in_=eye64).then_inc(sem["ld"], 16)
        e.dma_start(out=rbb[:], in_=rbias).then_inc(sem["ld"], 16)
    p.on("sync", ph0_sync)
    p.add("ld", 8 * 16)

    def ph0_vec(e):
        e.wait_ge(sem["ld"], 8 * 16)
        e.memset(ones1[:], 1.0)
        e.memset(onesc[:], 1.0)
        e.memset(onesf[:], 1.0)
        e.memset(epsc[:], EPS)
        e.memset(ST[:], 0.0)
        e.memset(wmT[32:34, :], 1.0)
        e.drain().then_inc(sem["vs"], 1)
    p.on("vector", ph0_vec)
    p.add("vs", 1)
    p.barrier()

    if stage < 1:
        nst = p.get("st"); nld = p.get("ld")
        def finz(e):
            e.wait_ge(sem["ld"], nld)
            e.wait_ge(sem["st"], nst)
        p.on("sync", finz)
        return p, dbg

    emit_stats(p, x_fm, T, S, CAt, CBt, None, None)
    emit_stats(p, x_own, TH, S, None, None, CCt, CDt)

    def sto_sync(e):
        e.dma_start(out=stO_out[0:1, :], in_=CCt[0:1, :]).then_inc(sem["st"], 16)
        e.dma_start(out=stO_out[1:2, :], in_=CDt[0:1, :]).then_inc(sem["st"], 16)
    p.on("sync", sto_sync)
    p.add("st", 32)

    if stage < 2:
        nst = p.get("st"); nwl = p.get("wl")
        def fin0(e):
            e.wait_ge(sem["st"], nst)
            e.wait_ge(sem["wl"], nwl)
        p.on("sync", fin0)
        return p, dbg

    tc0 = p.get("ts"); vc0 = p.get("vs")
    NB = TH // 128
    n_tgrp = (NB + 3) // 4

    def phc_tens(e):
        for ch in range(8):
            e.matmul(pb[0][:, 0:NE], onesf[:, :], rwTs[:, ch * NE:(ch + 1) * NE],
                     start=(ch == 0), stop=(ch == 7))
        e.drain().then_inc(sem["ts"], 1)
        for b in range(NB):
            e.transpose(pb[3][0:128, (b % 4) * 2:(b % 4) * 2 + 1],
                        CCt[0:1, b * 128:(b + 1) * 128], eyes[0:1, 0:1])
            e.transpose(pb[3][0:128, (b % 4) * 2 + 1:(b % 4) * 2 + 2],
                        CDt[0:1, b * 128:(b + 1) * 128], eyes[0:1, 0:1])
            if b % 4 == 3 or b == NB - 1:
                e.drain().then_inc(sem["ts"], 1)
                e.wait_ge(sem["vs"], vc0 + 2 + b // 4)
    p.on("tensor", phc_tens)
    p.add("ts", 1 + n_tgrp)

    def phc_vec(e):
        e.wait_ge(sem["ts"], tc0 + 1)
        e.tensor_copy(rsumb[:, :], pb[0][:, 0:NE])
        e.drain().then_inc(sem["vs"], 2)
        for g in range(n_tgrp):
            e.wait_ge(sem["ts"], tc0 + 2 + g)
            nb = min(4, NB - g * 4)
            e.tensor_copy(stoc[:, g * 8: g * 8 + nb * 2], pb[3][0:128, 0:nb * 2])
            e.drain().then_inc(sem["vs"], 1)
        e.tensor_scalar(stoc[:].rearrange("p (b k) -> p b k", k=2)[:, :, 1:2],
                        stoc[:].rearrange("p (b k) -> p b k", k=2)[:, :, 1:2],
                        -1.0, None, ALU.mult)
        e.drain().then_inc(sem["vs"], 1)
    p.on("vector", phc_vec)
    p.add("vs", 2 + n_tgrp + 1)
    p.barrier()

    wx0 = p.get("wl"); tx0 = p.get("ts"); vx0 = p.get("vs"); sx0 = p.get("ss")
    RV1 = 3
    RV2 = 30
    VT = 16 + 4 * (RV1 + RV2)

    def phx_gp(e):
        for tt in range(NTH):
            if tt > 0:
                e.wait_ge(sem["ts"], tx0 + 2 * tt)
            e.dma_start(out=xt[:].rearrange("p (c n) -> p c n", c=8),
                        in_=x_own.rearrange("(c p) (u n) -> u p c n", c=8, u=NTH)[tt]
                        ).then_inc(sem["wl"], 16)
    p.on("gpsimd", phx_gp)
    p.add("wl", 16 * NTH)

    def phx_tens(e):
        for tt in range(NTH):
            e.wait_ge(sem["wl"], wx0 + 16 * (tt + 1))
            for it in range(4):
                for ch in range(8):
                    e.matmul(pb[2][:, it * NE:(it + 1) * NE],
                             xt[:, ch * 512 + it * 128: ch * 512 + it * 128 + 128],
                             rwTs[:, ch * NE:(ch + 1) * NE],
                             start=(ch == 0), stop=(ch == 7))
            e.drain().then_inc(sem["ts"], 1)
            e.wait_ge(sem["vs"], vx0 + (tt + 1) * VT)
            e.drain().then_inc(sem["ts"], 1)
    p.on("tensor", phx_tens)
    p.add("ts", 2 * NTH)

    def phx_scal(e):
        for tt in range(NTH):
            for it in range(4):
                e.wait_ge(sem["vs"], vx0 + tt * VT + 16 + it * (RV1 + RV2) + RV1)
                e.activation(tok2[:, :], tokw[:, :], AF.Sigmoid).then_inc(sem["ss"], 1)
    p.on("scalar", phx_scal)
    p.add("ss", 4 * NTH)

    def phx_vec(e):
        for tt in range(NTH):
            e.wait_ge(sem["ts"], tx0 + 2 * tt + 1)
            slt = slice(tt * 512, (tt + 1) * 512)
            for ch in range(8):
                e.tensor_tensor(sqt[:, ch * 512:(ch + 1) * 512], xt[:, ch * 512:(ch + 1) * 512],
                                CCt[:, slt], ALU.mult)
                e.tensor_tensor(xnbf[:, ch * TH + tt * 512: ch * TH + (tt + 1) * 512],
                                sqt[:, ch * 512:(ch + 1) * 512], CDt[:, slt], ALU.subtract)
                e.drain().then_inc(sem["vs"], 2)
            def fz():
                e.drain(fusable=False)
            for it in range(4):
                tok0 = tt * 512 + it * 128
                blk = tok0 // 128
                ccol = stoc[:, blk * 2: blk * 2 + 1]
                cdn = stoc[:, blk * 2 + 1: blk * 2 + 2]
                e.tensor_scalar(tok2[:, :], pb[2][:, it * NE:(it + 1) * NE], ccol, None, ALU.mult)
                e.drain().then_inc(sem["vs"], 1)
                e.scalar_tensor_tensor(tokw[:, :], rsumb[:, :], cdn, tok2[:, :], ALU.mult, ALU.add)
                e.drain().then_inc(sem["vs"], 2)
                e.wait_ge(sem["ss"], sx0 + tt * 4 + it + 1)
                e.tensor_tensor(tokw[:, :], tok2[:, :], rbb[:, :], ALU.add)
                fz()
                e.tensor_reduce(tokg[:, 0:4], tokw[:, :].rearrange("p (g k) -> p g k", g=4),
                                AX.X, ALU.max)
                fz()
                for g in range(4):
                    e.tensor_scalar(sqt[:, g * 8:(g + 1) * 8], tokw[:, g * 8:(g + 1) * 8],
                                    tokg[:, g:g + 1], None, ALU.subtract)
                fz()
                e.tensor_scalar(sqt[:, 0:NE], sqt[:, 0:NE], 0.0, None, ALU.is_ge)
                fz()
                e.scalar_tensor_tensor(sqt[:, NE:2 * NE], sqt[:, 0:NE], -BIG, tokw[:, :],
                                       ALU.mult, ALU.add)
                fz()
                e.tensor_reduce(tokg[:, 4:8], sqt[:, NE:2 * NE].rearrange("p (g k) -> p g k", g=4),
                                AX.X, ALU.max)
                fz()
                e.tensor_tensor(tokg[:, 0:4], tokg[:, 0:4], tokg[:, 4:8], ALU.add)
                fz()
                e.tensor_reduce(tcol[:, 0:1], tokg[:, 0:4], AX.X, ALU.max)
                fz()
                e.tensor_scalar(tokg[:, 4:8], tokg[:, 0:4], tcol[:, 0:1], None, ALU.subtract)
                fz()
                e.tensor_scalar(tokg[:, 4:8], tokg[:, 4:8], 0.0, None, ALU.is_ge)
                fz()
                e.scalar_tensor_tensor(tokg[:, 4:8], tokg[:, 4:8], -BIG, tokg[:, 0:4],
                                       ALU.mult, ALU.add)
                fz()
                e.tensor_reduce(tcol[:, 1:2], tokg[:, 4:8], AX.X, ALU.max)
                fz()
                e.tensor_scalar(tokg[:, 0:4], tokg[:, 0:4], tcol[:, 1:2], None, ALU.subtract)
                fz()
                e.tensor_scalar(tokg[:, 0:4], tokg[:, 0:4], 0.0, None, ALU.is_ge)
                fz()
                for g in range(4):
                    e.tensor_scalar(wtok[:, g * 8:(g + 1) * 8], tokw[:, g * 8:(g + 1) * 8],
                                    tokg[:, g:g + 1], None, ALU.mult)
                e.drain().then_inc(sem["vs"], 16)
                e.tensor_copy(sqt[:, 0:NE], wtok[:, :])
                fz()
                for _ in range(3):
                    e.tensor_reduce(tcol[:, 2:3], sqt[:, 0:NE], AX.X, ALU.max)
                    fz()
                    e.tensor_scalar(sqt[:, NE:2 * NE], sqt[:, 0:NE], tcol[:, 2:3], None, ALU.subtract)
                    fz()
                    e.tensor_scalar(sqt[:, NE:2 * NE], sqt[:, NE:2 * NE], 0.0, None, ALU.is_ge)
                    fz()
                    e.scalar_tensor_tensor(sqt[:, 0:NE], sqt[:, NE:2 * NE], -BIG, sqt[:, 0:NE],
                                           ALU.mult, ALU.add)
                    fz()
                e.tensor_reduce(tcol[:, 2:3], sqt[:, 0:NE], AX.X, ALU.max)
                fz()
                e.tensor_scalar(sqt[:, 0:NE], wtok[:, :], tcol[:, 2:3], None, ALU.subtract)
                fz()
                e.tensor_scalar(sqt[:, 0:NE], sqt[:, 0:NE], 0.0, None, ALU.is_ge)
                fz()
                e.tensor_tensor(wtok[:, :], tok2[:, :], sqt[:, 0:NE], ALU.mult)
                fz()
                e.tensor_reduce(tcol[:, 3:4], wtok[:, :], AX.X, ALU.add)
                fz()
                e.tensor_scalar(tcol[:, 3:4], tcol[:, 3:4], 1e-20, None, ALU.add)
                fz()
                e.reciprocal(tcol[:, 3:4], tcol[:, 3:4])
                fz()
                e.tensor_scalar(wtok[:, :], wtok[:, :], tcol[:, 3:4], None, ALU.mult)
                fz()
                e.tensor_scalar(wtok[:, :], wtok[:, :], 2.5, None, ALU.mult)
                fz()
                for b4 in range(4):
                    e.transpose(wmT[0:32, tok0 + b4 * 32: tok0 + (b4 + 1) * 32],
                                wtok[b4 * 32:(b4 + 1) * 32, 0:32])
                e.drain().then_inc(sem["vs"], 14)
    p.on("vector", phx_vec)
    p.add("vs", NTH * VT)
    p.barrier()

    if stage < 3:
        if debug:
            def dbg2(e):
                e.dma_start(out=dbg["W"], in_=wmT[:]).then_inc(sem["st"], 16)
                e.dma_start(out=dbg["xn"].rearrange("(c p) n -> p c n", c=8),
                            in_=xnbf[:].rearrange("p (c n) -> p c n", c=8)).then_inc(sem["st"], 16)
            p.on("sync", dbg2)
            p.add("st", 32)
        if debug >= 2:
            dbg["stoc"] = p.dram_out("dbgstoc", (128, (TH // 128) * 2))
            dbg["rsumb"] = p.dram_out("dbgrsumb", (128, NE))
            dbg["tokw"] = p.dram_out("dbgtokw", (128, NE))
            dbg["tok2"] = p.dram_out("dbgtok2", (128, NE))
            dbg["wtok"] = p.dram_out("dbgwtok", (128, NE))
            dbg["tokg"] = p.dram_out("dbgtokg", (128, 8))
            dbg["tcol"] = p.dram_out("dbgtcol", (128, 4))
            dbg["sqt"] = p.dram_out("dbgsqt", (128, 2 * NE))
            def dbg3(e):
                for k in ("stoc", "rsumb", "tokw", "tok2", "wtok", "tokg", "tcol"):
                    src = {"stoc": stoc, "rsumb": rsumb, "tokw": tokw, "tok2": tok2,
                           "wtok": wtok, "tokg": tokg, "tcol": tcol}[k]
                    e.dma_start(out=dbg[k], in_=src[:]).then_inc(sem["st"], 16)
                e.dma_start(out=dbg["sqt"], in_=sqt[:, 0:2 * NE]).then_inc(sem["st"], 16)
            p.on("sync", dbg3)
            p.add("st", 8 * 16)
        nst = p.get("st"); nwl = p.get("wl")
        def fin1(e):
            e.wait_ge(sem["st"], nst)
            e.wait_ge(sem["wl"], nwl)
        p.on("sync", fin1)
        return p, dbg

    NTT = NTH
    wl0 = p.get("wl"); t0 = p.get("ts"); v0 = p.get("vs"); s0 = p.get("ss")
    ac0 = p.get("ac"); vroute = p.get("vs")

    def tbase(ee):
        return t0 + 2 * NTT * ee

    def sbase(ee):
        return s0 + NTT * ee

    def vbase(ee):
        return v0 + 2 * NTT * ee

    def moe_gp(e):
        e.dma_start(out=wm_d, in_=wmT[:]).then_inc(sem["wm"], 16)
        for ee in range(35):
            if ee >= 1:
                ep = ee - 1
                for tt in range(NTT):
                    e.wait_ge(sem["vs"], vbase(ep) + 2 * tt + 2)
                    e.dma_start(out=ffn_out[:, :, tt * 512:(tt + 1) * 512].rearrange("c p n -> p c n"),
                                in_=eacc[:].rearrange("p (c n) -> p c n", c=8),
                                accum_op=(ALU.bypass if ep == 0 else ALU.add)).then_inc(sem["ac"], 16)
            if ee < 34:
                if ee >= 2:
                    e.wait_ge(sem["fr"], 2 * (ee - 2) + 1)
                e.dma_start(out=gub[ee % 2][:].rearrange("p (c n) -> p c n", c=8),
                            in_=gu_w[ee].rearrange("(c p) n -> p c n", c=8)).then_inc(sem["wl"], 16)
                if ee >= 1:
                    e.wait_ge(sem["fr"], 2 * ee)
                e.dma_start(out=dnb[:].rearrange("p (c n) -> p c n", c=4),
                            in_=dn_w[ee].rearrange("(c p) n -> p c n", c=4)).then_inc(sem["wl"], 16)
    p.on("gpsimd", moe_gp)
    p.add("wl", 34 * 32)
    p.add("ac", 34 * NTT * 16)
    p.add("wm", 16)

    def moe_sync(e):
        e.wait_ge(sem["wm"], 16)
        for ee in range(NE):
            for tt in range(NTT):
                e.wait_ge(sem["vs"], max(0, vbase(ee) + 2 * tt - 1))
                e.dma_start(out=wrow[:, :],
                            in_=wm_d[ee:ee + 1, tt * 512:(tt + 1) * 512].broadcast_to([128, 512])
                            ).then_inc(sem["wb"], 16)
    p.on("sync", moe_sync)
    p.add("wb", NE * NTT * 16)

    def moe_tens(e):
        for ee in range(34):
            e.wait_ge(sem["wl"], wl0 + 32 * (ee + 1))
            tb = tbase(ee)
            for tt in range(NTT):
                e.wait_ge(sem["vs"], vbase(ee) + 2 * tt)
                for ic in range(4):
                    for ch in range(8):
                        e.matmul(pb[ic][:, :],
                                 gub[ee % 2][:, ch * 1024 + ic * 128: ch * 1024 + ic * 128 + 128],
                                 xnbf[:, ch * TH + tt * 512: ch * TH + (tt + 1) * 512],
                                 start=(ch == 0), stop=(ch == 7))
                for ic in range(4):
                    for ch in range(8):
                        mm = e.matmul(pb[4 + ic][:, :],
                                      gub[ee % 2][:, ch * 1024 + 512 + ic * 128: ch * 1024 + 512 + ic * 128 + 128],
                                      xnbf[:, ch * TH + tt * 512: ch * TH + (tt + 1) * 512],
                                      start=(ch == 0), stop=(ch == 7))
                mm.then_inc(sem["ts"], 1)
                if tt == NTT - 1:
                    e.drain().then_inc(sem["fr"], 1)
                e.wait_ge(sem["vs"], vbase(ee) + 2 * tt + 1)
                for dt in range(8):
                    for ic in range(4):
                        mm = e.matmul(pb[dt][:, :],
                                      dnb[:, ic * 1024 + dt * 128: ic * 1024 + dt * 128 + 128],
                                      hbuf[:, ic * 512:(ic + 1) * 512],
                                      start=(ic == 0), stop=(ic == 3))
                mm.then_inc(sem["ts"], 1)
            e.drain().then_inc(sem["fr"], 1)
    p.on("tensor", moe_tens)
    p.add("ts", 34 * NTT * 2)
    p.add("fr", 68)

    def moe_scal(e):
        for ee in range(34):
            tb = tbase(ee)
            for tt in range(NTT):
                e.wait_ge(sem["ts"], tb + 2 * tt + 1)
                for ic in range(4):
                    e.activation(sqt[:, 2048 + ic * 512: 2048 + (ic + 1) * 512], pb[ic][:, :], AF.Silu)
                e.drain().then_inc(sem["ss"], 1)
    p.on("scalar", moe_scal)
    p.add("ss", 34 * NTT)

    def moe_vec(e):
        for ee in range(34):
            tb = tbase(ee)
            for tt in range(NTT):
                e.wait_ge(sem["ts"], tb + 2 * tt + 1)
                e.wait_ge(sem["ss"], sbase(ee) + tt + 1)
                if ee < NE:
                    e.wait_ge(sem["wb"], 16 * (ee * NTT + tt + 1))
                for ic in range(4):
                    if ee < NE:
                        e.tensor_tensor(sqt[:, ic * 512:(ic + 1) * 512], pb[4 + ic][:, :],
                                        wrow[:, :], ALU.mult)
                    else:
                        e.tensor_copy(sqt[:, ic * 512:(ic + 1) * 512], pb[4 + ic][:, :])
                for ic in range(4):
                    e.tensor_tensor(hbuf[:, ic * 512:(ic + 1) * 512],
                                    sqt[:, 2048 + ic * 512: 2048 + (ic + 1) * 512],
                                    sqt[:, ic * 512:(ic + 1) * 512], ALU.mult)
                e.drain().then_inc(sem["vs"], 1)
                e.wait_ge(sem["ts"], tb + 2 * tt + 2)
                e.wait_ge(sem["ac"], ac0 + 16 * (ee * NTT + tt))
                for dt in range(8):
                    e.tensor_copy(eacc[:, dt * 512:(dt + 1) * 512], pb[dt][:, :])
                e.drain().then_inc(sem["vs"], 1)
    p.on("vector", moe_vec)
    p.add("vs", 34 * NTT * 2)

    if debug:
        nvs_moe = p.get("vs")
        def moe_dbg(e):
            e.wait_ge(sem["vs"], nvs_moe)
            e.dma_start(out=dbg["W"], in_=wmT[:]).then_inc(sem["st"], 16)
            e.dma_start(out=dbg["xn"].rearrange("(c p) n -> p c n", c=8),
                        in_=xnbf[:].rearrange("p (c n) -> p c n", c=8)).then_inc(sem["st"], 16)
        p.on("sync", moe_dbg)
        p.add("st", 32)
    p.barrier()

    if stage < 4:
        nst = p.get("st"); nwl = p.get("wl"); nac = p.get("ac")
        def fin2(e):
            e.wait_ge(sem["st"], nst)
            e.wait_ge(sem["wl"], nwl)
            e.wait_ge(sem["ac"], nac)
        p.on("sync", fin2)
        return p, dbg

    emit_deltanet(p, T, sem, pb, dict(
        x_fm=x_fm, CAt=CAt, CBt=CBt, lngs=lngs, lnbs=lnbs, bqs=bqs, gub=gub,
        wqkv=wqkv, eyes=eyes, msku=msku, mskg=mskg, xt=xt, xnl=xnl, qkvb=qkvb,
        sqt=sqt, ktm2=ktm2, vtm2=vtm2, btm2=btm2, ktmS=ktmS, vtmS=vtmS,
        btmS=btmS, Asp=Asp, Gm=Gm, Umat=Umat, Wp=Wp, ST=ST, Ost=Ost,
        onesf=onesf), O_out)

    nst = p.get("st"); nwl = p.get("wl"); nac = p.get("ac")
    def fin_sync(e):
        e.wait_ge(sem["st"], nst)
        e.wait_ge(sem["wl"], nwl)
        e.wait_ge(sem["ac"], nac)
    p.on("sync", fin_sync)
    return p, dbg


def emit_deltanet(p, T, sem, pb, B, O_out):
    NSTRIP = T // STRIP
    x_fm = B["x_fm"]
    CAt, CBt = B["CAt"], B["CBt"]
    lngs, lnbs, bqs = B["lngs"], B["lnbs"], B["bqs"]
    gub, wqkv = B["gub"], B["wqkv"]
    eyes, msku, mskg = B["eyes"], B["msku"], B["mskg"]
    xt, xnl, qkvb, sqt = B["xt"], B["xnl"], B["qkvb"], B["sqt"]
    ktm2, vtm2, btm2 = B["ktm2"], B["vtm2"], B["btm2"]
    ktmS, vtmS, btmS = B["ktmS"], B["vtmS"], B["btmS"]
    Asp, Gm = B["Asp"], B["Gm"]
    Umat, Wp, ST, Ost = B["Umat"], B["Wp"], B["ST"], B["Ost"]
    onesf = B["onesf"]

    NIT = CC - 1
    w0 = p.get("wl"); t0 = p.get("ts"); v0 = p.get("vs"); s0 = p.get("ss")

    def wq_ap(ch, j):
        return gub[ch // 4][:, (ch % 4) * 2048 + j * 64: (ch % 4) * 2048 + (j + 1) * 64]

    NG = 2
    GH = 8 // NG
    GW = GH * 64
    TG = 1 + NIT + 1 + 1
    VG = 1 + NIT + 1
    T_PRE = 4
    V_PRE = 4
    S_PRE = 6
    SPS = S_PRE + 8
    GPS = 16 * (1 + 24 + 8)
    tg = [sem[f"tg{i}"] for i in range(NG)]
    vg = [sem[f"vg{i}"] for i in range(NG)]
    tg0 = [p.get(f"tg{i}") for i in range(NG)]
    vg0 = [p.get(f"vg{i}") for i in range(NG)]

    def tw(s, k):
        return t0 + s * T_PRE + k

    def vw(s, k):
        return v0 + s * V_PRE + k

    def sw(s, k):
        return s0 + s * SPS + k

    def tgw(g, n, k):
        return tg0[g] + n * TG + k

    def vgw(g, n, k):
        return vg0[g] + n * VG + k

    def gp(e):
        e.wait_ge(sem["fr"], 68)
        for half in range(2):
            e.dma_start(out=gub[half][:].rearrange("p (c n) -> p c n", c=4),
                        in_=wqkv.rearrange("(u c p) n -> u p c n", u=2, c=4)[half]
                        ).then_inc(sem["wl"], 16)
        for s in range(NSTRIP):
            if s > 0:
                e.wait_ge(sem["vs"], vw(s - 1, 1))
            e.dma_start(out=xt[:, 0:8 * STRIP].rearrange("p (c n) -> p c n", c=8),
                        in_=x_fm.rearrange("(c p) n -> p c n", c=8)[:, :, s * STRIP:(s + 1) * STRIP]
                        ).then_inc(sem["wl"], 16)
            e.wait_ge(sem["ss"], sw(s, 5))
            for c in range(8):
                if c >= 2:
                    for g in range(NG):
                        e.wait_ge(vg[g], vgw(g, s * 8 + c - 1, 1))
                sl = slice((c % 2) * 512, (c % 2) * 512 + 512)
                e.dma_start(out=ktm2[:, sl], in_=ktmS[c * CC:(c + 1) * CC, :]).then_inc(sem["wl"], 16)
                e.dma_start(out=vtm2[:, sl], in_=vtmS[c * CC:(c + 1) * CC, :]).then_inc(sem["wl"], 16)
                e.dma_start(out=btm2[:, sl], in_=btmS[c * CC:(c + 1) * CC, :]).then_inc(sem["wl"], 16)
                if c >= 1:
                    cp = c - 1
                    e.wait_ge(sem["ss"], sw(s, S_PRE + cp + 1))
                    e.dma_start(out=O_out.rearrange("(u c p) f -> u c p f", u=NSTRIP, c=8)[s, cp],
                                in_=Ost[:, (cp % 2) * 512:(cp % 2) * 512 + 512]).then_inc(sem["wl"], 16)
            e.wait_ge(sem["ss"], sw(s, S_PRE + 8))
            e.dma_start(out=O_out.rearrange("(u c p) f -> u c p f", u=NSTRIP, c=8)[s, 7],
                        in_=Ost[:, 512:1024]).then_inc(sem["wl"], 16)
    p.on("gpsimd", gp)
    p.add("wl", 32 + NSTRIP * GPS)
    nwq = w0 + 32

    def tens(e):
        e.wait_ge(sem["wl"], nwq)
        for s in range(NSTRIP):
            e.wait_ge(sem["ss"], sw(s, 1))
            mm = None
            for j in range(32):
                bank, col = j // 4, (j % 4) * 128
                for ch in range(8):
                    mm = e.matmul(pb[bank][0:64, col:col + 128], wq_ap(ch, j),
                                  xnl[:, ch * STRIP:(ch + 1) * STRIP],
                                  start=(ch == 0), stop=(ch == 7))
            mm.then_inc(sem["ts"], 1)
            e.wait_ge(sem["ss"], sw(s, 3))
            for h in range(8):
                bank = pb[0] if h < 4 else pb[1]
                mm = e.matmul(bank[:, (h % 4) * 128:(h % 4 + 1) * 128],
                              onesf[0:64, :], sqt[0:64, h * 128:(h + 1) * 128],
                              start=True, stop=True)
            mm.then_inc(sem["ts"], 1)
            e.wait_ge(sem["vs"], vw(s, 3))
            for h in range(8):
                e.transpose(pb[4][0:128, h * 64:(h + 1) * 64], qkvb[:, (8 + h) * 128:(9 + h) * 128], eyes[:])
                e.transpose(pb[5][0:128, h * 64:(h + 1) * 64], qkvb[:, (16 + h) * 128:(17 + h) * 128], eyes[:])
                mm = e.transpose(pb[6][0:128, h * 64:(h + 1) * 64], qkvb[:, (24 + h) * 128:(25 + h) * 128], eyes[:])
            mm.then_inc(sem["ts"], 1)
            e.wait_ge(sem["ss"], sw(s, 5))
            for c in range(8):
                bank = pb[4] if c < 4 else pb[5]
                colb = (c % 4) * 128
                for h in range(8):
                    ksl = qkvb[:, (8 + h) * 128 + c * CC: (8 + h) * 128 + (c + 1) * CC]
                    e.matmul(bank[0:CC, colb + h * CC: colb + (h + 1) * CC], ksl, ksl,
                             start=True, stop=True)
            for c in range(8):
                bank = pb[6] if c < 4 else pb[7]
                colb = (c % 4) * 128
                for h in range(8):
                    ksl = qkvb[:, (8 + h) * 128 + c * CC: (8 + h) * 128 + (c + 1) * CC]
                    qsl = qkvb[:, h * 128 + c * CC: h * 128 + (c + 1) * CC]
                    mm = e.matmul(bank[0:CC, colb + h * CC: colb + (h + 1) * CC], ksl, qsl,
                                  start=True, stop=True)
            mm.then_inc(sem["ts"], 1)
            for c in range(8):
                n = s * 8 + c
                for g in range(NG):
                    e.wait_ge(vg[g], vgw(g, n, 0))
                    if c == 0:
                        e.wait_ge(sem["vs"], vw(s, 4))
                    e.wait_ge(sem["ss"], sw(s, S_PRE + c))
                    for h in range(g * GH, (g + 1) * GH):
                        mm = e.matmul(pb[3 * g][0:CC, h * 64:(h + 1) * 64],
                                      qkvb[:, (8 + h) * 128 + c * CC: (8 + h) * 128 + (c + 1) * CC],
                                      ST[:, h * 64:(h + 1) * 64], start=True, stop=True)
                    mm.then_inc(tg[g], 1)
                for it in range(NIT):
                    for g in range(NG):
                        bank = pb[3 * g + 1 + (it % 2)]
                        e.wait_ge(vg[g], vgw(g, n, 1 + it))
                        for h in range(g * GH, (g + 1) * GH):
                            mm = e.matmul(bank[0:CC, h * 64:(h + 1) * 64],
                                          Asp[:, (c * 8 + h) * CC: (c * 8 + h + 1) * CC],
                                          Umat[:, h * 64:(h + 1) * 64], start=True, stop=True)
                        mm.then_inc(tg[g], 1)
                for g in range(NG):
                    e.wait_ge(vg[g], vgw(g, n, 1 + NIT))
                    for h in range(g * GH, (g + 1) * GH):
                        e.matmul(pb[3 * g][0:CC, h * 64:(h + 1) * 64],
                                 qkvb[:, h * 128 + c * CC: h * 128 + (c + 1) * CC],
                                 ST[:, h * 64:(h + 1) * 64], start=True, stop=False)
                        mm = e.matmul(pb[3 * g][0:CC, h * 64:(h + 1) * 64],
                                      Gm[:, (c * 8 + h) * CC: (c * 8 + h + 1) * CC],
                                      Umat[:, h * 64:(h + 1) * 64], start=False, stop=True)
                    mm.then_inc(tg[g], 1)
                for g in range(NG):
                    for h in range(g * GH, (g + 1) * GH):
                        mm = e.matmul(pb[3 * g + 2][0:64, h * 64:(h + 1) * 64],
                                      ktm2[:, (c % 2) * 512 + h * 64: (c % 2) * 512 + (h + 1) * 64],
                                      Umat[:, h * 64:(h + 1) * 64], start=True, stop=True)
                    mm.then_inc(tg[g], 1)
    p.on("tensor", tens)
    p.add("ts", NSTRIP * T_PRE)
    for i in range(NG):
        p.add(f"tg{i}", NSTRIP * 8 * TG)

    def scal(e):
        for s in range(NSTRIP):
            e.wait_ge(sem["vs"], vw(s, 1))
            for ch in range(8):
                e.activation(xnl[:, ch * STRIP:(ch + 1) * STRIP],
                             sqt[:, 1024 + ch * STRIP: 1024 + (ch + 1) * STRIP],
                             AF.Identity, bias=lnbs[:, ch:ch + 1], scale=lngs[:, ch:ch + 1])
            e.drain().then_inc(sem["ss"], 1)
            e.wait_ge(sem["ts"], tw(s, 1))
            for j in range(32):
                bank, col = j // 4, (j % 4) * 128
                func = AF.Sigmoid if j >= 24 else AF.Identity
                e.activation(qkvb[:, j * 128:(j + 1) * 128], pb[bank][0:64, col:col + 128],
                             func, bias=bqs[0:64, j:j + 1])
            e.drain().then_inc(sem["ss"], 1)
            e.activation(sqt[0:64, 0:512], qkvb[0:64, 8 * 128:12 * 128], AF.Square)
            e.activation(sqt[0:64, 512:1024], qkvb[0:64, 12 * 128:16 * 128], AF.Square)
            e.drain().then_inc(sem["ss"], 1)
            e.wait_ge(sem["ts"], tw(s, 2))
            e.activation(sqt[:, 2048:2560], pb[0][:, :], AF.Sqrt)
            e.activation(sqt[:, 2560:3072], pb[1][:, :], AF.Sqrt)
            e.drain().then_inc(sem["ss"], 1)
            e.wait_ge(sem["ts"], tw(s, 3))
            e.copy(ktmS[:, :], pb[4][0:128, 0:512])
            e.copy(vtmS[:, :], pb[5][0:128, 0:512])
            e.copy(btmS[:, :], pb[6][0:128, 0:512])
            e.drain().then_inc(sem["ss"], 1)
            e.wait_ge(sem["ts"], tw(s, 4))
            e.copy(Asp[:, 0:512], pb[4][0:CC, 0:512])
            e.copy(Asp[:, 512:1024], pb[5][0:CC, 0:512])
            e.copy(Gm[:, 0:512], pb[6][0:CC, 0:512])
            e.copy(Gm[:, 512:1024], pb[7][0:CC, 0:512])
            e.drain().then_inc(sem["ss"], 1)
            for c in range(8):
                for g in range(NG):
                    e.wait_ge(tg[g], tgw(g, s * 8 + c, 1 + NIT + 1))
                if s * 8 + c >= 2:
                    prev = s * 8 + c - 2
                    ps_, pc_ = prev // 8, prev % 8
                    e.wait_ge(sem["wl"], w0 + 32 + ps_ * GPS + 16 * (4 * pc_ + 9))
                off = (c % 2) * 512
                for g in range(NG):
                    cp = e.copy(Ost[:, off + g * GW:off + (g + 1) * GW],
                                pb[3 * g][0:CC, g * GW:(g + 1) * GW])
                cp.then_inc(sem["ss"], 1)
    p.on("scalar", scal)
    p.add("ss", NSTRIP * SPS)

    def vec(e):
        for s in range(NSTRIP):
            e.wait_ge(sem["wl"], w0 + 32 + s * GPS + 16)
            ssl = slice(s * STRIP, (s + 1) * STRIP)
            for ch in range(8):
                e.tensor_tensor(sqt[:, ch * STRIP:(ch + 1) * STRIP],
                                xt[:, ch * STRIP:(ch + 1) * STRIP], CAt[:, ssl], ALU.mult)
                e.tensor_tensor(sqt[:, 1024 + ch * STRIP: 1024 + (ch + 1) * STRIP],
                                sqt[:, ch * STRIP:(ch + 1) * STRIP], CBt[:, ssl], ALU.subtract)
            e.drain().then_inc(sem["vs"], 1)
            e.wait_ge(sem["ss"], sw(s, 4))
            e.tensor_scalar(sqt[:, 2048:3072], sqt[:, 2048:3072], 1e-6, None, ALU.max)
            e.reciprocal(sqt[:, 2048:3072], sqt[:, 2048:3072])
            e.drain().then_inc(sem["vs"], 1)
            e.tensor_tensor(qkvb[:, 8 * 128:16 * 128], qkvb[:, 8 * 128:16 * 128],
                            sqt[0:64, 2048:3072], ALU.mult)
            e.drain().then_inc(sem["vs"], 1)
            e.wait_ge(sem["ss"], sw(s, 5 + 1))
            e.tensor_tensor(Asp[:, :], Asp[:, :], msku[:, :], ALU.mult)
            e.tensor_tensor(Gm[:, :], Gm[:, :], mskg[:, :], ALU.mult)
            e.drain().then_inc(sem["vs"], 1)
            for c in range(8):
                n = s * 8 + c
                e.wait_ge(sem["wl"], w0 + 32 + s * GPS + 16 * (4 * c + 4))
                sl0 = (c % 2) * 512
                for g in range(NG):
                    gs = slice(g * GW, (g + 1) * GW)
                    gsl = slice(sl0 + g * GW, sl0 + (g + 1) * GW)
                    e.wait_ge(tg[g], tgw(g, n, 1))
                    e.tensor_tensor(Wp[:, gs], vtm2[:, gsl], pb[3 * g][0:CC, gs], ALU.subtract)
                    e.tensor_tensor(Wp[:, gs], btm2[:, gsl], Wp[:, gs], ALU.mult)
                    e.tensor_copy(Umat[:, gs], Wp[:, gs]).then_inc(vg[g], 1)
                for it in range(NIT):
                    for g in range(NG):
                        bank = pb[3 * g + 1 + (it % 2)]
                        gs = slice(g * GW, (g + 1) * GW)
                        gsl = slice(sl0 + g * GW, sl0 + (g + 1) * GW)
                        e.wait_ge(tg[g], tgw(g, n, 1 + it + 1))
                        e.tensor_tensor(sqt[0:CC, 1536 + gs.start:1536 + gs.stop],
                                        btm2[:, gsl], bank[0:CC, gs], ALU.mult)
                        e.tensor_tensor(Umat[:, gs], Wp[:, gs],
                                        sqt[0:CC, 1536 + gs.start:1536 + gs.stop],
                                        ALU.subtract).then_inc(vg[g], 1)
                for g in range(NG):
                    gs = slice(g * GW, (g + 1) * GW)
                    e.wait_ge(tg[g], tgw(g, n, TG))
                    e.tensor_tensor(ST[:, gs], ST[:, gs], pb[3 * g + 2][0:64, gs], ALU.add).then_inc(vg[g], 1)
    p.on("vector", vec)
    p.add("vs", NSTRIP * V_PRE)
    for i in range(NG):
        p.add(f"vg{i}", NSTRIP * 8 * VG)



H1 = 4096
NH1C = H1 // 128


def build_B(T, debug=0):
    NT = T // 512
    p = Prog()

    x_fm = p.dram_in("x_fm", (D, T))
    OT_in = p.dram_in("OT_in", (D, T), BF16)
    ffn0_in = p.dram_in("ffn0_in", (D, T))
    stB = p.dram_in("stB", (2, T))
    wo0T = p.dram_in("wo0T", (D, D))
    b_out0c = p.dram_in("b_out0c", (128, 8))
    WinT = p.dram_in("WinT", (D, D))
    WigT = p.dram_in("WigT", (D, D))
    WrgT = p.dram_in("WrgT", (D, D))
    WoutT = p.dram_in("WoutT", (D, D))
    bigc = p.dram_in("bigc", (128, 8))
    brgc = p.dram_in("brgc", (128, 8))
    lamc = p.dram_in("lamc", (128, 8))
    h0c = p.dram_in("h0c", (128, 8))
    nwc = p.dram_in("nwc", (128, 8))
    g1c = p.dram_in("g1c", (128, 8))
    b1c = p.dram_in("b1c", (128, 8))
    bb1c = p.dram_in("bb1c", (128, 32))
    g2c = p.dram_in("g2c", (128, 32))
    b2c = p.dram_in("b2c", (128, 32))
    bb2c = p.dram_in("bb2c", (128, 8))
    w1T = p.dram_in("w1T", (D, H1))
    w2T = p.dram_in("w2T", (H1, D))
    eye128 = p.dram_in("eye128", (128, 128))

    xout = p.dram_out("xout", (D, T))

    x1d = p.nc.dram_tensor("x1d", [D, T], F32, kind="Internal").ap()
    xn1d = p.nc.dram_tensor("xn1d", [D, T], BF16, kind="Internal").ap()
    qw1d = p.nc.dram_tensor("qw1d", [D, H1], BF16, kind="Internal").ap()
    qw2d = p.nc.dram_tensor("qw2d", [H1, D], BF16, kind="Internal").ap()
    rowd = p.nc.dram_tensor("rowd", [4, 512], F32, kind="Internal").ap()

    ones1 = p.sb("ones1", (1, 128))
    onesf = p.sb("onesf", (128, 128))
    onesb = p.sb("onesb", (128, 128), BF16)
    epsc = p.sb("epsc", (128, 1))
    eyes = p.sb("eyes", (128, 128))
    f32a = p.sb("f32a", (128, 16384))
    bf16a = p.sb("bf16a", (128, 32768), BF16)
    CAt = p.sb("CAt", (128, T), BF16)
    CBt = p.sb("CBt", (128, T), BF16)
    xnt = p.sb("xnt", (128, 4096), BF16)
    qt = p.sb("qt", (128, 4096), BF16)
    wdb = p.sb("wdb", (128, 8192), BF16)
    t8a = p.sb("t8a", (128, 512))
    t8b = p.sb("t8b", (128, 512))
    t8c = p.sb("t8c", (128, 512))
    t8d = p.sb("t8d", (128, 512))
    murb = p.sb("murb", (128, 512))
    rsrb = p.sb("rsrb", (128, 512))
    amr = p.sb("amr", (128, 512))
    am2r = p.sb("am2r", (128, 512))
    stbsA = p.sb("stbsA", (1, 512))
    stbsB = p.sb("stbsB", (1, 512))
    wcol = p.sb("wcol", (128, 32))
    swc = p.sb("swc", (128, 8))
    logac = p.sb("logac", (128, 8))
    pb = [p.ps(f"pb{i}", (128, 512)) for i in range(8)]
    sem = {k: p.sem(k) for k in ("ld", "wl", "ts", "vs", "ss", "st")}

    fA = lambda i: f32a[:, i * 4096:(i + 1) * 4096]

    def fz(e):
        e.drain(fusable=False)

    def h0c_col(c):
        return CB_h0[:, c:c + 1]

    def ph0_sync(e):
        for dst, src in ((b_out0c, None),):
            pass
        for nm, dr, sb_ in (("b_out0c", b_out0c, None),):
            pass
        for dr, w in ((b_out0c, 8), (bigc, 8), (brgc, 8), (lamc, 8), (h0c, 8),
                      (nwc, 8), (g1c, 8), (b1c, 8), (bb1c, 32), (g2c, 32),
                      (b2c, 32), (bb2c, 8)):
            pass
    colsB = p.sb("colsB", (128, 160))
    CB_b_out0 = colsB[:, 0:8]
    CB_big = colsB[:, 8:16]
    CB_brg = colsB[:, 16:24]
    CB_h0 = colsB[:, 24:32]
    CB_nw = colsB[:, 32:40]
    CB_g1 = colsB[:, 40:48]
    CB_b1 = colsB[:, 48:56]
    CB_bb1 = colsB[:, 56:88]
    CB_g2 = colsB[:, 88:120]
    CB_b2 = colsB[:, 120:152]
    CB_bb2 = colsB[:, 152:160]

    def ph0_sync2(e):
        for dst, src in ((CB_b_out0, b_out0c), (CB_big, bigc), (CB_brg, brgc),
                         (CB_h0, h0c), (CB_nw, nwc), (CB_g1, g1c), (CB_b1, b1c),
                         (CB_bb1, bb1c), (CB_g2, g2c), (CB_b2, b2c), (CB_bb2, bb2c)):
            e.dma_start(out=dst, in_=src).then_inc(sem["ld"], 16)
        e.dma_start(out=eyes[:], in_=eye128).then_inc(sem["ld"], 16)
        e.dma_start(out=swc[:, 0:8], in_=lamc).then_inc(sem["ld"], 16)
    p.on("sync", ph0_sync2)
    p.add("ld", 13 * 16)

    def ph0_gp(e):
        e.dma_start(out=wdb[:].rearrange("p (k n) -> p k n", k=8),
                    in_=wo0T.rearrange("(k p) n -> p k n", k=8)).then_inc(sem["wl"], 16)
    p.on("gpsimd", ph0_gp)
    p.add("wl", 16)

    def ph0_vec(e):
        e.memset(ones1[:], 1.0)
        e.memset(onesf[:], 1.0)
        e.memset(onesb[:], 1.0)
        e.memset(epsc[:], EPS)
        e.drain().then_inc(sem["vs"], 1)
        e.wait_ge(sem["ss"], 1)
        e.tensor_scalar(logac[:, :], logac[:, :], 1e-8, None, ALU.add)
        e.drain().then_inc(sem["vs"], 1)
    p.on("vector", ph0_vec)
    p.add("vs", 2)

    def ph0_scal(e):
        e.wait_ge(sem["ld"], 13 * 16)
        e.activation(logac[:, :], swc[:, 0:8], AF.Sigmoid)
        e.drain().then_inc(sem["ss"], 1)
        e.wait_ge(sem["vs"], 2)
        e.activation(logac[:, :], logac[:, :], AF.Ln)
        e.drain().then_inc(sem["ss"], 1)
    p.on("scalar", ph0_scal)
    p.add("ss", 2)

    def ph0_fin(e):
        e.wait_ge(sem["wl"], 16)
        e.wait_ge(sem["ss"], 2)
    p.on("sync", ph0_fin)
    p.barrier()

    xr = x_fm.rearrange("(c p) (u n) -> u p c n", c=8, u=NT)
    fr = ffn0_in.rearrange("(c p) (u n) -> u p c n", c=8, u=NT)
    Or = OT_in.rearrange("(c p) (u n) -> u p c n", c=8, u=NT)
    x1r = x1d.rearrange("(c p) (u n) -> u c p n", c=8, u=NT)
    xn1r = xn1d.rearrange("(c p) (u n) -> u c p n", c=8, u=NT)

    for tt in range(NT):
        l0 = p.get("ld"); t0 = p.get("ts"); v0 = p.get("vs"); s0 = p.get("ss")

        def p1_sync(e, tt=tt, l0=l0, v0=v0):
            e.dma_start(out=stbsA[:], in_=stB[0:1, tt * 512:(tt + 1) * 512]).then_inc(sem["ld"], 16)
            e.dma_start(out=stbsB[:], in_=stB[1:2, tt * 512:(tt + 1) * 512]).then_inc(sem["ld"], 16)
            e.dma_start(out=fA(0).rearrange("p (c n) -> p c n", c=8), in_=xr[tt]).then_inc(sem["ld"], 16)
            e.dma_start(out=fA(1).rearrange("p (c n) -> p c n", c=8), in_=fr[tt]).then_inc(sem["ld"], 16)
            e.dma_start(out=xnt[:].rearrange("p (c n) -> p c n", c=8), in_=Or[tt]).then_inc(sem["ld"], 16)
            for c in range(8):
                e.wait_ge(sem["vs"], v0 + c + 1)
                e.dma_start(out=x1r[tt, c], in_=fA(0)[:, c * 512:(c + 1) * 512]).then_inc(sem["st"], 16)
        p.on("sync", p1_sync)
        p.add("ld", 80)
        p.add("st", 8 * 16)

        def p1_tens(e, l0=l0, s0=s0):
            e.wait_ge(sem["ld"], l0 + 80)
            bc(e, pb[6][:, :], ones1, stbsA[0:1, :])
            bc(e, pb[7][:, :], ones1, stbsB[0:1, :])
            for c in range(4):
                for k in range(8):
                    e.matmul(pb[c][:, :], wdb[:, k * 1024 + c * 128: k * 1024 + c * 128 + 128],
                             xnt[:, k * 512:(k + 1) * 512], start=(k == 0), stop=(k == 7))
            e.drain().then_inc(sem["ts"], 1)
            e.wait_ge(sem["ss"], s0 + 4)
            for c in range(4):
                for k in range(8):
                    e.matmul(pb[c][:, :], wdb[:, k * 1024 + (c + 4) * 128: k * 1024 + (c + 4) * 128 + 128],
                             xnt[:, k * 512:(k + 1) * 512], start=(k == 0), stop=(k == 7))
            e.drain().then_inc(sem["ts"], 1)
        p.on("tensor", p1_tens)
        p.add("ts", 2)

        def p1_scal(e, t0=t0, v0=v0):
            for c in range(8):
                e.wait_ge(sem["ts"], t0 + (1 if c < 4 else 2))
                if c > 0:
                    e.wait_ge(sem["vs"], v0 + c)
                e.activation(t8a[:, :], pb[c % 4][:, :], AF.Identity,
                             bias=CB_b_out0[:, c:c + 1])
                e.drain().then_inc(sem["ss"], 1)
        p.on("scalar", p1_scal)
        p.add("ss", 8)

        def p1_vec(e, t0=t0, s0=s0):
            e.wait_ge(sem["ts"], t0 + 1)
            for c in range(8):
                xc = fA(0)[:, c * 512:(c + 1) * 512]
                fc = fA(1)[:, c * 512:(c + 1) * 512]
                e.wait_ge(sem["ss"], s0 + c + 1)
                e.tensor_tensor(t8b[:, :], xc, pb[6][:, :], ALU.mult)
                e.tensor_tensor(t8b[:, :], t8b[:, :], pb[7][:, :], ALU.subtract)
                e.tensor_tensor(t8b[:, :], t8b[:, :], xc, ALU.add)
                e.tensor_tensor(t8b[:, :], t8b[:, :], fc, ALU.add)
                e.tensor_tensor(xc, t8b[:, :], t8a[:, :], ALU.add)
                e.drain().then_inc(sem["vs"], 1)
        p.on("vector", p1_vec)
        p.add("vs", 8)
        p.barrier()

    nst1 = p.get("st")
    def p1_fin(e, nst1=nst1):
        e.wait_ge(sem["st"], nst1)
    p.on("sync", p1_fin)
    p.barrier()

    CCv = f32a[:, 8192:12288]
    CDv = f32a[:, 12288:16384]
    S2 = dict(xt=fA(0), onesf=onesf, sqt=fA(1), t8a=t8a, t8b=t8b, pb=pb,
              sem=sem, epsc=epsc)
    emit_stats(p, x1d, T, S2, CAt, CBt, CCv, CDv)

    for tt in range(NT):
        l0 = p.get("ld"); v0 = p.get("vs")

        def p3_sync(e, tt=tt, v0=v0):
            e.dma_start(out=fA(0).rearrange("p (c n) -> p c n", c=8),
                        in_=x1r[tt].rearrange("c p n -> p c n")).then_inc(sem["ld"], 16)
            e.wait_ge(sem["vs"], v0 + 1)
            e.dma_start(out=xn1r[tt].rearrange("c p n -> p c n"), in_=xnt[:].rearrange("p (c n) -> p c n", c=8)).then_inc(sem["st"], 16)
        p.on("sync", p3_sync)
        p.add("ld", 16)
        p.add("st", 16)

        def p3_vec(e, tt=tt, l0=l0):
            e.wait_ge(sem["ld"], l0 + 16)
            sl = slice(tt * 512, (tt + 1) * 512)
            for c in range(8):
                xc = fA(0)[:, c * 512:(c + 1) * 512]
                e.tensor_tensor(t8a[:, :], xc, CCv[:, sl], ALU.mult)
                e.tensor_tensor(xnt[:, c * 512:(c + 1) * 512], t8a[:, :], CDv[:, sl], ALU.subtract)
            e.drain().then_inc(sem["vs"], 1)
        p.on("vector", p3_vec)
        p.add("vs", 1)
        p.barrier()

    nst3 = p.get("st")
    def p3_fin(e, nst3=nst3):
        e.wait_ge(sem["st"], nst3)
    p.on("sync", p3_fin)
    p.barrier()
    areg, greg, hsreg, rmsacc = fA(0), fA(1), fA(2), fA(3)

    for c in range(8):
        l0 = p.get("ld"); w0 = p.get("wl"); t0 = p.get("ts")
        v0 = p.get("vs"); s0 = p.get("ss")

        def p4_gp(e, c=c):
            for i, W in enumerate((WinT, WigT, WrgT)):
                e.dma_start(out=wdb[:, i * 1024:(i + 1) * 1024].rearrange("p (k n) -> p k n", k=8),
                            in_=W.rearrange("(k p) n -> p k n", k=8)[:, :, c * 128:(c + 1) * 128]
                            ).then_inc(sem["wl"], 16)
        p.on("gpsimd", p4_gp)
        p.add("wl", 48)

        def p4_sync(e, t0=t0):
            for tt in range(NT):
                if tt > 0:
                    e.wait_ge(sem["ts"], t0 + tt)
                e.dma_start(out=xnt[:].rearrange("p (c n) -> p c n", c=8),
                            in_=xn1r[tt].rearrange("c p n -> p c n")).then_inc(sem["ld"], 16)
        p.on("sync", p4_sync)
        p.add("ld", 16 * NT)

        def p4_tens(e, l0=l0, w0=w0, s0=s0, v0=v0):
            e.wait_ge(sem["wl"], w0 + 48)
            for tt in range(NT):
                e.wait_ge(sem["ld"], l0 + 16 * (tt + 1))
                if tt > 0:
                    e.wait_ge(sem["ss"], s0 + 5 * tt - 2)
                for i in range(3):
                    for k in range(8):
                        e.matmul(pb[i][:, :], wdb[:, i * 1024 + k * 128:i * 1024 + (k + 1) * 128],
                                 xnt[:, k * 512:(k + 1) * 512], start=(k == 0), stop=(k == 7))
                e.drain().then_inc(sem["ts"], 1)
            for tt in range(NT):
                e.wait_ge(sem["ss"], s0 + 5 * NT + 1 + tt)
                if tt > 0:
                    e.wait_ge(sem["vs"], v0 + 3 * NT + 1 + tt)
                e.matmul(pb[5][:, :], onesf[:, :], t8a[:, :], start=True, stop=True)
                e.drain().then_inc(sem["ts"], 1)
        p.on("tensor", p4_tens)
        p.add("ts", 2 * NT)

        def p4_scal(e, c=c, t0=t0, v0=v0):
            for tt in range(NT):
                e.wait_ge(sem["ts"], t0 + tt + 1)
                if tt > 0:
                    e.wait_ge(sem["vs"], v0 + 3 * tt)
                e.activation(t8a[:, :], pb[0][:, :], AF.Copy)
                e.drain().then_inc(sem["ss"], 1)
                e.activation(t8b[:, :], pb[1][:, :], AF.Sigmoid, bias=CB_big[:, c:c + 1])
                e.drain().then_inc(sem["ss"], 1)
                e.activation(t8c[:, :], pb[2][:, :], AF.Sigmoid, bias=CB_brg[:, c:c + 1])
                e.drain().then_inc(sem["ss"], 1)
                e.wait_ge(sem["vs"], v0 + 3 * tt + 1)
                e.activation(areg[:, tt * 512:(tt + 1) * 512], areg[:, tt * 512:(tt + 1) * 512],
                             AF.Exp, scale=8.0)
                e.drain().then_inc(sem["ss"], 1)
                e.wait_ge(sem["vs"], v0 + 3 * tt + 2)
                e.activation(t8c[:, :], t8d[:, :], AF.Sqrt)
                e.drain().then_inc(sem["ss"], 1)
            e.wait_ge(sem["vs"], v0 + 3 * NT + 1)
            for tt in range(NT):
                if tt > 0:
                    e.wait_ge(sem["ts"], t0 + NT + tt)
                e.activation(t8a[:, :], hsreg[:, tt * 512:(tt + 1) * 512], AF.Square)
                e.drain().then_inc(sem["ss"], 1)
        p.on("scalar", p4_scal)
        p.add("ss", 6 * NT)

        def p4_vec(e, c=c, t0=t0, s0=s0):
            for tt in range(NT):
                e.wait_ge(sem["ss"], s0 + 5 * tt + 3)
                e.tensor_tensor(greg[:, tt * 512:(tt + 1) * 512], t8b[:, :], t8a[:, :], ALU.mult)
                e.tensor_scalar(areg[:, tt * 512:(tt + 1) * 512], t8c[:, :],
                                logac[:, c:c + 1], None, ALU.mult)
                e.drain().then_inc(sem["vs"], 1)
                e.wait_ge(sem["ss"], s0 + 5 * tt + 4)
                e.tensor_tensor(t8d[:, :], areg[:, tt * 512:(tt + 1) * 512],
                                areg[:, tt * 512:(tt + 1) * 512], ALU.mult)
                e.tensor_scalar(t8d[:, :], t8d[:, :], -1.0, float(1.0 + 1e-6), ALU.mult, ALU.add)
                e.drain().then_inc(sem["vs"], 1)
                e.wait_ge(sem["ss"], s0 + 5 * tt + 5)
                e.tensor_tensor(greg[:, tt * 512:(tt + 1) * 512],
                                greg[:, tt * 512:(tt + 1) * 512], t8c[:, :], ALU.mult)
                e.drain().then_inc(sem["vs"], 1)
            for tt in range(NT):
                init = h0c_col(c) if tt == 0 else hsreg[:, tt * 512 - 1:tt * 512]
                e.tensor_tensor_scan(hsreg[:, tt * 512:(tt + 1) * 512],
                                     areg[:, tt * 512:(tt + 1) * 512],
                                     greg[:, tt * 512:(tt + 1) * 512],
                                     init, ALU.mult, ALU.add)
                fz(e)
            e.tensor_copy(bf16a[:, c * T:(c + 1) * T], hsreg[:, 0:T])
            e.drain().then_inc(sem["vs"], 1)
            for tt in range(NT):
                e.wait_ge(sem["ts"], t0 + NT + tt + 1)
                if c == 0:
                    e.tensor_copy(rmsacc[:, tt * 512:(tt + 1) * 512], pb[5][:, :])
                else:
                    e.tensor_tensor(rmsacc[:, tt * 512:(tt + 1) * 512],
                                    rmsacc[:, tt * 512:(tt + 1) * 512], pb[5][:, :], ALU.add)
                e.drain().then_inc(sem["vs"], 1)
        p.on("vector", p4_vec)
        p.add("vs", 3 * NT + 1 + NT)
        p.barrier()

    v0 = p.get("vs"); s0 = p.get("ss")

    def p5_vec(e, v0=v0, s0=s0):
        for tt in range(NT):
            e.tensor_scalar(t8a[:, :], rmsacc[:, tt * 512:(tt + 1) * 512],
                            float(1.0 / D), 1e-6, ALU.mult, ALU.add)
            e.drain().then_inc(sem["vs"], 1)
            e.wait_ge(sem["ss"], s0 + tt + 1)
            e.reciprocal(rmsacc[:, tt * 512:(tt + 1) * 512], t8b[:, :])
            e.drain().then_inc(sem["vs"], 1)
    p.on("vector", p5_vec)
    p.add("vs", 2 * NT)

    def p5_scal(e, v0=v0):
        for tt in range(NT):
            e.wait_ge(sem["vs"], v0 + 2 * tt + 1)
            e.activation(t8b[:, :], t8a[:, :], AF.Sqrt)
            e.drain().then_inc(sem["ss"], 1)
    p.on("scalar", p5_scal)
    p.add("ss", NT)
    p.barrier()

    xor_ = xout.rearrange("(c p) (u n) -> u c p n", c=8, u=NT)
    for tt in range(NT):
        l0 = p.get("ld"); w0 = p.get("wl"); t0 = p.get("ts")
        v0 = p.get("vs"); s0 = p.get("ss")

        def p6_vec(e, tt=tt, s0=s0, l0=l0, st0=p.get("st")):
            for c in range(8):
                e.tensor_tensor(t8a[:, :], bf16a[:, c * T + tt * 512: c * T + (tt + 1) * 512],
                                rmsacc[:, tt * 512:(tt + 1) * 512], ALU.mult)
                e.tensor_scalar(xnt[:, c * 512:(c + 1) * 512], t8a[:, :],
                                CB_nw[:, c:c + 1], None, ALU.mult)
            e.drain().then_inc(sem["vs"], 1)
            for oc in range(8):
                e.wait_ge(sem["ss"], s0 + oc + 1)
                e.wait_ge(sem["ld"], l0 + 16 * (oc + 1))
                if oc > 0:
                    e.wait_ge(sem["st"], st0 + 16 * oc)
                e.tensor_tensor(t8c[:, :], t8b[:, :], fA(0)[:, oc * 512:(oc + 1) * 512], ALU.add)
                e.drain().then_inc(sem["vs"], 1)
        p.on("vector", p6_vec)
        p.add("vs", 9)

        def p6_gp(e, w0=w0, t0=t0):
            for oc in range(8):
                if oc >= 2:
                    e.wait_ge(sem["ts"], t0 + oc - 1)
                e.dma_start(out=wdb[:, (oc % 2) * 1024:(oc % 2) * 1024 + 1024].rearrange("p (k n) -> p k n", k=8),
                            in_=WoutT.rearrange("(k p) n -> p k n", k=8)[:, :, oc * 128:(oc + 1) * 128]
                            ).then_inc(sem["wl"], 16)
        p.on("gpsimd", p6_gp)
        p.add("wl", 8 * 16)

        def p6_sync(e, tt=tt, v0=v0):
            for oc in range(8):
                e.dma_start(out=fA(0)[:, oc * 512:(oc + 1) * 512],
                            in_=x1r[tt, oc]).then_inc(sem["ld"], 16)
            for oc in range(8):
                e.wait_ge(sem["vs"], v0 + 2 + oc)
                e.dma_start(out=xor_[tt, oc], in_=t8c[:, :]).then_inc(sem["st"], 16)
        p.on("sync", p6_sync)
        p.add("ld", 8 * 16)
        p.add("st", 8 * 16)

        def p6_tens(e, w0=w0, v0=v0, s0=s0):
            e.wait_ge(sem["vs"], v0 + 1)
            for oc in range(8):
                e.wait_ge(sem["wl"], w0 + 16 * (oc + 1))
                if oc >= 4:
                    e.wait_ge(sem["ss"], s0 + oc - 3)
                for k in range(8):
                    e.matmul(pb[oc % 4][:, :], wdb[:, (oc % 2) * 1024 + k * 128:(oc % 2) * 1024 + (k + 1) * 128],
                             xnt[:, k * 512:(k + 1) * 512], start=(k == 0), stop=(k == 7))
                e.drain().then_inc(sem["ts"], 1)
        p.on("tensor", p6_tens)
        p.add("ts", 8)

        def p6_scal(e, t0=t0, v0=v0):
            for oc in range(8):
                e.wait_ge(sem["ts"], t0 + oc + 1)
                if oc > 0:
                    e.wait_ge(sem["vs"], v0 + 1 + oc)
                e.activation(t8b[:, :], pb[oc % 4][:, :], AF.Copy)
                e.drain().then_inc(sem["ss"], 1)
        p.on("scalar", p6_scal)
        p.add("ss", 8)
        p.barrier()

    nst6 = p.get("st")
    def p6_fin(e, nst6=nst6):
        e.wait_ge(sem["st"], nst6)
    p.on("sync", p6_fin)
    p.barrier()

    if debug:
        dbg_x2a = p.dram_out("dbg_x2a", (D, T))
        dbg_hs = p.dram_out("dbg_hs", (8, 128, T), BF16)
        dbg_rs = p.dram_out("dbg_rs", (128, T))
        dbg_xnt = p.dram_out("dbg_xnt", (128, 4096), BF16)

        def pdbg_sync(e):
            e.dma_start(out=dbg_x2a, in_=xout).then_inc(sem["st"], 16)
            e.dma_start(out=dbg_hs.rearrange("c p n -> p c n"),
                        in_=bf16a[:, 0:8 * T].rearrange("p (c n) -> p c n", c=8)
                        ).then_inc(sem["st"], 16)
            e.dma_start(out=dbg_rs, in_=rmsacc[:, 0:T]).then_inc(sem["st"], 16)
            e.dma_start(out=dbg_xnt, in_=xnt[:, 0:4096]).then_inc(sem["st"], 16)
        p.on("sync", pdbg_sync)
        p.add("st", 64)
        p.barrier()
    qw1r = qw1d.rearrange("(k p) n -> k p n", k=8)
    qw2r = qw2d.rearrange("(k p) n -> k p n", k=32)
    for (Wt, KC, WD, qwr, swi) in ((w1T, 8, H1, qw1r, 0), (w2T, 32, D, qw2r, 2)):
        l0 = p.get("ld"); v0 = p.get("vs"); t0 = p.get("ts")
        Wr = Wt.rearrange("(k p) n -> k p n", k=KC)

        def p7a_sync(e, Wr=Wr, KC=KC, WD=WD, v0=v0):
            for k in range(KC):
                if k > 0:
                    e.wait_ge(sem["vs"], v0 + k)
                e.dma_start(out=fA(0)[:, 0:WD], in_=Wr[k]).then_inc(sem["ld"], 16)
        p.on("sync", p7a_sync)
        p.add("ld", 16 * KC)

        def p7a_vec(e, KC=KC, WD=WD, l0=l0, t0=t0, swi=swi):
            for k in range(KC):
                e.wait_ge(sem["ld"], l0 + 16 * (k + 1))
                e.scalar_tensor_tensor(fA(1)[:, 0:WD], fA(0)[:, 0:WD], -1.0,
                                       fA(0)[:, 0:WD], ALU.mult, ALU.max)
                e.tensor_reduce(wcol[:, k:k + 1], fA(1)[:, 0:WD], AX.X, ALU.add)
                e.drain().then_inc(sem["vs"], 1)
            e.wait_ge(sem["ts"], t0 + 1)
            fz(e)
            e.tensor_reduce(swc[:, swi + 1:swi + 2], pb[5][:, 0:KC], AX.X, ALU.add)
            fz(e)
            e.tensor_scalar(swc[:, swi + 1:swi + 2], swc[:, swi + 1:swi + 2],
                            float(1.0 / (D * H1)), 1e-5, ALU.mult, ALU.max)
            fz(e)
            e.reciprocal(swc[:, swi:swi + 1], swc[:, swi + 1:swi + 2])
            fz(e)
            e.drain().then_inc(sem["vs"], 1)
        p.on("vector", p7a_vec)
        p.add("vs", KC + 1)

        def p7a_tens(e, KC=KC, v0=v0):
            e.wait_ge(sem["vs"], v0 + KC)
            e.matmul(pb[5][:, 0:KC], onesf[:, :], wcol[:, 0:KC], start=True, stop=True)
            e.drain().then_inc(sem["ts"], 1)
        p.on("tensor", p7a_tens)
        p.add("ts", 1)
        p.barrier()

        l0 = p.get("ld"); v0 = p.get("vs"); st0 = p.get("st")

        def p7b_sync(e, Wr=Wr, qwr=qwr, KC=KC, WD=WD, v0=v0):
            for k in range(KC):
                if k > 0:
                    e.wait_ge(sem["vs"], v0 + k)
                e.dma_start(out=fA(0)[:, 0:WD], in_=Wr[k]).then_inc(sem["ld"], 16)
                e.wait_ge(sem["vs"], v0 + k + 1)
                e.dma_start(out=qwr[k], in_=xnt[:, 0:WD]).then_inc(sem["st"], 16)
        p.on("sync", p7b_sync)
        p.add("ld", 16 * KC)
        p.add("st", 16 * KC)

        def p7b_vec(e, KC=KC, WD=WD, l0=l0, st0=st0, swi=swi):
            for k in range(KC):
                e.wait_ge(sem["ld"], l0 + 16 * (k + 1))
                if k > 0:
                    e.wait_ge(sem["st"], st0 + 16 * k)
                e.tensor_scalar(fA(1)[:, 0:WD], fA(0)[:, 0:WD], swc[:, swi:swi + 1],
                                None, ALU.mult)
                e.tensor_scalar(fA(2)[:, 0:WD], fA(1)[:, 0:WD], 0.5, None, ALU.is_ge)
                e.tensor_scalar(fA(3)[:, 0:WD], fA(1)[:, 0:WD], -0.5, None, ALU.is_lt)
                e.tensor_tensor(xnt[:, 0:WD], fA(2)[:, 0:WD], fA(3)[:, 0:WD], ALU.subtract)
                e.drain().then_inc(sem["vs"], 1)
        p.on("vector", p7b_vec)
        p.add("vs", KC)
        p.barrier()

    BITS = [128.0, 64.0, 32.0, 16.0, 8.0, 4.0, 2.0, 1.0]
    o1v = bf16a[:, 0:16384]
    q2v = bf16a[:, 16384:32768]

    def ladder(e, ybuf, qdst):
        e.tensor_scalar(t8c[:, :], ybuf, 1.0, 128.5, ALU.mult, ALU.add)
        e.tensor_copy(t8d[:, :], t8c[:, :])
        for b in BITS:
            e.tensor_scalar(t8b[:, :], t8d[:, :], float(b), None, ALU.is_ge)
            e.scalar_tensor_tensor(t8d[:, :], t8b[:, :], float(-b), t8d[:, :],
                                   ALU.mult, ALU.add)
        e.tensor_tensor(t8c[:, :], t8c[:, :], t8d[:, :], ALU.subtract)
        e.tensor_scalar(qdst, t8c[:, :], 1.0, -128.0, ALU.mult, ALU.add)

    for tt in range(NT):
        sl = slice(tt * 512, (tt + 1) * 512)
        l0 = p.get("ld"); v0 = p.get("vs"); t0 = p.get("ts"); s0 = p.get("ss")

        def p8a_sync(e, tt=tt):
            e.dma_start(out=fA(1).rearrange("p (c n) -> p c n", c=8),
                        in_=x1r[tt].rearrange("c p n -> p c n")).then_inc(sem["ld"], 16)
        p.on("sync", p8a_sync)
        p.add("ld", 16)

        def p8a_vec(e, tt=tt, sl=sl, l0=l0, t0=t0, s0=s0):
            e.wait_ge(sem["ld"], l0 + 16)
            for c in range(8):
                x1c = fA(1)[:, c * 512:(c + 1) * 512]
                hc = fA(0)[:, c * 512:(c + 1) * 512]
                e.tensor_tensor(t8a[:, :], x1c, CAt[:, sl], ALU.mult)
                e.tensor_tensor(t8a[:, :], t8a[:, :], CBt[:, sl], ALU.subtract)
                e.tensor_scalar(t8a[:, :], t8a[:, :], CB_g1[:, c:c + 1], None, ALU.mult)
                e.tensor_scalar(hc, t8a[:, :], CB_b1[:, c:c + 1], None, ALU.add)
                if c == 0:
                    e.scalar_tensor_tensor(t8b[:, :], hc, -1.0, hc, ALU.mult, ALU.max)
                else:
                    e.scalar_tensor_tensor(t8c[:, :], hc, -1.0, hc, ALU.mult, ALU.max)
                    e.tensor_tensor(t8b[:, :], t8b[:, :], t8c[:, :], ALU.max)
            e.drain().then_inc(sem["vs"], 1)
            e.wait_ge(sem["ts"], t0 + 1)
            for blk in range(4):
                e.tensor_reduce(wcol[:, blk:blk + 1], pb[6][:, blk * 128:(blk + 1) * 128],
                                AX.X, ALU.max)
                fz(e)
            e.tensor_scalar(wcol[:, 0:4], wcol[:, 0:4], 1e-5, None, ALU.max)
            fz(e)
            e.reciprocal(wcol[:, 4:8], wcol[:, 0:4])
            fz(e)
            e.tensor_scalar(wcol[:, 4:8], wcol[:, 4:8], 127.0, None, ALU.mult)
            e.tensor_scalar(wcol[:, 8:12], wcol[:, 0:4], float(1.0 / 127.0), None, ALU.mult)
            e.drain().then_inc(sem["vs"], 1)
            e.wait_ge(sem["ts"], t0 + 3)
            e.tensor_copy(amr[:, :], pb[4][:, :])
            e.tensor_copy(am2r[:, :], pb[5][:, :])
            e.drain().then_inc(sem["vs"], 1)
            for c in range(8):
                hc = fA(0)[:, c * 512:(c + 1) * 512]
                e.tensor_tensor(fA(2)[:, 0:512], hc, amr[:, :], ALU.mult)
                ladder(e, fA(2)[:, 0:512], qt[:, c * 512:(c + 1) * 512])
            e.drain().then_inc(sem["vs"], 1)
        p.on("vector", p8a_vec)
        p.add("vs", 4)

        def p8a_tens(e, v0=v0, s0=s0):
            e.wait_ge(sem["vs"], v0 + 1)
            for blk in range(4):
                e.transpose(pb[6][:, blk * 128:(blk + 1) * 128],
                            t8b[:, blk * 128:(blk + 1) * 128], eyes[:, :])
            e.drain().then_inc(sem["ts"], 1)
            e.wait_ge(sem["vs"], v0 + 2)
            for blk in range(4):
                e.transpose(pb[7][0:1, blk * 128:(blk + 1) * 128],
                            wcol[:, 4 + blk:5 + blk], eyes[:, :])
                e.transpose(pb[3][0:1, blk * 128:(blk + 1) * 128],
                            wcol[:, 8 + blk:9 + blk], eyes[:, :])
            e.drain().then_inc(sem["ts"], 1)
            e.wait_ge(sem["ss"], s0 + 1)
            bc(e, pb[4][:, :], ones1, t8d[0:1, :])
            bc(e, pb[5][:, :], ones1, t8c[0:1, :])
            e.drain().then_inc(sem["ts"], 1)
        p.on("tensor", p8a_tens)
        p.add("ts", 3)

        def p8a_scal(e, t0=t0):
            e.wait_ge(sem["ts"], t0 + 2)
            e.activation(t8d[0:1, :], pb[7][0:1, 0:512], AF.Copy)
            e.activation(t8c[0:1, :], pb[3][0:1, 0:512], AF.Copy)
            e.drain().then_inc(sem["ss"], 1)
        p.on("scalar", p8a_scal)
        p.add("ss", 1)
        p.barrier()

        l0 = p.get("ld"); v0 = p.get("vs"); t0 = p.get("ts"); s0 = p.get("ss")

        def p8b_sync(e, tt=tt, t0=t0):
            for oc in range(NH1C):
                if oc >= 2:
                    e.wait_ge(sem["ts"], t0 + 2 * (oc - 1))
                e.dma_start(out=wdb[:, (oc % 2) * 1024:(oc % 2) * 1024 + 1024].rearrange("p (k n) -> p k n", k=8),
                            in_=qw1d.rearrange("(k p) (o n) -> o p k n", k=8, o=NH1C)[oc]
                            ).then_inc(sem["ld"], 16)
        p.on("sync", p8b_sync)
        p.add("ld", 16 * NH1C)

        def p8b_tens(e, l0=l0, s0=s0, v0=v0):
            for oc in range(NH1C):
                e.wait_ge(sem["ld"], l0 + 16 * (oc + 1))
                if oc >= 4:
                    e.wait_ge(sem["ss"], s0 + 2 * (oc - 4) + 1)
                for k in range(8):
                    e.matmul(pb[oc % 4][:, :], wdb[:, (oc % 2) * 1024 + k * 128:(oc % 2) * 1024 + (k + 1) * 128],
                             qt[:, k * 512:(k + 1) * 512], start=(k == 0), stop=(k == 7))
                e.drain().then_inc(sem["ts"], 1)
                e.wait_ge(sem["ss"], s0 + 2 * oc + 2)
                e.matmul(pb[6][:, :], onesb[:, :], o1v[:, oc * 512:(oc + 1) * 512],
                         start=(oc == 0), stop=(oc == NH1C - 1))
                e.matmul(pb[7][:, :], onesf[:, :], t8a[:, :],
                         start=(oc == 0), stop=(oc == NH1C - 1))
                e.drain().then_inc(sem["ts"], 1)
        p.on("tensor", p8b_tens)
        p.add("ts", 2 * NH1C)

        def p8b_scal(e, t0=t0, v0=v0):
            for oc in range(NH1C):
                e.wait_ge(sem["ts"], t0 + 2 * oc + 1)
                if oc > 0:
                    e.wait_ge(sem["vs"], v0 + oc)
                e.activation(t8b[:, :], pb[oc % 4][:, :], AF.Copy)
                e.drain().then_inc(sem["ss"], 1)
                e.wait_ge(sem["vs"], v0 + oc + 1)
                if oc > 0:
                    e.wait_ge(sem["ts"], t0 + 2 * oc)
                e.activation(o1v[:, oc * 512:(oc + 1) * 512], t8b[:, :], AF.Gelu)
                e.activation(t8a[:, :], o1v[:, oc * 512:(oc + 1) * 512], AF.Square)
                e.drain().then_inc(sem["ss"], 1)
        p.on("scalar", p8b_scal)
        p.add("ss", 2 * NH1C)

        def p8b_vec(e, s0=s0):
            for oc in range(NH1C):
                e.wait_ge(sem["ss"], s0 + 2 * oc + 1)
                e.tensor_tensor(t8b[:, :], t8b[:, :], am2r[:, :], ALU.mult)
                e.tensor_scalar(t8b[:, :], t8b[:, :], swc[:, 1:2], None, ALU.mult)
                e.tensor_scalar(t8b[:, :], t8b[:, :], CB_bb1[:, oc:oc + 1], None, ALU.add)
                e.drain().then_inc(sem["vs"], 1)
        p.on("vector", p8b_vec)
        p.add("vs", NH1C)
        p.barrier()

        v0 = p.get("vs"); s0 = p.get("ss")

        def p8c_vec(e, v0=v0, s0=s0):
            e.tensor_scalar(murb[:, :], pb[6][:, :], float(1.0 / H1), None, ALU.mult)
            e.tensor_scalar(t8a[:, :], pb[7][:, :], float(1.0 / H1), None, ALU.mult)
            e.tensor_tensor(t8d[:, :], murb[:, :], murb[:, :], ALU.mult)
            e.tensor_tensor(t8a[:, :], t8a[:, :], t8d[:, :], ALU.subtract)
            e.tensor_scalar(t8a[:, :], t8a[:, :], EPS, None, ALU.add)
            e.drain().then_inc(sem["vs"], 1)
            e.wait_ge(sem["ss"], s0 + 1)
            e.reciprocal(rsrb[:, :], t8b[:, :])
            e.drain().then_inc(sem["vs"], 1)
        p.on("vector", p8c_vec)
        p.add("vs", 2)

        def p8c_scal(e, v0=v0):
            e.wait_ge(sem["vs"], v0 + 1)
            e.activation(t8b[:, :], t8a[:, :], AF.Sqrt)
            e.drain().then_inc(sem["ss"], 1)
        p.on("scalar", p8c_scal)
        p.add("ss", 1)
        p.barrier()

        v0 = p.get("vs"); t0 = p.get("ts"); s0 = p.get("ss")

        def xn2_into(e, oc, dst):
            oslc = slice(oc * 512, (oc + 1) * 512)
            e.tensor_tensor(dst, o1v[:, oslc], murb[:, :], ALU.subtract)
            e.tensor_tensor(dst, dst, rsrb[:, :], ALU.mult)
            e.tensor_scalar(dst, dst, CB_g2[:, oc:oc + 1], None, ALU.mult)
            e.tensor_scalar(dst, dst, CB_b2[:, oc:oc + 1], None, ALU.add)

        def p8d_vec(e, t0=t0, s0=s0):
            for oc in range(NH1C):
                xn2_into(e, oc, fA(2)[:, 0:512])
                if oc == 0:
                    e.scalar_tensor_tensor(t8b[:, :], fA(2)[:, 0:512], -1.0,
                                           fA(2)[:, 0:512], ALU.mult, ALU.max)
                else:
                    e.scalar_tensor_tensor(t8c[:, :], fA(2)[:, 0:512], -1.0,
                                           fA(2)[:, 0:512], ALU.mult, ALU.max)
                    e.tensor_tensor(t8b[:, :], t8b[:, :], t8c[:, :], ALU.max)
            e.drain().then_inc(sem["vs"], 1)
            e.wait_ge(sem["ts"], t0 + 1)
            for blk in range(4):
                e.tensor_reduce(wcol[:, blk:blk + 1], pb[6][:, blk * 128:(blk + 1) * 128],
                                AX.X, ALU.max)
                fz(e)
            e.tensor_scalar(wcol[:, 0:4], wcol[:, 0:4], 1e-5, None, ALU.max)
            fz(e)
            e.reciprocal(wcol[:, 4:8], wcol[:, 0:4])
            fz(e)
            e.tensor_scalar(wcol[:, 4:8], wcol[:, 4:8], 127.0, None, ALU.mult)
            e.tensor_scalar(wcol[:, 8:12], wcol[:, 0:4], float(1.0 / 127.0), None, ALU.mult)
            e.drain().then_inc(sem["vs"], 1)
            e.wait_ge(sem["ts"], t0 + 3)
            e.tensor_copy(amr[:, :], pb[4][:, :])
            e.tensor_copy(am2r[:, :], pb[5][:, :])
            e.drain().then_inc(sem["vs"], 1)
            for oc in range(NH1C):
                xn2_into(e, oc, fA(2)[:, 0:512])
                e.tensor_tensor(fA(2)[:, 0:512], fA(2)[:, 0:512], amr[:, :], ALU.mult)
                ladder(e, fA(2)[:, 0:512], q2v[:, oc * 512:(oc + 1) * 512])
            e.drain().then_inc(sem["vs"], 1)
        p.on("vector", p8d_vec)
        p.add("vs", 4)

        def p8d_tens(e, v0=v0, s0=s0):
            e.wait_ge(sem["vs"], v0 + 1)
            for blk in range(4):
                e.transpose(pb[6][:, blk * 128:(blk + 1) * 128],
                            t8b[:, blk * 128:(blk + 1) * 128], eyes[:, :])
            e.drain().then_inc(sem["ts"], 1)
            e.wait_ge(sem["vs"], v0 + 2)
            for blk in range(4):
                e.transpose(pb[7][0:1, blk * 128:(blk + 1) * 128],
                            wcol[:, 4 + blk:5 + blk], eyes[:, :])
                e.transpose(pb[3][0:1, blk * 128:(blk + 1) * 128],
                            wcol[:, 8 + blk:9 + blk], eyes[:, :])
            e.drain().then_inc(sem["ts"], 1)
            e.wait_ge(sem["ss"], s0 + 1)
            bc(e, pb[4][:, :], ones1, t8d[0:1, :])
            bc(e, pb[5][:, :], ones1, t8c[0:1, :])
            e.drain().then_inc(sem["ts"], 1)
        p.on("tensor", p8d_tens)
        p.add("ts", 3)

        def p8d_scal(e, t0=t0):
            e.wait_ge(sem["ts"], t0 + 2)
            e.activation(t8d[0:1, :], pb[7][0:1, 0:512], AF.Copy)
            e.activation(t8c[0:1, :], pb[3][0:1, 0:512], AF.Copy)
            e.drain().then_inc(sem["ss"], 1)
        p.on("scalar", p8d_scal)
        p.add("ss", 1)
        p.barrier()

        l0 = p.get("ld"); v0 = p.get("vs"); t0 = p.get("ts")
        s0 = p.get("ss"); st0 = p.get("st")

        def p8e_sync(e, tt=tt, t0=t0, v0=v0):
            for c in range(8):
                if c >= 2:
                    e.wait_ge(sem["ts"], t0 + c - 1)
                e.dma_start(out=wdb[:, (c % 2) * 4096:(c % 2) * 4096 + 4096].rearrange("p (k n) -> p k n", k=32),
                            in_=qw2d.rearrange("(k p) (o n) -> o p k n", k=32, o=8)[c]
                            ).then_inc(sem["ld"], 16)
                e.dma_start(out=fA(1)[:, c * 512:(c + 1) * 512], in_=xor_[tt, c]).then_inc(sem["ld"], 16)
            for c in range(8):
                e.wait_ge(sem["vs"], v0 + c + 1)
                e.dma_start(out=xor_[tt, c], in_=t8d[:, :]).then_inc(sem["st"], 16)
        p.on("sync", p8e_sync)
        p.add("ld", 32 * 8)
        p.add("st", 16 * 8)

        def p8e_tens(e, l0=l0, s0=s0):
            for c in range(8):
                e.wait_ge(sem["ld"], l0 + 32 * (c + 1))
                if c >= 4:
                    e.wait_ge(sem["ss"], s0 + c - 3)
                for k in range(32):
                    e.matmul(pb[c % 4][:, :], wdb[:, (c % 2) * 4096 + k * 128:(c % 2) * 4096 + (k + 1) * 128],
                             q2v[:, k * 512:(k + 1) * 512], start=(k == 0), stop=(k == 31))
                e.drain().then_inc(sem["ts"], 1)
        p.on("tensor", p8e_tens)
        p.add("ts", 8)

        def p8e_scal(e, t0=t0, v0=v0):
            for c in range(8):
                e.wait_ge(sem["ts"], t0 + c + 1)
                if c > 0:
                    e.wait_ge(sem["vs"], v0 + c)
                e.activation(t8b[:, :], pb[c % 4][:, :], AF.Copy)
                e.drain().then_inc(sem["ss"], 1)
        p.on("scalar", p8e_scal)
        p.add("ss", 8)

        def p8e_vec(e, s0=s0, l0=l0, st0=st0):
            for c in range(8):
                e.wait_ge(sem["ss"], s0 + c + 1)
                e.wait_ge(sem["ld"], l0 + 32 * (c + 1))
                if c > 0:
                    e.wait_ge(sem["st"], st0 + 16 * c)
                e.tensor_tensor(t8d[:, :], t8b[:, :], am2r[:, :], ALU.mult)
                e.tensor_scalar(t8d[:, :], t8d[:, :], swc[:, 3:4], None, ALU.mult)
                e.tensor_scalar(t8d[:, :], t8d[:, :], CB_bb2[:, c:c + 1], None, ALU.add)
                e.tensor_tensor(t8d[:, :], t8d[:, :], fA(1)[:, c * 512:(c + 1) * 512], ALU.add)
                e.drain().then_inc(sem["vs"], 1)
        p.on("vector", p8e_vec)
        p.add("vs", 8)
        p.barrier()

    if debug:
        dbg_q1 = p.dram_out("dbg_q1", (128, 4096), BF16)
        dbg_o1 = p.dram_out("dbg_o1", (128, 16384), BF16)
        dbg_q2 = p.dram_out("dbg_q2", (128, 16384), BF16)
        dbg_amr = p.dram_out("dbg_amr", (128, 512))
        dbg_am2r = p.dram_out("dbg_am2r", (128, 512))
        dbg_mur = p.dram_out("dbg_mur", (128, 512))
        dbg_rsr = p.dram_out("dbg_rsr", (128, 512))
        dbg_sw = p.dram_out("dbg_sw", (128, 8))

        def pdbg2_sync(e):
            e.dma_start(out=dbg_q1, in_=qt[:, 0:4096]).then_inc(sem["st"], 16)
            e.dma_start(out=dbg_o1, in_=o1v[:, :]).then_inc(sem["st"], 16)
            e.dma_start(out=dbg_q2, in_=q2v[:, :]).then_inc(sem["st"], 16)
            e.dma_start(out=dbg_amr, in_=amr[:, :]).then_inc(sem["st"], 16)
            e.dma_start(out=dbg_am2r, in_=am2r[:, :]).then_inc(sem["st"], 16)
            e.dma_start(out=dbg_mur, in_=murb[:, :]).then_inc(sem["st"], 16)
            e.dma_start(out=dbg_rsr, in_=rsrb[:, :]).then_inc(sem["st"], 16)
            e.dma_start(out=dbg_sw, in_=swc[:, :]).then_inc(sem["st"], 16)
        p.on("sync", pdbg2_sync)
        p.add("st", 8 * 16)
        p.barrier()

    nstF = p.get("st")
    def p8_fin(e, nstF=nstF):
        e.wait_ge(sem["st"], nstF)
    p.on("sync", p8_fin)
    return p, dict()


def stage_B(d, x, OT, ffn0, stB):
    F = np.float32
    return dict(
        x_fm=np.ascontiguousarray(x.T.astype(F)),
        OT_in=np.ascontiguousarray(OT),
        ffn0_in=np.ascontiguousarray(ffn0.astype(F)),
        stB=np.ascontiguousarray(stB.astype(F)),
        wo0T=np.ascontiguousarray(d["L0_attn_w_out"].astype(F)),
        b_out0c=np.ascontiguousarray(d["L0_attn_b_out"].astype(F).reshape(8, 128).T),
        WinT=np.ascontiguousarray(d["L1_attn_W_in"].T.astype(F)),
        WigT=np.ascontiguousarray(d["L1_attn_W_ig"].T.astype(F)),
        WrgT=np.ascontiguousarray(d["L1_attn_W_rg"].T.astype(F)),
        WoutT=np.ascontiguousarray(d["L1_attn_W_out"].T.astype(F)),
        bigc=np.ascontiguousarray(d["L1_attn_b_ig"].astype(F).reshape(8, 128).T),
        brgc=np.ascontiguousarray(d["L1_attn_b_rg"].astype(F).reshape(8, 128).T),
        lamc=np.ascontiguousarray(d["L1_attn_lam"].astype(F).reshape(8, 128).T),
        h0c=np.ascontiguousarray(d["L1_attn_h0"][0, 0].astype(F).reshape(8, 128).T),
        nwc=np.ascontiguousarray(d["L1_attn_norm_w"].astype(F).reshape(8, 128).T),
        g1c=np.ascontiguousarray(d["L1_ffn_ln1_g"].astype(F).reshape(8, 128).T),
        b1c=np.ascontiguousarray(d["L1_ffn_ln1_b"].astype(F).reshape(8, 128).T),
        bb1c=np.ascontiguousarray(d["L1_ffn_b1"].astype(F).reshape(32, 128).T),
        g2c=np.ascontiguousarray(d["L1_ffn_ln2_g"].astype(F).reshape(32, 128).T),
        b2c=np.ascontiguousarray(d["L1_ffn_ln2_b"].astype(F).reshape(32, 128).T),
        bb2c=np.ascontiguousarray(d["L1_ffn_b2"].astype(F).reshape(8, 128).T),
        w1T=np.ascontiguousarray(d["L1_ffn_w1"].T.astype(F)),
        w2T=np.ascontiguousarray(d["L1_ffn_w2"].T.astype(F)),
        eye128=np.eye(128, dtype=F),
    )



def stage_A_full(d, core, T):
    F = np.float32
    TH = T // 2
    b, half = core // 2, core % 2
    heads = range(half * 8, half * 8 + 8)
    x = np.ascontiguousarray(d["L0_attn_x"][b, :T].astype(F))
    x_fm = np.ascontiguousarray(x.T)
    x_own = np.ascontiguousarray(x_fm[:, half * TH:(half + 1) * TH])
    wq = d["L0_attn_w_qkvb"]; bq = d["L0_attn_b_qkvb"]
    wcols, bcols = [], []
    for qq in range(4):
        for h in heads:
            base = qq * 1024 + h * 64
            wcols.append(wq[:, base:base + 64])
            bcols.append(bq[base:base + 64])
    wqkv = np.ascontiguousarray(np.concatenate(wcols, axis=1))
    bqkv = np.ascontiguousarray(np.concatenate(bcols))
    gu = np.empty((34, 1024, 1024), F)
    dn = np.empty((34, 512, 1024), F)
    for e in range(32):
        gu[e, :, 0:512] = d["L0_ffn_gate_w"][e].T
        gu[e, :, 512:1024] = d["L0_ffn_up_w"][e].T
        dn[e] = d["L0_ffn_down_w"][e].T
    sgT = d["L0_ffn_shared_gate_w"].T; suT = d["L0_ffn_shared_up_w"].T
    sdT = d["L0_ffn_shared_down_w"].T
    gu[32, :, 0:512] = sgT[:, 0:512]; gu[32, :, 512:1024] = suT[:, 0:512]
    gu[33, :, 0:512] = sgT[:, 512:1024]; gu[33, :, 512:1024] = suT[:, 512:1024]
    dn[32] = sdT[0:512]; dn[33] = sdT[512:1024]
    m = np.arange(16)
    msku = (m[:, None] < m[None, :]).astype(F)
    mskg = (m[:, None] <= m[None, :]).astype(F)
    return dict(
        x_fm=x_fm, x_own=x_own, wqkv=wqkv,
        bqkv=np.ascontiguousarray(bqkv.reshape(32, 64).T),
        ln_g=np.ascontiguousarray(d["L0_attn_ln_g"].astype(F).reshape(8, 128).T),
        ln_b=np.ascontiguousarray(d["L0_attn_ln_b"].astype(F).reshape(8, 128).T),
        rwT=np.ascontiguousarray(d["L0_ffn_router_weight"].T.astype(F)),
        rbias=np.ascontiguousarray(np.repeat(d["L0_ffn_router_bias"].reshape(1, 32), 128, 0).astype(F)),
        gu_w=gu, dn_w=dn,
        msku_in=np.ascontiguousarray(np.tile(msku, (1, 64))).astype(ml_dtypes.bfloat16),
        mskg_in=np.ascontiguousarray(np.tile(mskg, (1, 64))).astype(ml_dtypes.bfloat16),
        eye64=np.eye(64, dtype=F),
    )


def kernel(**inputs):
    d = inputs
    T = 4096
    TH = T // 2

    imsA = [stage_A_full(d, core, T) for core in range(8)]
    pA, _ = build_A(T)
    ncA = pA.finalize()
    resA = run_bass_kernel_spmd(ncA, imsA, core_ids=list(range(8)))

    imsB = []
    for core in range(8):
        b = core % 4
        rE = resA.results[2 * b]
        rO = resA.results[2 * b + 1]
        O = np.concatenate([np.asarray(rE["O"]), np.asarray(rO["O"])], axis=1)
        OT = np.ascontiguousarray(O.T)
        ffn0 = np.concatenate([np.asarray(rE["ffn"]).reshape(D, TH),
                               np.asarray(rO["ffn"]).reshape(D, TH)], axis=1)
        stB = np.concatenate([np.asarray(rE["stO"]), np.asarray(rO["stO"])], axis=1)
        x = d["L0_attn_x"][b].astype(np.float32)
        imsB.append(stage_B(d, x, OT, ffn0, stB))
    pB, _ = build_B(T)
    ncB = pB.finalize()
    resB = run_bass_kernel_spmd(ncB, imsB, core_ids=list(range(8)))

    out = np.stack([np.ascontiguousarray(np.asarray(resB.results[b]["xout"]).T)
                    for b in range(4)], axis=0)
    return out.astype(np.float32)


# version 08 (final)
# speedup vs baseline: 1.2287x; 1.0409x over previous
import numpy as np
import ml_dtypes
from contextlib import ExitStack

import concourse.bass as bass
import concourse.mybir as mybir
from concourse.bass_utils import run_bass_kernel_spmd

F32 = mybir.dt.float32
BF16 = mybir.dt.bfloat16
AF = mybir.ActivationFunctionType
ALU = mybir.AluOpType
AX = mybir.AxisListType
NPBF = ml_dtypes.bfloat16

D = 1024
NH_C = 8
HD = 64
FC = NH_C * HD
CC = 16
STRIP = 128
NE = 32
EI = 512
EPS = float(1e-5)
BIG = float(1e30)

ENGS = ("sync", "tensor", "vector", "scalar", "gpsimd")

R_CA, R_CB = 0, 1
R_CC, R_CD, R_CDN = 2, 3, 4
R_MU, R_EX2, R_VAR = 5, 6, 7
R_S0, R_S1, R_S2 = 8, 9, 10
R_NW0, R_NW1 = 11, 12
R_RB = 13
NROWS = 14


class Prog:
    def __init__(self):
        self.nc = bass.Bass("TRN2", target_bir_lowering=False, debug=False,
                            detect_race_conditions=False)
        self.es = ExitStack()
        self.es.__enter__()
        self.phases = [{e: [] for e in ENGS}]
        self.c = {}

    def add(self, name, n=1):
        self.c[name] = self.c.get(name, 0) + n
        return self.c[name]

    def get(self, name):
        return self.c.get(name, 0)

    def dram_in(self, name, shape, dt=F32):
        return self.nc.dram_tensor(name, list(shape), dt, kind="ExternalInput").ap()

    def dram_out(self, name, shape, dt=F32):
        return self.nc.dram_tensor(name, list(shape), dt, kind="ExternalOutput").ap()

    def sb(self, name, shape, dt=F32):
        return self.es.enter_context(self.nc.sbuf_tensor(name, list(shape), dt))

    def ps(self, name, shape, dt=F32):
        return self.es.enter_context(self.nc.psum_tensor(name, list(shape), dt))

    def sem(self, name):
        return self.es.enter_context(self.nc.semaphore(name))

    def on(self, eng, fn):
        self.phases[-1][eng].append(fn)

    def barrier(self):
        self.phases.append({e: [] for e in ENGS})

    def finalize(self):
        nc = self.nc
        ph = self.sem("ph")
        nphase = len(self.phases)
        block = self.es.enter_context(nc.Block())

        def stream(ename):
            def run(eng):
                for k in range(nphase):
                    if k > 0:
                        eng.wait_ge(ph, 5 * k)
                    for fn in self.phases[k][ename]:
                        fn(eng)
                    if k < nphase - 1:
                        if ename == "sync" or not self.phases[k][ename]:
                            eng.nop().then_inc(ph, 1)
                        else:
                            eng.drain().then_inc(ph, 1)
            return run

        block.sync(stream("sync"))
        block.tensor(stream("tensor"))
        block.vector(stream("vector"))
        block.scalar(stream("scalar"))
        block.gpsimd(stream("gpsimd"))
        self.es.__exit__(None, None, None)
        return nc


def bc(e, bank_ap, ones1, row_ap):
    m = bank_ap.shape[0]
    e.matmul(bank_ap, ones1[:, 0:m], row_ap, start=True, stop=True)


def emit_stats(p, x_dram, Tn, S, dst_ca, dst_cb, dst_cc, dst_cd):
    sem = S["sem"]
    xt, onesf, sqt, pb = S["xt"], S["onesf"], S["sqt"], S["pb"]
    t8a, t8b = S["t8a"], S["t8b"]
    epsc = S["epsc"]
    ntile = Tn // 512
    w0 = p.get("wl"); t0 = p.get("ts"); v0 = p.get("vs"); s0 = p.get("ss")

    def gp(e):
        for tt in range(ntile):
            if tt > 0:
                e.wait_ge(sem["ts"], t0 + 2 * tt)
            e.dma_start(out=xt[:].rearrange("p (c n) -> p c n", c=8),
                        in_=x_dram.rearrange("(c p) (u n) -> u p c n", c=8, u=ntile)[tt]
                        ).then_inc(sem["wl"], 16)
    p.on("gpsimd", gp)
    p.add("wl", 16 * ntile)

    def tens(e):
        for tt in range(ntile):
            e.wait_ge(sem["wl"], w0 + 16 * (tt + 1))
            for ch in range(8):
                e.matmul(pb[0][:, :], onesf[:, :], xt[:, ch * 512:(ch + 1) * 512],
                         start=(ch == 0), stop=(ch == 7))
            e.wait_ge(sem["ss"], s0 + 3 * tt + 1)
            for ch in range(8):
                e.matmul(pb[1][:, :], onesf[:, :], sqt[:, ch * 512:(ch + 1) * 512],
                         start=(ch == 0), stop=(ch == 7))
            e.drain().then_inc(sem["ts"], 1)
            e.wait_ge(sem["vs"], v0 + 3 * (tt + 1))
            e.drain().then_inc(sem["ts"], 1)
    p.on("tensor", tens)
    p.add("ts", 2 * ntile)

    def scal(e):
        for tt in range(ntile):
            e.wait_ge(sem["wl"], w0 + 16 * (tt + 1))
            e.activation(sqt[:, 0:2048], xt[:, 0:2048], AF.Square)
            e.activation(sqt[:, 2048:4096], xt[:, 2048:4096], AF.Square)
            e.drain().then_inc(sem["ss"], 1)
            e.wait_ge(sem["vs"], v0 + 3 * tt + 1)
            e.activation(t8b[:, :], t8a[:, :], AF.Sqrt, bias=epsc[:, 0:1])
            e.drain().then_inc(sem["ss"], 1)
            e.wait_ge(sem["vs"], v0 + 3 * tt + 2)
            e.activation(t8b[:, :], sqt[:, 1536:2048], AF.Sqrt)
            e.drain().then_inc(sem["ss"], 1)
    p.on("scalar", scal)
    p.add("ss", 3 * ntile)

    def vec(e):
        for tt in range(ntile):
            sl = slice(tt * 512, (tt + 1) * 512)
            e.wait_ge(sem["ts"], t0 + 2 * tt + 1)
            e.tensor_scalar(sqt[:, 512:1024], pb[0][:, :], float(1.0 / D), None, ALU.mult)
            e.tensor_scalar(t8a[:, :], pb[1][:, :], float(1.0 / D), None, ALU.mult)
            e.tensor_tensor(sqt[:, 0:512], sqt[:, 512:1024], sqt[:, 512:1024], ALU.mult)
            e.tensor_tensor(t8a[:, :], t8a[:, :], sqt[:, 0:512], ALU.subtract)
            e.drain().then_inc(sem["vs"], 1)
            e.wait_ge(sem["ss"], s0 + 3 * tt + 2)
            e.reciprocal(sqt[:, 1024:1536], t8b[:, :])
            if dst_cc is not None:
                e.tensor_copy(dst_cc[:, sl], sqt[:, 1024:1536])
            if dst_cd is not None:
                e.tensor_tensor(dst_cd[:, sl], sqt[:, 512:1024], sqt[:, 1024:1536], ALU.mult)
            e.tensor_tensor(sqt[:, 0:512], sqt[:, 1024:1536], sqt[:, 1024:1536], ALU.mult)
            e.tensor_tensor(sqt[:, 0:512], sqt[:, 0:512], t8a[:, :], ALU.mult)
            e.tensor_scalar(sqt[:, 0:512], sqt[:, 0:512], EPS, None, ALU.add)
            e.reciprocal(sqt[:, 1536:2048], sqt[:, 0:512])
            e.drain().then_inc(sem["vs"], 1)
            e.wait_ge(sem["ss"], s0 + 3 * tt + 3)
            if dst_ca is not None:
                e.tensor_tensor(dst_ca[:, sl], sqt[:, 1024:1536], t8b[:, :], ALU.mult)
                e.tensor_tensor(dst_cb[:, sl], sqt[:, 512:1024], dst_ca[:, sl], ALU.mult)
            e.drain().then_inc(sem["vs"], 1)
    p.on("vector", vec)
    p.add("vs", 3 * ntile)
    p.barrier()



def build_A(T, debug=0, stage=4):
    TH = T // 2
    NTH = TH // 512
    NSTRIP = T // STRIP
    p = Prog()

    x_fm = p.dram_in("x_fm", (D, T))
    x_own = p.dram_in("x_own", (D, TH))
    wqkv = p.dram_in("wqkv", (D, 2048))
    bqkv = p.dram_in("bqkv", (64, 32))
    ln_g = p.dram_in("ln_g", (128, 8))
    ln_b = p.dram_in("ln_b", (128, 8))
    rwT = p.dram_in("rwT", (D, NE))
    rbias = p.dram_in("rbias", (128, NE))
    gu_w = p.dram_in("gu_w", (34, D, 2 * EI))
    dn_w = p.dram_in("dn_w", (34, EI, D))
    msku_in = p.dram_in("msku_in", (CC, 1024), BF16)
    mskg_in = p.dram_in("mskg_in", (CC, 1024), BF16)
    eye64 = p.dram_in("eye64", (64, 64))

    O_out = p.dram_out("O", (T, FC), BF16)
    ffn_out = p.dram_out("ffn", (8, 128, TH))
    stO_out = p.dram_out("stO", (2, TH))
    dbg = {}
    if debug:
        dbg["W"] = p.dram_out("dbgW", (34, TH))
        dbg["xn"] = p.dram_out("dbgxn", (D, TH), BF16)

    ones1 = p.sb("ones1", (1, 128))
    onesc = p.sb("onesc", (128, 1))
    onesf = p.sb("onesf", (128, 128))
    xt = p.sb("xt", (128, 8 * 512))
    sqt = p.sb("sqt", (128, 8 * 512))
    t8a = p.sb("t8a", (128, 512))
    t8b = p.sb("t8b", (128, 512))
    epsc = p.sb("epsc", (128, 1))
    CAt = p.sb("CAt", (128, T), BF16)
    CBt = p.sb("CBt", (128, T), BF16)
    ovl1 = p.sb("ovl1", (128, max(2 * TH, 4096)))
    CCt = ovl1[:, 0:TH]
    CDt = ovl1[:, TH:2 * TH]
    stoc = p.sb("stoc", (128, (TH // 128) * 2))
    xnbf = p.sb("xnbf", (128, 8 * TH), BF16)
    wmT = p.sb("wmT", (34, TH))
    rsumb = p.sb("rsumb", (128, NE))
    rbb = p.sb("rbb", (128, NE))
    rwTs = p.sb("rwTs", (128, 8 * NE))
    lngs = p.sb("lngs", (128, 8))
    lnbs = p.sb("lnbs", (128, 8))
    bqs = p.sb("bqs", (64, 32))
    gub = [p.sb(f"gub{i}", (128, 8 * 1024), BF16) for i in range(2)]
    dnb = p.sb("dnb", (128, 4 * 1024), BF16)
    wrow = p.sb("wrow", (128, 512))
    hbuf = p.sb("hbuf", (128, 4 * 512), BF16)
    ovl2 = p.sb("ovl2", (128, 4096))
    eacc = ovl2[:, 0:4096]
    tokw = p.sb("tokw", (128, NE))
    tok2 = p.sb("tok2", (128, NE))
    tokg = p.sb("tokg", (128, 8))
    tcol = p.sb("tcol", (128, 4))
    wtok = p.sb("wtok", (128, NE))
    xnl = p.sb("xnl", (128, 8 * STRIP), BF16)
    qkvb = ovl1[0:64, 0:4096]
    ktm2 = ovl2[0:CC, 0:1024]
    vtm2 = ovl2[0:CC, 1024:2048]
    btm2 = ovl2[0:CC, 2048:3072]
    ktmS = p.sb("ktmS", (128, 512))
    vtmS = p.sb("vtmS", (128, 512))
    btmS = p.sb("btmS", (128, 512))
    Asp = p.sb("Asp", (CC, 1024))
    Gm = p.sb("Gm", (CC, 1024))
    msku = p.sb("msku", (CC, 1024), BF16)
    mskg = p.sb("mskg", (CC, 1024), BF16)
    eyes = p.sb("eyes", (64, 64))
    Umat = ovl2[0:CC, 3072:3584]
    Wp = ovl2[0:CC, 3584:4096]
    ST = p.sb("ST", (64, FC))
    Ost = p.sb("Ost", (CC, 2 * 512), BF16)

    pb = [p.ps(f"pb{i}", (128, 512)) for i in range(8)]

    wm_d = p.nc.dram_tensor("wm_d", [34, TH], F32, kind="Internal").ap()
    sem = {k: p.sem(k) for k in ("ld", "wl", "ts", "vs", "ss", "st", "fr", "ac", "wm", "wb",
                                 "tg0", "tg1", "tg2", "tg3", "vg0", "vg1", "vg2", "vg3")}
    S = dict(xt=xt, onesf=onesf, sqt=sqt, t8a=t8a, t8b=t8b, pb=pb, sem=sem, epsc=epsc)

    def ph0_sync(e):
        e.dma_start(out=rwTs[:].rearrange("p (c n) -> p c n", c=8),
                    in_=rwT.rearrange("(c p) n -> p c n", c=8)).then_inc(sem["ld"], 16)
        e.dma_start(out=lngs[:], in_=ln_g).then_inc(sem["ld"], 16)
        e.dma_start(out=lnbs[:], in_=ln_b).then_inc(sem["ld"], 16)
        e.dma_start(out=bqs[:], in_=bqkv).then_inc(sem["ld"], 16)
        e.dma_start(out=msku[:], in_=msku_in).then_inc(sem["ld"], 16)
        e.dma_start(out=mskg[:], in_=mskg_in).then_inc(sem["ld"], 16)
        e.dma_start(out=eyes[:], in_=eye64).then_inc(sem["ld"], 16)
        e.dma_start(out=rbb[:], in_=rbias).then_inc(sem["ld"], 16)
    p.on("sync", ph0_sync)
    p.add("ld", 8 * 16)

    def ph0_vec(e):
        e.wait_ge(sem["ld"], 8 * 16)
        e.memset(ones1[:], 1.0)
        e.memset(onesc[:], 1.0)
        e.memset(onesf[:], 1.0)
        e.memset(epsc[:], EPS)
        e.memset(ST[:], 0.0)
        e.memset(wmT[32:34, :], 1.0)
        e.drain().then_inc(sem["vs"], 1)
    p.on("vector", ph0_vec)
    p.add("vs", 1)
    p.barrier()

    if stage < 1:
        nst = p.get("st"); nld = p.get("ld")
        def finz(e):
            e.wait_ge(sem["ld"], nld)
            e.wait_ge(sem["st"], nst)
        p.on("sync", finz)
        return p, dbg

    emit_stats(p, x_fm, T, S, CAt, CBt, None, None)
    emit_stats(p, x_own, TH, S, None, None, CCt, CDt)

    def sto_sync(e):
        e.dma_start(out=stO_out[0:1, :], in_=CCt[0:1, :]).then_inc(sem["st"], 16)
        e.dma_start(out=stO_out[1:2, :], in_=CDt[0:1, :]).then_inc(sem["st"], 16)
    p.on("sync", sto_sync)
    p.add("st", 32)

    if stage < 2:
        nst = p.get("st"); nwl = p.get("wl")
        def fin0(e):
            e.wait_ge(sem["st"], nst)
            e.wait_ge(sem["wl"], nwl)
        p.on("sync", fin0)
        return p, dbg

    tc0 = p.get("ts"); vc0 = p.get("vs")
    NB = TH // 128
    n_tgrp = (NB + 3) // 4

    def phc_tens(e):
        for ch in range(8):
            e.matmul(pb[0][:, 0:NE], onesf[:, :], rwTs[:, ch * NE:(ch + 1) * NE],
                     start=(ch == 0), stop=(ch == 7))
        e.drain().then_inc(sem["ts"], 1)
        for b in range(NB):
            e.transpose(pb[3][0:128, (b % 4) * 2:(b % 4) * 2 + 1],
                        CCt[0:1, b * 128:(b + 1) * 128], eyes[0:1, 0:1])
            e.transpose(pb[3][0:128, (b % 4) * 2 + 1:(b % 4) * 2 + 2],
                        CDt[0:1, b * 128:(b + 1) * 128], eyes[0:1, 0:1])
            if b % 4 == 3 or b == NB - 1:
                e.drain().then_inc(sem["ts"], 1)
                e.wait_ge(sem["vs"], vc0 + 2 + b // 4)
    p.on("tensor", phc_tens)
    p.add("ts", 1 + n_tgrp)

    def phc_vec(e):
        e.wait_ge(sem["ts"], tc0 + 1)
        e.tensor_copy(rsumb[:, :], pb[0][:, 0:NE])
        e.drain().then_inc(sem["vs"], 2)
        for g in range(n_tgrp):
            e.wait_ge(sem["ts"], tc0 + 2 + g)
            nb = min(4, NB - g * 4)
            e.tensor_copy(stoc[:, g * 8: g * 8 + nb * 2], pb[3][0:128, 0:nb * 2])
            e.drain().then_inc(sem["vs"], 1)
        e.tensor_scalar(stoc[:].rearrange("p (b k) -> p b k", k=2)[:, :, 1:2],
                        stoc[:].rearrange("p (b k) -> p b k", k=2)[:, :, 1:2],
                        -1.0, None, ALU.mult)
        e.drain().then_inc(sem["vs"], 1)
    p.on("vector", phc_vec)
    p.add("vs", 2 + n_tgrp + 1)
    p.barrier()

    wx0 = p.get("wl"); tx0 = p.get("ts"); vx0 = p.get("vs"); sx0 = p.get("ss")
    RV1 = 3
    RV2 = 30
    VT = 16 + 4 * (RV1 + RV2)

    def phx_gp(e):
        for tt in range(NTH):
            if tt > 0:
                e.wait_ge(sem["ts"], tx0 + 2 * tt)
            e.dma_start(out=xt[:].rearrange("p (c n) -> p c n", c=8),
                        in_=x_own.rearrange("(c p) (u n) -> u p c n", c=8, u=NTH)[tt]
                        ).then_inc(sem["wl"], 16)
    p.on("gpsimd", phx_gp)
    p.add("wl", 16 * NTH)

    def phx_tens(e):
        for tt in range(NTH):
            e.wait_ge(sem["wl"], wx0 + 16 * (tt + 1))
            for it in range(4):
                for ch in range(8):
                    e.matmul(pb[2][:, it * NE:(it + 1) * NE],
                             xt[:, ch * 512 + it * 128: ch * 512 + it * 128 + 128],
                             rwTs[:, ch * NE:(ch + 1) * NE],
                             start=(ch == 0), stop=(ch == 7))
            e.drain().then_inc(sem["ts"], 1)
            e.wait_ge(sem["vs"], vx0 + (tt + 1) * VT)
            e.drain().then_inc(sem["ts"], 1)
    p.on("tensor", phx_tens)
    p.add("ts", 2 * NTH)

    def phx_scal(e):
        for tt in range(NTH):
            for it in range(4):
                e.wait_ge(sem["vs"], vx0 + tt * VT + 16 + it * (RV1 + RV2) + RV1)
                e.activation(tok2[:, :], tokw[:, :], AF.Sigmoid).then_inc(sem["ss"], 1)
    p.on("scalar", phx_scal)
    p.add("ss", 4 * NTH)

    def phx_vec(e):
        for tt in range(NTH):
            e.wait_ge(sem["ts"], tx0 + 2 * tt + 1)
            slt = slice(tt * 512, (tt + 1) * 512)
            for ch in range(8):
                e.tensor_tensor(sqt[:, ch * 512:(ch + 1) * 512], xt[:, ch * 512:(ch + 1) * 512],
                                CCt[:, slt], ALU.mult)
                e.tensor_tensor(xnbf[:, ch * TH + tt * 512: ch * TH + (tt + 1) * 512],
                                sqt[:, ch * 512:(ch + 1) * 512], CDt[:, slt], ALU.subtract)
                e.drain().then_inc(sem["vs"], 2)
            def fz():
                e.drain(fusable=False)
            for it in range(4):
                tok0 = tt * 512 + it * 128
                blk = tok0 // 128
                ccol = stoc[:, blk * 2: blk * 2 + 1]
                cdn = stoc[:, blk * 2 + 1: blk * 2 + 2]
                e.tensor_scalar(tok2[:, :], pb[2][:, it * NE:(it + 1) * NE], ccol, None, ALU.mult)
                e.drain().then_inc(sem["vs"], 1)
                e.scalar_tensor_tensor(tokw[:, :], rsumb[:, :], cdn, tok2[:, :], ALU.mult, ALU.add)
                e.drain().then_inc(sem["vs"], 2)
                e.wait_ge(sem["ss"], sx0 + tt * 4 + it + 1)
                e.tensor_tensor(tokw[:, :], tok2[:, :], rbb[:, :], ALU.add)
                fz()
                e.tensor_reduce(tokg[:, 0:4], tokw[:, :].rearrange("p (g k) -> p g k", g=4),
                                AX.X, ALU.max)
                fz()
                for g in range(4):
                    e.tensor_scalar(sqt[:, g * 8:(g + 1) * 8], tokw[:, g * 8:(g + 1) * 8],
                                    tokg[:, g:g + 1], None, ALU.subtract)
                fz()
                e.tensor_scalar(sqt[:, 0:NE], sqt[:, 0:NE], 0.0, None, ALU.is_ge)
                fz()
                e.scalar_tensor_tensor(sqt[:, NE:2 * NE], sqt[:, 0:NE], -BIG, tokw[:, :],
                                       ALU.mult, ALU.add)
                fz()
                e.tensor_reduce(tokg[:, 4:8], sqt[:, NE:2 * NE].rearrange("p (g k) -> p g k", g=4),
                                AX.X, ALU.max)
                fz()
                e.tensor_tensor(tokg[:, 0:4], tokg[:, 0:4], tokg[:, 4:8], ALU.add)
                fz()
                e.tensor_reduce(tcol[:, 0:1], tokg[:, 0:4], AX.X, ALU.max)
                fz()
                e.tensor_scalar(tokg[:, 4:8], tokg[:, 0:4], tcol[:, 0:1], None, ALU.subtract)
                fz()
                e.tensor_scalar(tokg[:, 4:8], tokg[:, 4:8], 0.0, None, ALU.is_ge)
                fz()
                e.scalar_tensor_tensor(tokg[:, 4:8], tokg[:, 4:8], -BIG, tokg[:, 0:4],
                                       ALU.mult, ALU.add)
                fz()
                e.tensor_reduce(tcol[:, 1:2], tokg[:, 4:8], AX.X, ALU.max)
                fz()
                e.tensor_scalar(tokg[:, 0:4], tokg[:, 0:4], tcol[:, 1:2], None, ALU.subtract)
                fz()
                e.tensor_scalar(tokg[:, 0:4], tokg[:, 0:4], 0.0, None, ALU.is_ge)
                fz()
                for g in range(4):
                    e.tensor_scalar(wtok[:, g * 8:(g + 1) * 8], tokw[:, g * 8:(g + 1) * 8],
                                    tokg[:, g:g + 1], None, ALU.mult)
                e.drain().then_inc(sem["vs"], 16)
                e.tensor_copy(sqt[:, 0:NE], wtok[:, :])
                fz()
                for _ in range(3):
                    e.tensor_reduce(tcol[:, 2:3], sqt[:, 0:NE], AX.X, ALU.max)
                    fz()
                    e.tensor_scalar(sqt[:, NE:2 * NE], sqt[:, 0:NE], tcol[:, 2:3], None, ALU.subtract)
                    fz()
                    e.tensor_scalar(sqt[:, NE:2 * NE], sqt[:, NE:2 * NE], 0.0, None, ALU.is_ge)
                    fz()
                    e.scalar_tensor_tensor(sqt[:, 0:NE], sqt[:, NE:2 * NE], -BIG, sqt[:, 0:NE],
                                           ALU.mult, ALU.add)
                    fz()
                e.tensor_reduce(tcol[:, 2:3], sqt[:, 0:NE], AX.X, ALU.max)
                fz()
                e.tensor_scalar(sqt[:, 0:NE], wtok[:, :], tcol[:, 2:3], None, ALU.subtract)
                fz()
                e.tensor_scalar(sqt[:, 0:NE], sqt[:, 0:NE], 0.0, None, ALU.is_ge)
                fz()
                e.tensor_tensor(wtok[:, :], tok2[:, :], sqt[:, 0:NE], ALU.mult)
                fz()
                e.tensor_reduce(tcol[:, 3:4], wtok[:, :], AX.X, ALU.add)
                fz()
                e.tensor_scalar(tcol[:, 3:4], tcol[:, 3:4], 1e-20, None, ALU.add)
                fz()
                e.reciprocal(tcol[:, 3:4], tcol[:, 3:4])
                fz()
                e.tensor_scalar(wtok[:, :], wtok[:, :], tcol[:, 3:4], None, ALU.mult)
                fz()
                e.tensor_scalar(wtok[:, :], wtok[:, :], 2.5, None, ALU.mult)
                fz()
                for b4 in range(4):
                    e.transpose(wmT[0:32, tok0 + b4 * 32: tok0 + (b4 + 1) * 32],
                                wtok[b4 * 32:(b4 + 1) * 32, 0:32])
                e.drain().then_inc(sem["vs"], 14)
    p.on("vector", phx_vec)
    p.add("vs", NTH * VT)
    p.barrier()

    if stage < 3:
        if debug:
            def dbg2(e):
                e.dma_start(out=dbg["W"], in_=wmT[:]).then_inc(sem["st"], 16)
                e.dma_start(out=dbg["xn"].rearrange("(c p) n -> p c n", c=8),
                            in_=xnbf[:].rearrange("p (c n) -> p c n", c=8)).then_inc(sem["st"], 16)
            p.on("sync", dbg2)
            p.add("st", 32)
        if debug >= 2:
            dbg["stoc"] = p.dram_out("dbgstoc", (128, (TH // 128) * 2))
            dbg["rsumb"] = p.dram_out("dbgrsumb", (128, NE))
            dbg["tokw"] = p.dram_out("dbgtokw", (128, NE))
            dbg["tok2"] = p.dram_out("dbgtok2", (128, NE))
            dbg["wtok"] = p.dram_out("dbgwtok", (128, NE))
            dbg["tokg"] = p.dram_out("dbgtokg", (128, 8))
            dbg["tcol"] = p.dram_out("dbgtcol", (128, 4))
            dbg["sqt"] = p.dram_out("dbgsqt", (128, 2 * NE))
            def dbg3(e):
                for k in ("stoc", "rsumb", "tokw", "tok2", "wtok", "tokg", "tcol"):
                    src = {"stoc": stoc, "rsumb": rsumb, "tokw": tokw, "tok2": tok2,
                           "wtok": wtok, "tokg": tokg, "tcol": tcol}[k]
                    e.dma_start(out=dbg[k], in_=src[:]).then_inc(sem["st"], 16)
                e.dma_start(out=dbg["sqt"], in_=sqt[:, 0:2 * NE]).then_inc(sem["st"], 16)
            p.on("sync", dbg3)
            p.add("st", 8 * 16)
        nst = p.get("st"); nwl = p.get("wl")
        def fin1(e):
            e.wait_ge(sem["st"], nst)
            e.wait_ge(sem["wl"], nwl)
        p.on("sync", fin1)
        return p, dbg

    NTT = NTH
    wl0 = p.get("wl"); t0 = p.get("ts"); v0 = p.get("vs"); s0 = p.get("ss")
    ac0 = p.get("ac"); vroute = p.get("vs")

    def tbase(ee):
        return t0 + 2 * NTT * ee

    def sbase(ee):
        return s0 + NTT * ee

    def vbase(ee):
        return v0 + 2 * NTT * ee

    def moe_gp(e):
        e.dma_start(out=wm_d, in_=wmT[:]).then_inc(sem["wm"], 16)
        for ee in range(35):
            if ee >= 1:
                ep = ee - 1
                for tt in range(NTT):
                    e.wait_ge(sem["vs"], vbase(ep) + 2 * tt + 2)
                    e.dma_start(out=ffn_out[:, :, tt * 512:(tt + 1) * 512].rearrange("c p n -> p c n"),
                                in_=eacc[:].rearrange("p (c n) -> p c n", c=8),
                                accum_op=(ALU.bypass if ep == 0 else ALU.add)).then_inc(sem["ac"], 16)
            if ee < 34:
                if ee >= 2:
                    e.wait_ge(sem["fr"], 2 * (ee - 2) + 1)
                e.dma_start(out=gub[ee % 2][:].rearrange("p (c n) -> p c n", c=8),
                            in_=gu_w[ee].rearrange("(c p) n -> p c n", c=8)).then_inc(sem["wl"], 16)
                if ee >= 1:
                    e.wait_ge(sem["fr"], 2 * ee)
                e.dma_start(out=dnb[:].rearrange("p (c n) -> p c n", c=4),
                            in_=dn_w[ee].rearrange("(c p) n -> p c n", c=4)).then_inc(sem["wl"], 16)
    p.on("gpsimd", moe_gp)
    p.add("wl", 34 * 32)
    p.add("ac", 34 * NTT * 16)
    p.add("wm", 16)

    def moe_sync(e):
        e.wait_ge(sem["wm"], 16)
        for ee in range(NE):
            for tt in range(NTT):
                e.wait_ge(sem["vs"], max(0, vbase(ee) + 2 * tt - 1))
                e.dma_start(out=wrow[:, :],
                            in_=wm_d[ee:ee + 1, tt * 512:(tt + 1) * 512].broadcast_to([128, 512])
                            ).then_inc(sem["wb"], 16)
    p.on("sync", moe_sync)
    p.add("wb", NE * NTT * 16)

    def moe_tens(e):
        for ee in range(34):
            e.wait_ge(sem["wl"], wl0 + 32 * (ee + 1))
            tb = tbase(ee)
            for tt in range(NTT):
                e.wait_ge(sem["vs"], vbase(ee) + 2 * tt)
                for ic in range(4):
                    for ch in range(8):
                        e.matmul(pb[ic][:, :],
                                 gub[ee % 2][:, ch * 1024 + ic * 128: ch * 1024 + ic * 128 + 128],
                                 xnbf[:, ch * TH + tt * 512: ch * TH + (tt + 1) * 512],
                                 start=(ch == 0), stop=(ch == 7))
                for ic in range(4):
                    for ch in range(8):
                        mm = e.matmul(pb[4 + ic][:, :],
                                      gub[ee % 2][:, ch * 1024 + 512 + ic * 128: ch * 1024 + 512 + ic * 128 + 128],
                                      xnbf[:, ch * TH + tt * 512: ch * TH + (tt + 1) * 512],
                                      start=(ch == 0), stop=(ch == 7))
                mm.then_inc(sem["ts"], 1)
                if tt == NTT - 1:
                    e.drain().then_inc(sem["fr"], 1)
                e.wait_ge(sem["vs"], vbase(ee) + 2 * tt + 1)
                for dt in range(8):
                    for ic in range(4):
                        mm = e.matmul(pb[dt][:, :],
                                      dnb[:, ic * 1024 + dt * 128: ic * 1024 + dt * 128 + 128],
                                      hbuf[:, ic * 512:(ic + 1) * 512],
                                      start=(ic == 0), stop=(ic == 3))
                mm.then_inc(sem["ts"], 1)
            e.drain().then_inc(sem["fr"], 1)
    p.on("tensor", moe_tens)
    p.add("ts", 34 * NTT * 2)
    p.add("fr", 68)

    def moe_scal(e):
        for ee in range(34):
            tb = tbase(ee)
            for tt in range(NTT):
                e.wait_ge(sem["ts"], tb + 2 * tt + 1)
                for ic in range(4):
                    e.activation(sqt[:, 2048 + ic * 512: 2048 + (ic + 1) * 512], pb[ic][:, :], AF.Silu)
                e.drain().then_inc(sem["ss"], 1)
    p.on("scalar", moe_scal)
    p.add("ss", 34 * NTT)

    def moe_vec(e):
        for ee in range(34):
            tb = tbase(ee)
            for tt in range(NTT):
                e.wait_ge(sem["ts"], tb + 2 * tt + 1)
                e.wait_ge(sem["ss"], sbase(ee) + tt + 1)
                if ee < NE:
                    e.wait_ge(sem["wb"], 16 * (ee * NTT + tt + 1))
                for ic in range(4):
                    if ee < NE:
                        e.tensor_tensor(sqt[:, ic * 512:(ic + 1) * 512], pb[4 + ic][:, :],
                                        wrow[:, :], ALU.mult)
                    else:
                        e.tensor_copy(sqt[:, ic * 512:(ic + 1) * 512], pb[4 + ic][:, :])
                for ic in range(4):
                    e.tensor_tensor(hbuf[:, ic * 512:(ic + 1) * 512],
                                    sqt[:, 2048 + ic * 512: 2048 + (ic + 1) * 512],
                                    sqt[:, ic * 512:(ic + 1) * 512], ALU.mult)
                e.drain().then_inc(sem["vs"], 1)
                e.wait_ge(sem["ts"], tb + 2 * tt + 2)
                e.wait_ge(sem["ac"], ac0 + 16 * (ee * NTT + tt))
                for dt in range(8):
                    e.tensor_copy(eacc[:, dt * 512:(dt + 1) * 512], pb[dt][:, :])
                e.drain().then_inc(sem["vs"], 1)
    p.on("vector", moe_vec)
    p.add("vs", 34 * NTT * 2)

    if debug:
        nvs_moe = p.get("vs")
        def moe_dbg(e):
            e.wait_ge(sem["vs"], nvs_moe)
            e.dma_start(out=dbg["W"], in_=wmT[:]).then_inc(sem["st"], 16)
            e.dma_start(out=dbg["xn"].rearrange("(c p) n -> p c n", c=8),
                        in_=xnbf[:].rearrange("p (c n) -> p c n", c=8)).then_inc(sem["st"], 16)
        p.on("sync", moe_dbg)
        p.add("st", 32)
    p.barrier()

    if stage < 4:
        nst = p.get("st"); nwl = p.get("wl"); nac = p.get("ac")
        def fin2(e):
            e.wait_ge(sem["st"], nst)
            e.wait_ge(sem["wl"], nwl)
            e.wait_ge(sem["ac"], nac)
        p.on("sync", fin2)
        return p, dbg

    emit_deltanet(p, T, sem, pb, dict(
        x_fm=x_fm, CAt=CAt, CBt=CBt, lngs=lngs, lnbs=lnbs, bqs=bqs, gub=gub,
        wqkv=wqkv, eyes=eyes, msku=msku, mskg=mskg, xt=xt, xnl=xnl, qkvb=qkvb,
        sqt=sqt, ktm2=ktm2, vtm2=vtm2, btm2=btm2, ktmS=ktmS, vtmS=vtmS,
        btmS=btmS, Asp=Asp, Gm=Gm, Umat=Umat, Wp=Wp, ST=ST, Ost=Ost,
        onesf=onesf), O_out)

    nst = p.get("st"); nwl = p.get("wl"); nac = p.get("ac")
    def fin_sync(e):
        e.wait_ge(sem["st"], nst)
        e.wait_ge(sem["wl"], nwl)
        e.wait_ge(sem["ac"], nac)
    p.on("sync", fin_sync)
    return p, dbg


def emit_deltanet(p, T, sem, pb, B, O_out):
    NSTRIP = T // STRIP
    x_fm = B["x_fm"]
    CAt, CBt = B["CAt"], B["CBt"]
    lngs, lnbs, bqs = B["lngs"], B["lnbs"], B["bqs"]
    gub, wqkv = B["gub"], B["wqkv"]
    eyes, msku, mskg = B["eyes"], B["msku"], B["mskg"]
    xt, xnl, qkvb, sqt = B["xt"], B["xnl"], B["qkvb"], B["sqt"]
    ktm2, vtm2, btm2 = B["ktm2"], B["vtm2"], B["btm2"]
    ktmS, vtmS, btmS = B["ktmS"], B["vtmS"], B["btmS"]
    Asp, Gm = B["Asp"], B["Gm"]
    Umat, Wp, ST, Ost = B["Umat"], B["Wp"], B["ST"], B["Ost"]
    onesf = B["onesf"]

    NIT = CC - 1
    w0 = p.get("wl"); t0 = p.get("ts"); v0 = p.get("vs"); s0 = p.get("ss")

    def wq_ap(ch, j):
        return gub[ch // 4][:, (ch % 4) * 2048 + j * 64: (ch % 4) * 2048 + (j + 1) * 64]

    NG = 4
    GH = 8 // NG
    GW = GH * 64
    TG = 1 + NIT + 1 + 1
    VG = 1 + NIT + 1
    T_PRE = 4
    V_PRE = 4
    S_PRE = 6
    SPS = S_PRE + 8
    GPS = 16 * (1 + 24 + 8)
    tg = [sem[f"tg{i}"] for i in range(NG)]
    vg = [sem[f"vg{i}"] for i in range(NG)]
    tg0 = [p.get(f"tg{i}") for i in range(NG)]
    vg0 = [p.get(f"vg{i}") for i in range(NG)]

    def tw(s, k):
        return t0 + s * T_PRE + k

    def vw(s, k):
        return v0 + s * V_PRE + k

    def sw(s, k):
        return s0 + s * SPS + k

    def tgw(g, n, k):
        return tg0[g] + n * TG + k

    def vgw(g, n, k):
        return vg0[g] + n * VG + k

    def gp(e):
        e.wait_ge(sem["fr"], 68)
        for half in range(2):
            e.dma_start(out=gub[half][:].rearrange("p (c n) -> p c n", c=4),
                        in_=wqkv.rearrange("(u c p) n -> u p c n", u=2, c=4)[half]
                        ).then_inc(sem["wl"], 16)
        for s in range(NSTRIP):
            if s > 0:
                e.wait_ge(sem["vs"], vw(s - 1, 1))
            e.dma_start(out=xt[:, 0:8 * STRIP].rearrange("p (c n) -> p c n", c=8),
                        in_=x_fm.rearrange("(c p) n -> p c n", c=8)[:, :, s * STRIP:(s + 1) * STRIP]
                        ).then_inc(sem["wl"], 16)
            e.wait_ge(sem["ss"], sw(s, 5))
            for c in range(8):
                if c >= 2:
                    for g in range(NG):
                        e.wait_ge(vg[g], vgw(g, s * 8 + c - 1, 1))
                sl = slice((c % 2) * 512, (c % 2) * 512 + 512)
                e.dma_start(out=ktm2[:, sl], in_=ktmS[c * CC:(c + 1) * CC, :]).then_inc(sem["wl"], 16)
                e.dma_start(out=vtm2[:, sl], in_=vtmS[c * CC:(c + 1) * CC, :]).then_inc(sem["wl"], 16)
                e.dma_start(out=btm2[:, sl], in_=btmS[c * CC:(c + 1) * CC, :]).then_inc(sem["wl"], 16)
                if c >= 1:
                    cp = c - 1
                    e.wait_ge(sem["ss"], sw(s, S_PRE + cp + 1))
                    e.dma_start(out=O_out.rearrange("(u c p) f -> u c p f", u=NSTRIP, c=8)[s, cp],
                                in_=Ost[:, (cp % 2) * 512:(cp % 2) * 512 + 512]).then_inc(sem["wl"], 16)
            e.wait_ge(sem["ss"], sw(s, S_PRE + 8))
            e.dma_start(out=O_out.rearrange("(u c p) f -> u c p f", u=NSTRIP, c=8)[s, 7],
                        in_=Ost[:, 512:1024]).then_inc(sem["wl"], 16)
    p.on("gpsimd", gp)
    p.add("wl", 32 + NSTRIP * GPS)
    nwq = w0 + 32

    def tens(e):
        e.wait_ge(sem["wl"], nwq)
        for s in range(NSTRIP):
            e.wait_ge(sem["ss"], sw(s, 1))
            mm = None
            for j in range(32):
                bank, col = j // 4, (j % 4) * 128
                for ch in range(8):
                    mm = e.matmul(pb[bank][0:64, col:col + 128], wq_ap(ch, j),
                                  xnl[:, ch * STRIP:(ch + 1) * STRIP],
                                  start=(ch == 0), stop=(ch == 7))
            mm.then_inc(sem["ts"], 1)
            e.wait_ge(sem["ss"], sw(s, 3))
            for h in range(8):
                bank = pb[0] if h < 4 else pb[1]
                mm = e.matmul(bank[:, (h % 4) * 128:(h % 4 + 1) * 128],
                              onesf[0:64, :], sqt[0:64, h * 128:(h + 1) * 128],
                              start=True, stop=True)
            mm.then_inc(sem["ts"], 1)
            e.wait_ge(sem["vs"], vw(s, 3))
            for h in range(8):
                e.transpose(pb[4][0:128, h * 64:(h + 1) * 64], qkvb[:, (8 + h) * 128:(9 + h) * 128], eyes[:])
                e.transpose(pb[5][0:128, h * 64:(h + 1) * 64], qkvb[:, (16 + h) * 128:(17 + h) * 128], eyes[:])
                mm = e.transpose(pb[6][0:128, h * 64:(h + 1) * 64], qkvb[:, (24 + h) * 128:(25 + h) * 128], eyes[:])
            mm.then_inc(sem["ts"], 1)
            e.wait_ge(sem["ss"], sw(s, 5))
            for c in range(8):
                bank = pb[4] if c < 4 else pb[5]
                colb = (c % 4) * 128
                for h in range(8):
                    ksl = qkvb[:, (8 + h) * 128 + c * CC: (8 + h) * 128 + (c + 1) * CC]
                    e.matmul(bank[0:CC, colb + h * CC: colb + (h + 1) * CC], ksl, ksl,
                             start=True, stop=True)
            for c in range(8):
                bank = pb[6] if c < 4 else pb[7]
                colb = (c % 4) * 128
                for h in range(8):
                    ksl = qkvb[:, (8 + h) * 128 + c * CC: (8 + h) * 128 + (c + 1) * CC]
                    qsl = qkvb[:, h * 128 + c * CC: h * 128 + (c + 1) * CC]
                    mm = e.matmul(bank[0:CC, colb + h * CC: colb + (h + 1) * CC], ksl, qsl,
                                  start=True, stop=True)
            mm.then_inc(sem["ts"], 1)
            for c in range(8):
                n = s * 8 + c
                for g in range(NG):
                    e.wait_ge(vg[g], vgw(g, n, 0))
                    if c == 0:
                        e.wait_ge(sem["vs"], vw(s, 4))
                    e.wait_ge(sem["ss"], sw(s, S_PRE + c))
                    for h in range(g * GH, (g + 1) * GH):
                        mm = e.matmul(pb[2 * g][0:CC, h * 64:(h + 1) * 64],
                                      qkvb[:, (8 + h) * 128 + c * CC: (8 + h) * 128 + (c + 1) * CC],
                                      ST[:, h * 64:(h + 1) * 64], start=True, stop=True)
                    mm.then_inc(tg[g], 1)
                for it in range(NIT):
                    for g in range(NG):
                        bank = pb[2 * g + 1 - (it % 2)]
                        e.wait_ge(vg[g], vgw(g, n, 1 + it))
                        for h in range(g * GH, (g + 1) * GH):
                            mm = e.matmul(bank[0:CC, h * 64:(h + 1) * 64],
                                          Asp[:, (c * 8 + h) * CC: (c * 8 + h + 1) * CC],
                                          Umat[:, h * 64:(h + 1) * 64], start=True, stop=True)
                        mm.then_inc(tg[g], 1)
                for g in range(NG):
                    e.wait_ge(vg[g], vgw(g, n, 1 + NIT))
                    for h in range(g * GH, (g + 1) * GH):
                        e.matmul(pb[2 * g][0:CC, h * 64:(h + 1) * 64],
                                 qkvb[:, h * 128 + c * CC: h * 128 + (c + 1) * CC],
                                 ST[:, h * 64:(h + 1) * 64], start=True, stop=False)
                        mm = e.matmul(pb[2 * g][0:CC, h * 64:(h + 1) * 64],
                                      Gm[:, (c * 8 + h) * CC: (c * 8 + h + 1) * CC],
                                      Umat[:, h * 64:(h + 1) * 64], start=False, stop=True)
                    mm.then_inc(tg[g], 1)
                for g in range(NG):
                    for h in range(g * GH, (g + 1) * GH):
                        mm = e.matmul(pb[2 * g + 1][0:64, h * 64:(h + 1) * 64],
                                      ktm2[:, (c % 2) * 512 + h * 64: (c % 2) * 512 + (h + 1) * 64],
                                      Umat[:, h * 64:(h + 1) * 64], start=True, stop=True)
                    mm.then_inc(tg[g], 1)
    p.on("tensor", tens)
    p.add("ts", NSTRIP * T_PRE)
    for i in range(NG):
        p.add(f"tg{i}", NSTRIP * 8 * TG)

    def scal(e):
        for s in range(NSTRIP):
            e.wait_ge(sem["vs"], vw(s, 1))
            for ch in range(8):
                e.activation(xnl[:, ch * STRIP:(ch + 1) * STRIP],
                             sqt[:, 1024 + ch * STRIP: 1024 + (ch + 1) * STRIP],
                             AF.Identity, bias=lnbs[:, ch:ch + 1], scale=lngs[:, ch:ch + 1])
            e.drain().then_inc(sem["ss"], 1)
            e.wait_ge(sem["ts"], tw(s, 1))
            for j in range(32):
                bank, col = j // 4, (j % 4) * 128
                func = AF.Sigmoid if j >= 24 else AF.Identity
                e.activation(qkvb[:, j * 128:(j + 1) * 128], pb[bank][0:64, col:col + 128],
                             func, bias=bqs[0:64, j:j + 1])
            e.drain().then_inc(sem["ss"], 1)
            e.activation(sqt[0:64, 0:512], qkvb[0:64, 8 * 128:12 * 128], AF.Square)
            e.activation(sqt[0:64, 512:1024], qkvb[0:64, 12 * 128:16 * 128], AF.Square)
            e.drain().then_inc(sem["ss"], 1)
            e.wait_ge(sem["ts"], tw(s, 2))
            e.activation(sqt[:, 2048:2560], pb[0][:, :], AF.Sqrt)
            e.activation(sqt[:, 2560:3072], pb[1][:, :], AF.Sqrt)
            e.drain().then_inc(sem["ss"], 1)
            e.wait_ge(sem["ts"], tw(s, 3))
            e.copy(ktmS[:, :], pb[4][0:128, 0:512])
            e.copy(vtmS[:, :], pb[5][0:128, 0:512])
            e.copy(btmS[:, :], pb[6][0:128, 0:512])
            e.drain().then_inc(sem["ss"], 1)
            e.wait_ge(sem["ts"], tw(s, 4))
            e.copy(Asp[:, 0:512], pb[4][0:CC, 0:512])
            e.copy(Asp[:, 512:1024], pb[5][0:CC, 0:512])
            e.copy(Gm[:, 0:512], pb[6][0:CC, 0:512])
            e.copy(Gm[:, 512:1024], pb[7][0:CC, 0:512])
            e.drain().then_inc(sem["ss"], 1)
            for c in range(8):
                for g in range(NG):
                    e.wait_ge(tg[g], tgw(g, s * 8 + c, 1 + NIT + 1))
                if s * 8 + c >= 2:
                    prev = s * 8 + c - 2
                    ps_, pc_ = prev // 8, prev % 8
                    e.wait_ge(sem["wl"], w0 + 32 + ps_ * GPS + 16 * (4 * pc_ + 9))
                off = (c % 2) * 512
                for g in range(NG):
                    cp = e.copy(Ost[:, off + g * GW:off + (g + 1) * GW],
                                pb[2 * g][0:CC, g * GW:(g + 1) * GW])
                cp.then_inc(sem["ss"], 1)
    p.on("scalar", scal)
    p.add("ss", NSTRIP * SPS)

    def vec(e):
        for s in range(NSTRIP):
            e.wait_ge(sem["wl"], w0 + 32 + s * GPS + 16)
            ssl = slice(s * STRIP, (s + 1) * STRIP)
            for ch in range(8):
                e.tensor_tensor(sqt[:, ch * STRIP:(ch + 1) * STRIP],
                                xt[:, ch * STRIP:(ch + 1) * STRIP], CAt[:, ssl], ALU.mult)
                e.tensor_tensor(sqt[:, 1024 + ch * STRIP: 1024 + (ch + 1) * STRIP],
                                sqt[:, ch * STRIP:(ch + 1) * STRIP], CBt[:, ssl], ALU.subtract)
            e.drain().then_inc(sem["vs"], 1)
            e.wait_ge(sem["ss"], sw(s, 4))
            e.tensor_scalar(sqt[:, 2048:3072], sqt[:, 2048:3072], 1e-6, None, ALU.max)
            e.reciprocal(sqt[:, 2048:3072], sqt[:, 2048:3072])
            e.drain().then_inc(sem["vs"], 1)
            e.tensor_tensor(qkvb[:, 8 * 128:16 * 128], qkvb[:, 8 * 128:16 * 128],
                            sqt[0:64, 2048:3072], ALU.mult)
            e.drain().then_inc(sem["vs"], 1)
            e.wait_ge(sem["ss"], sw(s, 5 + 1))
            e.tensor_tensor(Asp[:, :], Asp[:, :], msku[:, :], ALU.mult)
            e.tensor_tensor(Gm[:, :], Gm[:, :], mskg[:, :], ALU.mult)
            e.drain().then_inc(sem["vs"], 1)
            for c in range(8):
                n = s * 8 + c
                e.wait_ge(sem["wl"], w0 + 32 + s * GPS + 16 * (4 * c + 4))
                sl0 = (c % 2) * 512
                for g in range(NG):
                    gs = slice(g * GW, (g + 1) * GW)
                    gsl = slice(sl0 + g * GW, sl0 + (g + 1) * GW)
                    e.wait_ge(tg[g], tgw(g, n, 1))
                    e.tensor_tensor(Wp[:, gs], vtm2[:, gsl], pb[2 * g][0:CC, gs], ALU.subtract)
                    e.tensor_tensor(Wp[:, gs], btm2[:, gsl], Wp[:, gs], ALU.mult)
                    e.tensor_copy(Umat[:, gs], Wp[:, gs]).then_inc(vg[g], 1)
                for it in range(NIT):
                    for g in range(NG):
                        bank = pb[2 * g + 1 - (it % 2)]
                        gs = slice(g * GW, (g + 1) * GW)
                        gsl = slice(sl0 + g * GW, sl0 + (g + 1) * GW)
                        e.wait_ge(tg[g], tgw(g, n, 1 + it + 1))
                        e.tensor_tensor(sqt[0:CC, 1536 + gs.start:1536 + gs.stop],
                                        btm2[:, gsl], bank[0:CC, gs], ALU.mult)
                        e.tensor_tensor(Umat[:, gs], Wp[:, gs],
                                        sqt[0:CC, 1536 + gs.start:1536 + gs.stop],
                                        ALU.subtract).then_inc(vg[g], 1)
                for g in range(NG):
                    gs = slice(g * GW, (g + 1) * GW)
                    e.wait_ge(tg[g], tgw(g, n, TG))
                    e.tensor_tensor(ST[:, gs], ST[:, gs], pb[2 * g + 1][0:64, gs], ALU.add).then_inc(vg[g], 1)
    p.on("vector", vec)
    p.add("vs", NSTRIP * V_PRE)
    for i in range(NG):
        p.add(f"vg{i}", NSTRIP * 8 * VG)



H1 = 4096
NH1C = H1 // 128


def build_B(T, debug=0):
    NT = T // 512
    p = Prog()

    x_fm = p.dram_in("x_fm", (D, T))
    OT_in = p.dram_in("OT_in", (D, T), BF16)
    ffn0_in = p.dram_in("ffn0_in", (D, T))
    stB = p.dram_in("stB", (2, T))
    wo0T = p.dram_in("wo0T", (D, D))
    b_out0c = p.dram_in("b_out0c", (128, 8))
    WinT = p.dram_in("WinT", (D, D))
    WigT = p.dram_in("WigT", (D, D))
    WrgT = p.dram_in("WrgT", (D, D))
    WoutT = p.dram_in("WoutT", (D, D))
    bigc = p.dram_in("bigc", (128, 8))
    brgc = p.dram_in("brgc", (128, 8))
    lamc = p.dram_in("lamc", (128, 8))
    h0c = p.dram_in("h0c", (128, 8))
    nwc = p.dram_in("nwc", (128, 8))
    g1c = p.dram_in("g1c", (128, 8))
    b1c = p.dram_in("b1c", (128, 8))
    bb1c = p.dram_in("bb1c", (128, 32))
    g2c = p.dram_in("g2c", (128, 32))
    b2c = p.dram_in("b2c", (128, 32))
    bb2c = p.dram_in("bb2c", (128, 8))
    w1T = p.dram_in("w1T", (D, H1))
    w2T = p.dram_in("w2T", (H1, D))
    eye128 = p.dram_in("eye128", (128, 128))

    xout = p.dram_out("xout", (D, T))

    x1d = p.nc.dram_tensor("x1d", [D, T], F32, kind="Internal").ap()
    xn1d = p.nc.dram_tensor("xn1d", [D, T], BF16, kind="Internal").ap()
    qw1d = p.nc.dram_tensor("qw1d", [D, H1], BF16, kind="Internal").ap()
    qw2d = p.nc.dram_tensor("qw2d", [H1, D], BF16, kind="Internal").ap()
    rowd = p.nc.dram_tensor("rowd", [4, 512], F32, kind="Internal").ap()

    ones1 = p.sb("ones1", (1, 128))
    onesf = p.sb("onesf", (128, 128))
    onesb = p.sb("onesb", (128, 128), BF16)
    epsc = p.sb("epsc", (128, 1))
    eyes = p.sb("eyes", (128, 128))
    f32a = p.sb("f32a", (128, 16384))
    bf16a = p.sb("bf16a", (128, 32768), BF16)
    CAt = p.sb("CAt", (128, T), BF16)
    CBt = p.sb("CBt", (128, T), BF16)
    xnt = p.sb("xnt", (128, 4096), BF16)
    qt = p.sb("qt", (128, 4096), BF16)
    wdb = p.sb("wdb", (128, 8192), BF16)
    t8a = p.sb("t8a", (128, 512))
    t8b = p.sb("t8b", (128, 512))
    t8c = p.sb("t8c", (128, 512))
    t8d = p.sb("t8d", (128, 512))
    murb = p.sb("murb", (128, 512))
    rsrb = p.sb("rsrb", (128, 512))
    amr = p.sb("amr", (128, 512))
    am2r = p.sb("am2r", (128, 512))
    stbsA = p.sb("stbsA", (1, 512))
    stbsB = p.sb("stbsB", (1, 512))
    wcol = p.sb("wcol", (128, 32))
    swc = p.sb("swc", (128, 8))
    logac = p.sb("logac", (128, 8))
    pb = [p.ps(f"pb{i}", (128, 512)) for i in range(8)]
    sem = {k: p.sem(k) for k in ("ld", "wl", "ts", "vs", "ss", "st")}

    fA = lambda i: f32a[:, i * 4096:(i + 1) * 4096]

    def fz(e):
        e.drain(fusable=False)

    def h0c_col(c):
        return CB_h0[:, c:c + 1]

    def ph0_sync(e):
        for dst, src in ((b_out0c, None),):
            pass
        for nm, dr, sb_ in (("b_out0c", b_out0c, None),):
            pass
        for dr, w in ((b_out0c, 8), (bigc, 8), (brgc, 8), (lamc, 8), (h0c, 8),
                      (nwc, 8), (g1c, 8), (b1c, 8), (bb1c, 32), (g2c, 32),
                      (b2c, 32), (bb2c, 8)):
            pass
    colsB = p.sb("colsB", (128, 160))
    CB_b_out0 = colsB[:, 0:8]
    CB_big = colsB[:, 8:16]
    CB_brg = colsB[:, 16:24]
    CB_h0 = colsB[:, 24:32]
    CB_nw = colsB[:, 32:40]
    CB_g1 = colsB[:, 40:48]
    CB_b1 = colsB[:, 48:56]
    CB_bb1 = colsB[:, 56:88]
    CB_g2 = colsB[:, 88:120]
    CB_b2 = colsB[:, 120:152]
    CB_bb2 = colsB[:, 152:160]

    def ph0_sync2(e):
        for dst, src in ((CB_b_out0, b_out0c), (CB_big, bigc), (CB_brg, brgc),
                         (CB_h0, h0c), (CB_nw, nwc), (CB_g1, g1c), (CB_b1, b1c),
                         (CB_bb1, bb1c), (CB_g2, g2c), (CB_b2, b2c), (CB_bb2, bb2c)):
            e.dma_start(out=dst, in_=src).then_inc(sem["ld"], 16)
        e.dma_start(out=eyes[:], in_=eye128).then_inc(sem["ld"], 16)
        e.dma_start(out=swc[:, 0:8], in_=lamc).then_inc(sem["ld"], 16)
    p.on("sync", ph0_sync2)
    p.add("ld", 13 * 16)

    def ph0_gp(e):
        e.dma_start(out=wdb[:].rearrange("p (k n) -> p k n", k=8),
                    in_=wo0T.rearrange("(k p) n -> p k n", k=8)).then_inc(sem["wl"], 16)
    p.on("gpsimd", ph0_gp)
    p.add("wl", 16)

    def ph0_vec(e):
        e.memset(ones1[:], 1.0)
        e.memset(onesf[:], 1.0)
        e.memset(onesb[:], 1.0)
        e.memset(epsc[:], EPS)
        e.drain().then_inc(sem["vs"], 1)
        e.wait_ge(sem["ss"], 1)
        e.tensor_scalar(logac[:, :], logac[:, :], 1e-8, None, ALU.add)
        e.drain().then_inc(sem["vs"], 1)
    p.on("vector", ph0_vec)
    p.add("vs", 2)

    def ph0_scal(e):
        e.wait_ge(sem["ld"], 13 * 16)
        e.activation(logac[:, :], swc[:, 0:8], AF.Sigmoid)
        e.drain().then_inc(sem["ss"], 1)
        e.wait_ge(sem["vs"], 2)
        e.activation(logac[:, :], logac[:, :], AF.Ln)
        e.drain().then_inc(sem["ss"], 1)
    p.on("scalar", ph0_scal)
    p.add("ss", 2)

    def ph0_fin(e):
        e.wait_ge(sem["wl"], 16)
        e.wait_ge(sem["ss"], 2)
    p.on("sync", ph0_fin)
    p.barrier()

    xr = x_fm.rearrange("(c p) (u n) -> u p c n", c=8, u=NT)
    fr = ffn0_in.rearrange("(c p) (u n) -> u p c n", c=8, u=NT)
    Or = OT_in.rearrange("(c p) (u n) -> u p c n", c=8, u=NT)
    x1r = x1d.rearrange("(c p) (u n) -> u c p n", c=8, u=NT)
    xn1r = xn1d.rearrange("(c p) (u n) -> u c p n", c=8, u=NT)

    for tt in range(NT):
        l0 = p.get("ld"); t0 = p.get("ts"); v0 = p.get("vs"); s0 = p.get("ss")

        def p1_sync(e, tt=tt, l0=l0, v0=v0):
            e.dma_start(out=stbsA[:], in_=stB[0:1, tt * 512:(tt + 1) * 512]).then_inc(sem["ld"], 16)
            e.dma_start(out=stbsB[:], in_=stB[1:2, tt * 512:(tt + 1) * 512]).then_inc(sem["ld"], 16)
            e.dma_start(out=fA(0).rearrange("p (c n) -> p c n", c=8), in_=xr[tt]).then_inc(sem["ld"], 16)
            e.dma_start(out=fA(1).rearrange("p (c n) -> p c n", c=8), in_=fr[tt]).then_inc(sem["ld"], 16)
            e.dma_start(out=xnt[:].rearrange("p (c n) -> p c n", c=8), in_=Or[tt]).then_inc(sem["ld"], 16)
            for c in range(8):
                e.wait_ge(sem["vs"], v0 + c + 1)
                e.dma_start(out=x1r[tt, c], in_=fA(0)[:, c * 512:(c + 1) * 512]).then_inc(sem["st"], 16)
        p.on("sync", p1_sync)
        p.add("ld", 80)
        p.add("st", 8 * 16)

        def p1_tens(e, l0=l0, s0=s0):
            e.wait_ge(sem["ld"], l0 + 80)
            bc(e, pb[6][:, :], ones1, stbsA[0:1, :])
            bc(e, pb[7][:, :], ones1, stbsB[0:1, :])
            for c in range(4):
                for k in range(8):
                    e.matmul(pb[c][:, :], wdb[:, k * 1024 + c * 128: k * 1024 + c * 128 + 128],
                             xnt[:, k * 512:(k + 1) * 512], start=(k == 0), stop=(k == 7))
            e.drain().then_inc(sem["ts"], 1)
            e.wait_ge(sem["ss"], s0 + 4)
            for c in range(4):
                for k in range(8):
                    e.matmul(pb[c][:, :], wdb[:, k * 1024 + (c + 4) * 128: k * 1024 + (c + 4) * 128 + 128],
                             xnt[:, k * 512:(k + 1) * 512], start=(k == 0), stop=(k == 7))
            e.drain().then_inc(sem["ts"], 1)
        p.on("tensor", p1_tens)
        p.add("ts", 2)

        def p1_scal(e, t0=t0, v0=v0):
            for c in range(8):
                e.wait_ge(sem["ts"], t0 + (1 if c < 4 else 2))
                if c > 0:
                    e.wait_ge(sem["vs"], v0 + c)
                e.activation(t8a[:, :], pb[c % 4][:, :], AF.Identity,
                             bias=CB_b_out0[:, c:c + 1])
                e.drain().then_inc(sem["ss"], 1)
        p.on("scalar", p1_scal)
        p.add("ss", 8)

        def p1_vec(e, t0=t0, s0=s0):
            e.wait_ge(sem["ts"], t0 + 1)
            for c in range(8):
                xc = fA(0)[:, c * 512:(c + 1) * 512]
                fc = fA(1)[:, c * 512:(c + 1) * 512]
                e.wait_ge(sem["ss"], s0 + c + 1)
                e.tensor_tensor(t8b[:, :], xc, pb[6][:, :], ALU.mult)
                e.tensor_tensor(t8b[:, :], t8b[:, :], pb[7][:, :], ALU.subtract)
                e.tensor_tensor(t8b[:, :], t8b[:, :], xc, ALU.add)
                e.tensor_tensor(t8b[:, :], t8b[:, :], fc, ALU.add)
                e.tensor_tensor(xc, t8b[:, :], t8a[:, :], ALU.add)
                e.drain().then_inc(sem["vs"], 1)
        p.on("vector", p1_vec)
        p.add("vs", 8)
        p.barrier()

    nst1 = p.get("st")
    def p1_fin(e, nst1=nst1):
        e.wait_ge(sem["st"], nst1)
    p.on("sync", p1_fin)
    p.barrier()

    CCv = f32a[:, 8192:12288]
    CDv = f32a[:, 12288:16384]
    S2 = dict(xt=fA(0), onesf=onesf, sqt=fA(1), t8a=t8a, t8b=t8b, pb=pb,
              sem=sem, epsc=epsc)
    emit_stats(p, x1d, T, S2, CAt, CBt, CCv, CDv)

    for tt in range(NT):
        l0 = p.get("ld"); v0 = p.get("vs")

        def p3_sync(e, tt=tt, v0=v0):
            e.dma_start(out=fA(0).rearrange("p (c n) -> p c n", c=8),
                        in_=x1r[tt].rearrange("c p n -> p c n")).then_inc(sem["ld"], 16)
            e.wait_ge(sem["vs"], v0 + 1)
            e.dma_start(out=xn1r[tt].rearrange("c p n -> p c n"), in_=xnt[:].rearrange("p (c n) -> p c n", c=8)).then_inc(sem["st"], 16)
        p.on("sync", p3_sync)
        p.add("ld", 16)
        p.add("st", 16)

        def p3_vec(e, tt=tt, l0=l0):
            e.wait_ge(sem["ld"], l0 + 16)
            sl = slice(tt * 512, (tt + 1) * 512)
            for c in range(8):
                xc = fA(0)[:, c * 512:(c + 1) * 512]
                e.tensor_tensor(t8a[:, :], xc, CCv[:, sl], ALU.mult)
                e.tensor_tensor(xnt[:, c * 512:(c + 1) * 512], t8a[:, :], CDv[:, sl], ALU.subtract)
            e.drain().then_inc(sem["vs"], 1)
        p.on("vector", p3_vec)
        p.add("vs", 1)
        p.barrier()

    nst3 = p.get("st")
    def p3_fin(e, nst3=nst3):
        e.wait_ge(sem["st"], nst3)
    p.on("sync", p3_fin)
    p.barrier()
    areg, greg, hsreg, rmsacc = fA(0), fA(1), fA(2), fA(3)

    for c in range(8):
        l0 = p.get("ld"); w0 = p.get("wl"); t0 = p.get("ts")
        v0 = p.get("vs"); s0 = p.get("ss")

        def p4_gp(e, c=c):
            for i, W in enumerate((WinT, WigT, WrgT)):
                e.dma_start(out=wdb[:, i * 1024:(i + 1) * 1024].rearrange("p (k n) -> p k n", k=8),
                            in_=W.rearrange("(k p) n -> p k n", k=8)[:, :, c * 128:(c + 1) * 128]
                            ).then_inc(sem["wl"], 16)
        p.on("gpsimd", p4_gp)
        p.add("wl", 48)

        def p4_sync(e, t0=t0):
            for tt in range(NT):
                if tt > 0:
                    e.wait_ge(sem["ts"], t0 + tt)
                e.dma_start(out=xnt[:].rearrange("p (c n) -> p c n", c=8),
                            in_=xn1r[tt].rearrange("c p n -> p c n")).then_inc(sem["ld"], 16)
        p.on("sync", p4_sync)
        p.add("ld", 16 * NT)

        def p4_tens(e, l0=l0, w0=w0, s0=s0, v0=v0):
            e.wait_ge(sem["wl"], w0 + 48)
            for tt in range(NT):
                e.wait_ge(sem["ld"], l0 + 16 * (tt + 1))
                if tt > 0:
                    e.wait_ge(sem["ss"], s0 + 5 * tt - 2)
                for i in range(3):
                    for k in range(8):
                        e.matmul(pb[i][:, :], wdb[:, i * 1024 + k * 128:i * 1024 + (k + 1) * 128],
                                 xnt[:, k * 512:(k + 1) * 512], start=(k == 0), stop=(k == 7))
                e.drain().then_inc(sem["ts"], 1)
            for tt in range(NT):
                e.wait_ge(sem["ss"], s0 + 5 * NT + 1 + tt)
                if tt > 0:
                    e.wait_ge(sem["vs"], v0 + 3 * NT + 1 + tt)
                e.matmul(pb[5][:, :], onesf[:, :], t8a[:, :], start=True, stop=True)
                e.drain().then_inc(sem["ts"], 1)
        p.on("tensor", p4_tens)
        p.add("ts", 2 * NT)

        def p4_scal(e, c=c, t0=t0, v0=v0):
            for tt in range(NT):
                e.wait_ge(sem["ts"], t0 + tt + 1)
                if tt > 0:
                    e.wait_ge(sem["vs"], v0 + 3 * tt)
                e.activation(t8a[:, :], pb[0][:, :], AF.Copy)
                e.drain().then_inc(sem["ss"], 1)
                e.activation(t8b[:, :], pb[1][:, :], AF.Sigmoid, bias=CB_big[:, c:c + 1])
                e.drain().then_inc(sem["ss"], 1)
                e.activation(t8c[:, :], pb[2][:, :], AF.Sigmoid, bias=CB_brg[:, c:c + 1])
                e.drain().then_inc(sem["ss"], 1)
                e.wait_ge(sem["vs"], v0 + 3 * tt + 1)
                e.activation(areg[:, tt * 512:(tt + 1) * 512], areg[:, tt * 512:(tt + 1) * 512],
                             AF.Exp, scale=8.0)
                e.drain().then_inc(sem["ss"], 1)
                e.wait_ge(sem["vs"], v0 + 3 * tt + 2)
                e.activation(t8c[:, :], t8d[:, :], AF.Sqrt)
                e.drain().then_inc(sem["ss"], 1)
            e.wait_ge(sem["vs"], v0 + 3 * NT + 1)
            for tt in range(NT):
                if tt > 0:
                    e.wait_ge(sem["ts"], t0 + NT + tt)
                e.activation(t8a[:, :], hsreg[:, tt * 512:(tt + 1) * 512], AF.Square)
                e.drain().then_inc(sem["ss"], 1)
        p.on("scalar", p4_scal)
        p.add("ss", 6 * NT)

        def p4_vec(e, c=c, t0=t0, s0=s0):
            for tt in range(NT):
                e.wait_ge(sem["ss"], s0 + 5 * tt + 3)
                e.tensor_tensor(greg[:, tt * 512:(tt + 1) * 512], t8b[:, :], t8a[:, :], ALU.mult)
                e.tensor_scalar(areg[:, tt * 512:(tt + 1) * 512], t8c[:, :],
                                logac[:, c:c + 1], None, ALU.mult)
                e.drain().then_inc(sem["vs"], 1)
                e.wait_ge(sem["ss"], s0 + 5 * tt + 4)
                e.tensor_tensor(t8d[:, :], areg[:, tt * 512:(tt + 1) * 512],
                                areg[:, tt * 512:(tt + 1) * 512], ALU.mult)
                e.tensor_scalar(t8d[:, :], t8d[:, :], -1.0, float(1.0 + 1e-6), ALU.mult, ALU.add)
                e.drain().then_inc(sem["vs"], 1)
                e.wait_ge(sem["ss"], s0 + 5 * tt + 5)
                e.tensor_tensor(greg[:, tt * 512:(tt + 1) * 512],
                                greg[:, tt * 512:(tt + 1) * 512], t8c[:, :], ALU.mult)
                e.drain().then_inc(sem["vs"], 1)
            for tt in range(NT):
                init = h0c_col(c) if tt == 0 else hsreg[:, tt * 512 - 1:tt * 512]
                e.tensor_tensor_scan(hsreg[:, tt * 512:(tt + 1) * 512],
                                     areg[:, tt * 512:(tt + 1) * 512],
                                     greg[:, tt * 512:(tt + 1) * 512],
                                     init, ALU.mult, ALU.add)
                fz(e)
            e.tensor_copy(bf16a[:, c * T:(c + 1) * T], hsreg[:, 0:T])
            e.drain().then_inc(sem["vs"], 1)
            for tt in range(NT):
                e.wait_ge(sem["ts"], t0 + NT + tt + 1)
                if c == 0:
                    e.tensor_copy(rmsacc[:, tt * 512:(tt + 1) * 512], pb[5][:, :])
                else:
                    e.tensor_tensor(rmsacc[:, tt * 512:(tt + 1) * 512],
                                    rmsacc[:, tt * 512:(tt + 1) * 512], pb[5][:, :], ALU.add)
                e.drain().then_inc(sem["vs"], 1)
        p.on("vector", p4_vec)
        p.add("vs", 3 * NT + 1 + NT)
        p.barrier()

    v0 = p.get("vs"); s0 = p.get("ss")

    def p5_vec(e, v0=v0, s0=s0):
        for tt in range(NT):
            e.tensor_scalar(t8a[:, :], rmsacc[:, tt * 512:(tt + 1) * 512],
                            float(1.0 / D), 1e-6, ALU.mult, ALU.add)
            e.drain().then_inc(sem["vs"], 1)
            e.wait_ge(sem["ss"], s0 + tt + 1)
            e.reciprocal(rmsacc[:, tt * 512:(tt + 1) * 512], t8b[:, :])
            e.drain().then_inc(sem["vs"], 1)
    p.on("vector", p5_vec)
    p.add("vs", 2 * NT)

    def p5_scal(e, v0=v0):
        for tt in range(NT):
            e.wait_ge(sem["vs"], v0 + 2 * tt + 1)
            e.activation(t8b[:, :], t8a[:, :], AF.Sqrt)
            e.drain().then_inc(sem["ss"], 1)
    p.on("scalar", p5_scal)
    p.add("ss", NT)
    p.barrier()

    xor_ = xout.rearrange("(c p) (u n) -> u c p n", c=8, u=NT)
    for tt in range(NT):
        l0 = p.get("ld"); w0 = p.get("wl"); t0 = p.get("ts")
        v0 = p.get("vs"); s0 = p.get("ss")

        def p6_vec(e, tt=tt, s0=s0, l0=l0, st0=p.get("st")):
            for c in range(8):
                e.tensor_tensor(t8a[:, :], bf16a[:, c * T + tt * 512: c * T + (tt + 1) * 512],
                                rmsacc[:, tt * 512:(tt + 1) * 512], ALU.mult)
                e.tensor_scalar(xnt[:, c * 512:(c + 1) * 512], t8a[:, :],
                                CB_nw[:, c:c + 1], None, ALU.mult)
            e.drain().then_inc(sem["vs"], 1)
            for oc in range(8):
                e.wait_ge(sem["ss"], s0 + oc + 1)
                e.wait_ge(sem["ld"], l0 + 16 * (oc + 1))
                if oc > 0:
                    e.wait_ge(sem["st"], st0 + 16 * oc)
                e.tensor_tensor(t8c[:, :], t8b[:, :], fA(0)[:, oc * 512:(oc + 1) * 512], ALU.add)
                e.drain().then_inc(sem["vs"], 1)
        p.on("vector", p6_vec)
        p.add("vs", 9)

        def p6_gp(e, w0=w0, t0=t0):
            for oc in range(8):
                if oc >= 2:
                    e.wait_ge(sem["ts"], t0 + oc - 1)
                e.dma_start(out=wdb[:, (oc % 2) * 1024:(oc % 2) * 1024 + 1024].rearrange("p (k n) -> p k n", k=8),
                            in_=WoutT.rearrange("(k p) n -> p k n", k=8)[:, :, oc * 128:(oc + 1) * 128]
                            ).then_inc(sem["wl"], 16)
        p.on("gpsimd", p6_gp)
        p.add("wl", 8 * 16)

        def p6_sync(e, tt=tt, v0=v0):
            for oc in range(8):
                e.dma_start(out=fA(0)[:, oc * 512:(oc + 1) * 512],
                            in_=x1r[tt, oc]).then_inc(sem["ld"], 16)
            for oc in range(8):
                e.wait_ge(sem["vs"], v0 + 2 + oc)
                e.dma_start(out=xor_[tt, oc], in_=t8c[:, :]).then_inc(sem["st"], 16)
        p.on("sync", p6_sync)
        p.add("ld", 8 * 16)
        p.add("st", 8 * 16)

        def p6_tens(e, w0=w0, v0=v0, s0=s0):
            e.wait_ge(sem["vs"], v0 + 1)
            for oc in range(8):
                e.wait_ge(sem["wl"], w0 + 16 * (oc + 1))
                if oc >= 4:
                    e.wait_ge(sem["ss"], s0 + oc - 3)
                for k in range(8):
                    e.matmul(pb[oc % 4][:, :], wdb[:, (oc % 2) * 1024 + k * 128:(oc % 2) * 1024 + (k + 1) * 128],
                             xnt[:, k * 512:(k + 1) * 512], start=(k == 0), stop=(k == 7))
                e.drain().then_inc(sem["ts"], 1)
        p.on("tensor", p6_tens)
        p.add("ts", 8)

        def p6_scal(e, t0=t0, v0=v0):
            for oc in range(8):
                e.wait_ge(sem["ts"], t0 + oc + 1)
                if oc > 0:
                    e.wait_ge(sem["vs"], v0 + 1 + oc)
                e.activation(t8b[:, :], pb[oc % 4][:, :], AF.Copy)
                e.drain().then_inc(sem["ss"], 1)
        p.on("scalar", p6_scal)
        p.add("ss", 8)
        p.barrier()

    nst6 = p.get("st")
    def p6_fin(e, nst6=nst6):
        e.wait_ge(sem["st"], nst6)
    p.on("sync", p6_fin)
    p.barrier()

    if debug:
        dbg_x2a = p.dram_out("dbg_x2a", (D, T))
        dbg_hs = p.dram_out("dbg_hs", (8, 128, T), BF16)
        dbg_rs = p.dram_out("dbg_rs", (128, T))
        dbg_xnt = p.dram_out("dbg_xnt", (128, 4096), BF16)

        def pdbg_sync(e):
            e.dma_start(out=dbg_x2a, in_=xout).then_inc(sem["st"], 16)
            e.dma_start(out=dbg_hs.rearrange("c p n -> p c n"),
                        in_=bf16a[:, 0:8 * T].rearrange("p (c n) -> p c n", c=8)
                        ).then_inc(sem["st"], 16)
            e.dma_start(out=dbg_rs, in_=rmsacc[:, 0:T]).then_inc(sem["st"], 16)
            e.dma_start(out=dbg_xnt, in_=xnt[:, 0:4096]).then_inc(sem["st"], 16)
        p.on("sync", pdbg_sync)
        p.add("st", 64)
        p.barrier()
    qw1r = qw1d.rearrange("(k p) n -> k p n", k=8)
    qw2r = qw2d.rearrange("(k p) n -> k p n", k=32)
    for (Wt, KC, WD, qwr, swi) in ((w1T, 8, H1, qw1r, 0), (w2T, 32, D, qw2r, 2)):
        l0 = p.get("ld"); v0 = p.get("vs"); t0 = p.get("ts")
        Wr = Wt.rearrange("(k p) n -> k p n", k=KC)

        def p7a_sync(e, Wr=Wr, KC=KC, WD=WD, v0=v0):
            for k in range(KC):
                if k > 0:
                    e.wait_ge(sem["vs"], v0 + k)
                e.dma_start(out=fA(0)[:, 0:WD], in_=Wr[k]).then_inc(sem["ld"], 16)
        p.on("sync", p7a_sync)
        p.add("ld", 16 * KC)

        def p7a_vec(e, KC=KC, WD=WD, l0=l0, t0=t0, swi=swi):
            for k in range(KC):
                e.wait_ge(sem["ld"], l0 + 16 * (k + 1))
                e.scalar_tensor_tensor(fA(1)[:, 0:WD], fA(0)[:, 0:WD], -1.0,
                                       fA(0)[:, 0:WD], ALU.mult, ALU.max)
                e.tensor_reduce(wcol[:, k:k + 1], fA(1)[:, 0:WD], AX.X, ALU.add)
                e.drain().then_inc(sem["vs"], 1)
            e.wait_ge(sem["ts"], t0 + 1)
            fz(e)
            e.tensor_reduce(swc[:, swi + 1:swi + 2], pb[5][:, 0:KC], AX.X, ALU.add)
            fz(e)
            e.tensor_scalar(swc[:, swi + 1:swi + 2], swc[:, swi + 1:swi + 2],
                            float(1.0 / (D * H1)), 1e-5, ALU.mult, ALU.max)
            fz(e)
            e.reciprocal(swc[:, swi:swi + 1], swc[:, swi + 1:swi + 2])
            fz(e)
            e.drain().then_inc(sem["vs"], 1)
        p.on("vector", p7a_vec)
        p.add("vs", KC + 1)

        def p7a_tens(e, KC=KC, v0=v0):
            e.wait_ge(sem["vs"], v0 + KC)
            e.matmul(pb[5][:, 0:KC], onesf[:, :], wcol[:, 0:KC], start=True, stop=True)
            e.drain().then_inc(sem["ts"], 1)
        p.on("tensor", p7a_tens)
        p.add("ts", 1)
        p.barrier()

        l0 = p.get("ld"); v0 = p.get("vs"); st0 = p.get("st")

        def p7b_sync(e, Wr=Wr, qwr=qwr, KC=KC, WD=WD, v0=v0):
            for k in range(KC):
                if k > 0:
                    e.wait_ge(sem["vs"], v0 + k)
                e.dma_start(out=fA(0)[:, 0:WD], in_=Wr[k]).then_inc(sem["ld"], 16)
                e.wait_ge(sem["vs"], v0 + k + 1)
                e.dma_start(out=qwr[k], in_=xnt[:, 0:WD]).then_inc(sem["st"], 16)
        p.on("sync", p7b_sync)
        p.add("ld", 16 * KC)
        p.add("st", 16 * KC)

        def p7b_vec(e, KC=KC, WD=WD, l0=l0, st0=st0, swi=swi):
            for k in range(KC):
                e.wait_ge(sem["ld"], l0 + 16 * (k + 1))
                if k > 0:
                    e.wait_ge(sem["st"], st0 + 16 * k)
                e.tensor_scalar(fA(1)[:, 0:WD], fA(0)[:, 0:WD], swc[:, swi:swi + 1],
                                None, ALU.mult)
                e.tensor_scalar(fA(2)[:, 0:WD], fA(1)[:, 0:WD], 0.5, None, ALU.is_ge)
                e.tensor_scalar(fA(3)[:, 0:WD], fA(1)[:, 0:WD], -0.5, None, ALU.is_lt)
                e.tensor_tensor(xnt[:, 0:WD], fA(2)[:, 0:WD], fA(3)[:, 0:WD], ALU.subtract)
                e.drain().then_inc(sem["vs"], 1)
        p.on("vector", p7b_vec)
        p.add("vs", KC)
        p.barrier()

    BITS = [128.0, 64.0, 32.0, 16.0, 8.0, 4.0, 2.0, 1.0]
    o1v = bf16a[:, 0:16384]
    q2v = bf16a[:, 16384:32768]

    def ladder(e, ybuf, qdst):
        e.tensor_scalar(t8c[:, :], ybuf, 1.0, 128.5, ALU.mult, ALU.add)
        e.tensor_copy(t8d[:, :], t8c[:, :])
        for b in BITS:
            e.tensor_scalar(t8b[:, :], t8d[:, :], float(b), None, ALU.is_ge)
            e.scalar_tensor_tensor(t8d[:, :], t8b[:, :], float(-b), t8d[:, :],
                                   ALU.mult, ALU.add)
        e.tensor_tensor(t8c[:, :], t8c[:, :], t8d[:, :], ALU.subtract)
        e.tensor_scalar(qdst, t8c[:, :], 1.0, -128.0, ALU.mult, ALU.add)

    for tt in range(NT):
        sl = slice(tt * 512, (tt + 1) * 512)
        l0 = p.get("ld"); v0 = p.get("vs"); t0 = p.get("ts"); s0 = p.get("ss")

        def p8a_sync(e, tt=tt):
            e.dma_start(out=fA(1).rearrange("p (c n) -> p c n", c=8),
                        in_=x1r[tt].rearrange("c p n -> p c n")).then_inc(sem["ld"], 16)
        p.on("sync", p8a_sync)
        p.add("ld", 16)

        def p8a_vec(e, tt=tt, sl=sl, l0=l0, t0=t0, s0=s0):
            e.wait_ge(sem["ld"], l0 + 16)
            for c in range(8):
                x1c = fA(1)[:, c * 512:(c + 1) * 512]
                hc = fA(0)[:, c * 512:(c + 1) * 512]
                e.tensor_tensor(t8a[:, :], x1c, CAt[:, sl], ALU.mult)
                e.tensor_tensor(t8a[:, :], t8a[:, :], CBt[:, sl], ALU.subtract)
                e.tensor_scalar(t8a[:, :], t8a[:, :], CB_g1[:, c:c + 1], None, ALU.mult)
                e.tensor_scalar(hc, t8a[:, :], CB_b1[:, c:c + 1], None, ALU.add)
                if c == 0:
                    e.scalar_tensor_tensor(t8b[:, :], hc, -1.0, hc, ALU.mult, ALU.max)
                else:
                    e.scalar_tensor_tensor(t8c[:, :], hc, -1.0, hc, ALU.mult, ALU.max)
                    e.tensor_tensor(t8b[:, :], t8b[:, :], t8c[:, :], ALU.max)
            e.drain().then_inc(sem["vs"], 1)
            e.wait_ge(sem["ts"], t0 + 1)
            for blk in range(4):
                e.tensor_reduce(wcol[:, blk:blk + 1], pb[6][:, blk * 128:(blk + 1) * 128],
                                AX.X, ALU.max)
                fz(e)
            e.tensor_scalar(wcol[:, 0:4], wcol[:, 0:4], 1e-5, None, ALU.max)
            fz(e)
            e.reciprocal(wcol[:, 4:8], wcol[:, 0:4])
            fz(e)
            e.tensor_scalar(wcol[:, 4:8], wcol[:, 4:8], 127.0, None, ALU.mult)
            e.tensor_scalar(wcol[:, 8:12], wcol[:, 0:4], float(1.0 / 127.0), None, ALU.mult)
            e.drain().then_inc(sem["vs"], 1)
            e.wait_ge(sem["ts"], t0 + 3)
            e.tensor_copy(amr[:, :], pb[4][:, :])
            e.tensor_copy(am2r[:, :], pb[5][:, :])
            e.drain().then_inc(sem["vs"], 1)
            for c in range(8):
                hc = fA(0)[:, c * 512:(c + 1) * 512]
                e.tensor_tensor(fA(2)[:, 0:512], hc, amr[:, :], ALU.mult)
                ladder(e, fA(2)[:, 0:512], qt[:, c * 512:(c + 1) * 512])
            e.drain().then_inc(sem["vs"], 1)
        p.on("vector", p8a_vec)
        p.add("vs", 4)

        def p8a_tens(e, v0=v0, s0=s0):
            e.wait_ge(sem["vs"], v0 + 1)
            for blk in range(4):
                e.transpose(pb[6][:, blk * 128:(blk + 1) * 128],
                            t8b[:, blk * 128:(blk + 1) * 128], eyes[:, :])
            e.drain().then_inc(sem["ts"], 1)
            e.wait_ge(sem["vs"], v0 + 2)
            for blk in range(4):
                e.transpose(pb[7][0:1, blk * 128:(blk + 1) * 128],
                            wcol[:, 4 + blk:5 + blk], eyes[:, :])
                e.transpose(pb[3][0:1, blk * 128:(blk + 1) * 128],
                            wcol[:, 8 + blk:9 + blk], eyes[:, :])
            e.drain().then_inc(sem["ts"], 1)
            e.wait_ge(sem["ss"], s0 + 1)
            bc(e, pb[4][:, :], ones1, t8d[0:1, :])
            bc(e, pb[5][:, :], ones1, t8c[0:1, :])
            e.drain().then_inc(sem["ts"], 1)
        p.on("tensor", p8a_tens)
        p.add("ts", 3)

        def p8a_scal(e, t0=t0):
            e.wait_ge(sem["ts"], t0 + 2)
            e.activation(t8d[0:1, :], pb[7][0:1, 0:512], AF.Copy)
            e.activation(t8c[0:1, :], pb[3][0:1, 0:512], AF.Copy)
            e.drain().then_inc(sem["ss"], 1)
        p.on("scalar", p8a_scal)
        p.add("ss", 1)
        p.barrier()

        l0 = p.get("ld"); v0 = p.get("vs"); t0 = p.get("ts"); s0 = p.get("ss")

        def p8b_sync(e, tt=tt, t0=t0):
            for oc in range(NH1C):
                if oc >= 2:
                    e.wait_ge(sem["ts"], t0 + 2 * (oc - 1))
                e.dma_start(out=wdb[:, (oc % 2) * 1024:(oc % 2) * 1024 + 1024].rearrange("p (k n) -> p k n", k=8),
                            in_=qw1d.rearrange("(k p) (o n) -> o p k n", k=8, o=NH1C)[oc]
                            ).then_inc(sem["ld"], 16)
        p.on("sync", p8b_sync)
        p.add("ld", 16 * NH1C)

        def p8b_tens(e, l0=l0, s0=s0, v0=v0):
            for oc in range(NH1C):
                e.wait_ge(sem["ld"], l0 + 16 * (oc + 1))
                if oc >= 4:
                    e.wait_ge(sem["ss"], s0 + 2 * (oc - 4) + 1)
                for k in range(8):
                    e.matmul(pb[oc % 4][:, :], wdb[:, (oc % 2) * 1024 + k * 128:(oc % 2) * 1024 + (k + 1) * 128],
                             qt[:, k * 512:(k + 1) * 512], start=(k == 0), stop=(k == 7))
                e.drain().then_inc(sem["ts"], 1)
                e.wait_ge(sem["ss"], s0 + 2 * oc + 2)
                e.matmul(pb[6][:, :], onesb[:, :], o1v[:, oc * 512:(oc + 1) * 512],
                         start=(oc == 0), stop=(oc == NH1C - 1))
                e.matmul(pb[7][:, :], onesf[:, :], t8a[:, :],
                         start=(oc == 0), stop=(oc == NH1C - 1))
                e.drain().then_inc(sem["ts"], 1)
        p.on("tensor", p8b_tens)
        p.add("ts", 2 * NH1C)

        def p8b_scal(e, t0=t0, v0=v0):
            for oc in range(NH1C):
                e.wait_ge(sem["ts"], t0 + 2 * oc + 1)
                if oc > 0:
                    e.wait_ge(sem["vs"], v0 + oc)
                e.activation(t8b[:, :], pb[oc % 4][:, :], AF.Copy)
                e.drain().then_inc(sem["ss"], 1)
                e.wait_ge(sem["vs"], v0 + oc + 1)
                if oc > 0:
                    e.wait_ge(sem["ts"], t0 + 2 * oc)
                e.activation(o1v[:, oc * 512:(oc + 1) * 512], t8b[:, :], AF.Gelu)
                e.activation(t8a[:, :], o1v[:, oc * 512:(oc + 1) * 512], AF.Square)
                e.drain().then_inc(sem["ss"], 1)
        p.on("scalar", p8b_scal)
        p.add("ss", 2 * NH1C)

        def p8b_vec(e, s0=s0):
            for oc in range(NH1C):
                e.wait_ge(sem["ss"], s0 + 2 * oc + 1)
                e.tensor_tensor(t8b[:, :], t8b[:, :], am2r[:, :], ALU.mult)
                e.tensor_scalar(t8b[:, :], t8b[:, :], swc[:, 1:2], None, ALU.mult)
                e.tensor_scalar(t8b[:, :], t8b[:, :], CB_bb1[:, oc:oc + 1], None, ALU.add)
                e.drain().then_inc(sem["vs"], 1)
        p.on("vector", p8b_vec)
        p.add("vs", NH1C)
        p.barrier()

        v0 = p.get("vs"); s0 = p.get("ss")

        def p8c_vec(e, v0=v0, s0=s0):
            e.tensor_scalar(murb[:, :], pb[6][:, :], float(1.0 / H1), None, ALU.mult)
            e.tensor_scalar(t8a[:, :], pb[7][:, :], float(1.0 / H1), None, ALU.mult)
            e.tensor_tensor(t8d[:, :], murb[:, :], murb[:, :], ALU.mult)
            e.tensor_tensor(t8a[:, :], t8a[:, :], t8d[:, :], ALU.subtract)
            e.tensor_scalar(t8a[:, :], t8a[:, :], EPS, None, ALU.add)
            e.drain().then_inc(sem["vs"], 1)
            e.wait_ge(sem["ss"], s0 + 1)
            e.reciprocal(rsrb[:, :], t8b[:, :])
            e.drain().then_inc(sem["vs"], 1)
        p.on("vector", p8c_vec)
        p.add("vs", 2)

        def p8c_scal(e, v0=v0):
            e.wait_ge(sem["vs"], v0 + 1)
            e.activation(t8b[:, :], t8a[:, :], AF.Sqrt)
            e.drain().then_inc(sem["ss"], 1)
        p.on("scalar", p8c_scal)
        p.add("ss", 1)
        p.barrier()

        v0 = p.get("vs"); t0 = p.get("ts"); s0 = p.get("ss")

        def xn2_into(e, oc, dst):
            oslc = slice(oc * 512, (oc + 1) * 512)
            e.tensor_tensor(dst, o1v[:, oslc], murb[:, :], ALU.subtract)
            e.tensor_tensor(dst, dst, rsrb[:, :], ALU.mult)
            e.tensor_scalar(dst, dst, CB_g2[:, oc:oc + 1], None, ALU.mult)
            e.tensor_scalar(dst, dst, CB_b2[:, oc:oc + 1], None, ALU.add)

        def p8d_vec(e, t0=t0, s0=s0):
            for oc in range(NH1C):
                xn2_into(e, oc, fA(2)[:, 0:512])
                if oc == 0:
                    e.scalar_tensor_tensor(t8b[:, :], fA(2)[:, 0:512], -1.0,
                                           fA(2)[:, 0:512], ALU.mult, ALU.max)
                else:
                    e.scalar_tensor_tensor(t8c[:, :], fA(2)[:, 0:512], -1.0,
                                           fA(2)[:, 0:512], ALU.mult, ALU.max)
                    e.tensor_tensor(t8b[:, :], t8b[:, :], t8c[:, :], ALU.max)
            e.drain().then_inc(sem["vs"], 1)
            e.wait_ge(sem["ts"], t0 + 1)
            for blk in range(4):
                e.tensor_reduce(wcol[:, blk:blk + 1], pb[6][:, blk * 128:(blk + 1) * 128],
                                AX.X, ALU.max)
                fz(e)
            e.tensor_scalar(wcol[:, 0:4], wcol[:, 0:4], 1e-5, None, ALU.max)
            fz(e)
            e.reciprocal(wcol[:, 4:8], wcol[:, 0:4])
            fz(e)
            e.tensor_scalar(wcol[:, 4:8], wcol[:, 4:8], 127.0, None, ALU.mult)
            e.tensor_scalar(wcol[:, 8:12], wcol[:, 0:4], float(1.0 / 127.0), None, ALU.mult)
            e.drain().then_inc(sem["vs"], 1)
            e.wait_ge(sem["ts"], t0 + 3)
            e.tensor_copy(amr[:, :], pb[4][:, :])
            e.tensor_copy(am2r[:, :], pb[5][:, :])
            e.drain().then_inc(sem["vs"], 1)
            for oc in range(NH1C):
                xn2_into(e, oc, fA(2)[:, 0:512])
                e.tensor_tensor(fA(2)[:, 0:512], fA(2)[:, 0:512], amr[:, :], ALU.mult)
                ladder(e, fA(2)[:, 0:512], q2v[:, oc * 512:(oc + 1) * 512])
            e.drain().then_inc(sem["vs"], 1)
        p.on("vector", p8d_vec)
        p.add("vs", 4)

        def p8d_tens(e, v0=v0, s0=s0):
            e.wait_ge(sem["vs"], v0 + 1)
            for blk in range(4):
                e.transpose(pb[6][:, blk * 128:(blk + 1) * 128],
                            t8b[:, blk * 128:(blk + 1) * 128], eyes[:, :])
            e.drain().then_inc(sem["ts"], 1)
            e.wait_ge(sem["vs"], v0 + 2)
            for blk in range(4):
                e.transpose(pb[7][0:1, blk * 128:(blk + 1) * 128],
                            wcol[:, 4 + blk:5 + blk], eyes[:, :])
                e.transpose(pb[3][0:1, blk * 128:(blk + 1) * 128],
                            wcol[:, 8 + blk:9 + blk], eyes[:, :])
            e.drain().then_inc(sem["ts"], 1)
            e.wait_ge(sem["ss"], s0 + 1)
            bc(e, pb[4][:, :], ones1, t8d[0:1, :])
            bc(e, pb[5][:, :], ones1, t8c[0:1, :])
            e.drain().then_inc(sem["ts"], 1)
        p.on("tensor", p8d_tens)
        p.add("ts", 3)

        def p8d_scal(e, t0=t0):
            e.wait_ge(sem["ts"], t0 + 2)
            e.activation(t8d[0:1, :], pb[7][0:1, 0:512], AF.Copy)
            e.activation(t8c[0:1, :], pb[3][0:1, 0:512], AF.Copy)
            e.drain().then_inc(sem["ss"], 1)
        p.on("scalar", p8d_scal)
        p.add("ss", 1)
        p.barrier()

        l0 = p.get("ld"); v0 = p.get("vs"); t0 = p.get("ts")
        s0 = p.get("ss"); st0 = p.get("st")

        def p8e_sync(e, tt=tt, t0=t0, v0=v0):
            for c in range(8):
                if c >= 2:
                    e.wait_ge(sem["ts"], t0 + c - 1)
                e.dma_start(out=wdb[:, (c % 2) * 4096:(c % 2) * 4096 + 4096].rearrange("p (k n) -> p k n", k=32),
                            in_=qw2d.rearrange("(k p) (o n) -> o p k n", k=32, o=8)[c]
                            ).then_inc(sem["ld"], 16)
                e.dma_start(out=fA(1)[:, c * 512:(c + 1) * 512], in_=xor_[tt, c]).then_inc(sem["ld"], 16)
            for c in range(8):
                e.wait_ge(sem["vs"], v0 + c + 1)
                e.dma_start(out=xor_[tt, c], in_=t8d[:, :]).then_inc(sem["st"], 16)
        p.on("sync", p8e_sync)
        p.add("ld", 32 * 8)
        p.add("st", 16 * 8)

        def p8e_tens(e, l0=l0, s0=s0):
            for c in range(8):
                e.wait_ge(sem["ld"], l0 + 32 * (c + 1))
                if c >= 4:
                    e.wait_ge(sem["ss"], s0 + c - 3)
                for k in range(32):
                    e.matmul(pb[c % 4][:, :], wdb[:, (c % 2) * 4096 + k * 128:(c % 2) * 4096 + (k + 1) * 128],
                             q2v[:, k * 512:(k + 1) * 512], start=(k == 0), stop=(k == 31))
                e.drain().then_inc(sem["ts"], 1)
        p.on("tensor", p8e_tens)
        p.add("ts", 8)

        def p8e_scal(e, t0=t0, v0=v0):
            for c in range(8):
                e.wait_ge(sem["ts"], t0 + c + 1)
                if c > 0:
                    e.wait_ge(sem["vs"], v0 + c)
                e.activation(t8b[:, :], pb[c % 4][:, :], AF.Copy)
                e.drain().then_inc(sem["ss"], 1)
        p.on("scalar", p8e_scal)
        p.add("ss", 8)

        def p8e_vec(e, s0=s0, l0=l0, st0=st0):
            for c in range(8):
                e.wait_ge(sem["ss"], s0 + c + 1)
                e.wait_ge(sem["ld"], l0 + 32 * (c + 1))
                if c > 0:
                    e.wait_ge(sem["st"], st0 + 16 * c)
                e.tensor_tensor(t8d[:, :], t8b[:, :], am2r[:, :], ALU.mult)
                e.tensor_scalar(t8d[:, :], t8d[:, :], swc[:, 3:4], None, ALU.mult)
                e.tensor_scalar(t8d[:, :], t8d[:, :], CB_bb2[:, c:c + 1], None, ALU.add)
                e.tensor_tensor(t8d[:, :], t8d[:, :], fA(1)[:, c * 512:(c + 1) * 512], ALU.add)
                e.drain().then_inc(sem["vs"], 1)
        p.on("vector", p8e_vec)
        p.add("vs", 8)
        p.barrier()

    if debug:
        dbg_q1 = p.dram_out("dbg_q1", (128, 4096), BF16)
        dbg_o1 = p.dram_out("dbg_o1", (128, 16384), BF16)
        dbg_q2 = p.dram_out("dbg_q2", (128, 16384), BF16)
        dbg_amr = p.dram_out("dbg_amr", (128, 512))
        dbg_am2r = p.dram_out("dbg_am2r", (128, 512))
        dbg_mur = p.dram_out("dbg_mur", (128, 512))
        dbg_rsr = p.dram_out("dbg_rsr", (128, 512))
        dbg_sw = p.dram_out("dbg_sw", (128, 8))

        def pdbg2_sync(e):
            e.dma_start(out=dbg_q1, in_=qt[:, 0:4096]).then_inc(sem["st"], 16)
            e.dma_start(out=dbg_o1, in_=o1v[:, :]).then_inc(sem["st"], 16)
            e.dma_start(out=dbg_q2, in_=q2v[:, :]).then_inc(sem["st"], 16)
            e.dma_start(out=dbg_amr, in_=amr[:, :]).then_inc(sem["st"], 16)
            e.dma_start(out=dbg_am2r, in_=am2r[:, :]).then_inc(sem["st"], 16)
            e.dma_start(out=dbg_mur, in_=murb[:, :]).then_inc(sem["st"], 16)
            e.dma_start(out=dbg_rsr, in_=rsrb[:, :]).then_inc(sem["st"], 16)
            e.dma_start(out=dbg_sw, in_=swc[:, :]).then_inc(sem["st"], 16)
        p.on("sync", pdbg2_sync)
        p.add("st", 8 * 16)
        p.barrier()

    nstF = p.get("st")
    def p8_fin(e, nstF=nstF):
        e.wait_ge(sem["st"], nstF)
    p.on("sync", p8_fin)
    return p, dict()


def stage_B(d, x, OT, ffn0, stB):
    F = np.float32
    return dict(
        x_fm=np.ascontiguousarray(x.T.astype(F)),
        OT_in=np.ascontiguousarray(OT),
        ffn0_in=np.ascontiguousarray(ffn0.astype(F)),
        stB=np.ascontiguousarray(stB.astype(F)),
        wo0T=np.ascontiguousarray(d["L0_attn_w_out"].astype(F)),
        b_out0c=np.ascontiguousarray(d["L0_attn_b_out"].astype(F).reshape(8, 128).T),
        WinT=np.ascontiguousarray(d["L1_attn_W_in"].T.astype(F)),
        WigT=np.ascontiguousarray(d["L1_attn_W_ig"].T.astype(F)),
        WrgT=np.ascontiguousarray(d["L1_attn_W_rg"].T.astype(F)),
        WoutT=np.ascontiguousarray(d["L1_attn_W_out"].T.astype(F)),
        bigc=np.ascontiguousarray(d["L1_attn_b_ig"].astype(F).reshape(8, 128).T),
        brgc=np.ascontiguousarray(d["L1_attn_b_rg"].astype(F).reshape(8, 128).T),
        lamc=np.ascontiguousarray(d["L1_attn_lam"].astype(F).reshape(8, 128).T),
        h0c=np.ascontiguousarray(d["L1_attn_h0"][0, 0].astype(F).reshape(8, 128).T),
        nwc=np.ascontiguousarray(d["L1_attn_norm_w"].astype(F).reshape(8, 128).T),
        g1c=np.ascontiguousarray(d["L1_ffn_ln1_g"].astype(F).reshape(8, 128).T),
        b1c=np.ascontiguousarray(d["L1_ffn_ln1_b"].astype(F).reshape(8, 128).T),
        bb1c=np.ascontiguousarray(d["L1_ffn_b1"].astype(F).reshape(32, 128).T),
        g2c=np.ascontiguousarray(d["L1_ffn_ln2_g"].astype(F).reshape(32, 128).T),
        b2c=np.ascontiguousarray(d["L1_ffn_ln2_b"].astype(F).reshape(32, 128).T),
        bb2c=np.ascontiguousarray(d["L1_ffn_b2"].astype(F).reshape(8, 128).T),
        w1T=np.ascontiguousarray(d["L1_ffn_w1"].T.astype(F)),
        w2T=np.ascontiguousarray(d["L1_ffn_w2"].T.astype(F)),
        eye128=np.eye(128, dtype=F),
    )



def stage_A_full(d, core, T):
    F = np.float32
    TH = T // 2
    b, half = core // 2, core % 2
    heads = range(half * 8, half * 8 + 8)
    x = np.ascontiguousarray(d["L0_attn_x"][b, :T].astype(F))
    x_fm = np.ascontiguousarray(x.T)
    x_own = np.ascontiguousarray(x_fm[:, half * TH:(half + 1) * TH])
    wq = d["L0_attn_w_qkvb"]; bq = d["L0_attn_b_qkvb"]
    wcols, bcols = [], []
    for qq in range(4):
        for h in heads:
            base = qq * 1024 + h * 64
            wcols.append(wq[:, base:base + 64])
            bcols.append(bq[base:base + 64])
    wqkv = np.ascontiguousarray(np.concatenate(wcols, axis=1))
    bqkv = np.ascontiguousarray(np.concatenate(bcols))
    gu = np.empty((34, 1024, 1024), F)
    dn = np.empty((34, 512, 1024), F)
    for e in range(32):
        gu[e, :, 0:512] = d["L0_ffn_gate_w"][e].T
        gu[e, :, 512:1024] = d["L0_ffn_up_w"][e].T
        dn[e] = d["L0_ffn_down_w"][e].T
    sgT = d["L0_ffn_shared_gate_w"].T; suT = d["L0_ffn_shared_up_w"].T
    sdT = d["L0_ffn_shared_down_w"].T
    gu[32, :, 0:512] = sgT[:, 0:512]; gu[32, :, 512:1024] = suT[:, 0:512]
    gu[33, :, 0:512] = sgT[:, 512:1024]; gu[33, :, 512:1024] = suT[:, 512:1024]
    dn[32] = sdT[0:512]; dn[33] = sdT[512:1024]
    m = np.arange(16)
    msku = (m[:, None] < m[None, :]).astype(F)
    mskg = (m[:, None] <= m[None, :]).astype(F)
    return dict(
        x_fm=x_fm, x_own=x_own, wqkv=wqkv,
        bqkv=np.ascontiguousarray(bqkv.reshape(32, 64).T),
        ln_g=np.ascontiguousarray(d["L0_attn_ln_g"].astype(F).reshape(8, 128).T),
        ln_b=np.ascontiguousarray(d["L0_attn_ln_b"].astype(F).reshape(8, 128).T),
        rwT=np.ascontiguousarray(d["L0_ffn_router_weight"].T.astype(F)),
        rbias=np.ascontiguousarray(np.repeat(d["L0_ffn_router_bias"].reshape(1, 32), 128, 0).astype(F)),
        gu_w=gu, dn_w=dn,
        msku_in=np.ascontiguousarray(np.tile(msku, (1, 64))).astype(ml_dtypes.bfloat16),
        mskg_in=np.ascontiguousarray(np.tile(mskg, (1, 64))).astype(ml_dtypes.bfloat16),
        eye64=np.eye(64, dtype=F),
    )


def kernel(**inputs):
    d = inputs
    T = 4096
    TH = T // 2

    imsA = [stage_A_full(d, core, T) for core in range(8)]
    pA, _ = build_A(T)
    ncA = pA.finalize()
    resA = run_bass_kernel_spmd(ncA, imsA, core_ids=list(range(8)))

    imsB = []
    for core in range(8):
        b = core % 4
        rE = resA.results[2 * b]
        rO = resA.results[2 * b + 1]
        O = np.concatenate([np.asarray(rE["O"]), np.asarray(rO["O"])], axis=1)
        OT = np.ascontiguousarray(O.T)
        ffn0 = np.concatenate([np.asarray(rE["ffn"]).reshape(D, TH),
                               np.asarray(rO["ffn"]).reshape(D, TH)], axis=1)
        stB = np.concatenate([np.asarray(rE["stO"]), np.asarray(rO["stO"])], axis=1)
        x = d["L0_attn_x"][b].astype(np.float32)
        imsB.append(stage_B(d, x, OT, ffn0, stB))
    pB, _ = build_B(T)
    ncB = pB.finalize()
    resB = run_bass_kernel_spmd(ncB, imsB, core_ids=list(range(8)))

    out = np.stack([np.ascontiguousarray(np.asarray(resB.results[b]["xout"]).T)
                    for b in range(4)], axis=0)
    return out.astype(np.float32)
